# version 02
# baseline (speedup 1.0000x reference)
import numpy as np
import ml_dtypes
from contextlib import ExitStack

import concourse.bass as bass
import concourse.mybir as mybir
from concourse.bass_utils import run_bass_kernel_spmd

F32 = mybir.dt.float32
BF16 = mybir.dt.bfloat16
AF = mybir.ActivationFunctionType
ALU = mybir.AluOpType
AX = mybir.AxisListType
BF = ml_dtypes.bfloat16

D = 1024
NSEQ = 2048
NB = 4
NCORES = 8
HEADS = 16
HD = 64
DC = D // 128
NT = NSEQ // 128
EPS = 1e-5


class Sched:
    ENGS = ("sync", "gpsimd", "vector", "scalar", "tensor")
    POOLS = {"ld": ("syncdma", 16), "g": ("gpdma", 8)}

    def __init__(self, sem_names):
        self.streams = {e: [] for e in self.ENGS}
        self.counts = {n: 0 for n in sem_names if n not in self.POOLS}
        self.pool = {}
        self.pool_idx = {}
        for key, (pref, n) in self.POOLS.items():
            self.pool[key] = [f"{pref}{i}" for i in range(n)]
            self.pool_idx[key] = 0
            for s in self.pool[key]:
                self.counts[s] = 0

    def _norm_wait(self, wait):
        out = []
        for item in wait:
            s, v = item
            if isinstance(v, tuple):
                if v[1] > 0:
                    out.append(v)
            elif v:
                out.append((s, v))
        return out

    def add(self, eng, fn=None, wait=(), inc=None, inc_n=1):
        wait = self._norm_wait(wait)
        if inc in self.POOLS:
            key = inc
            s = self.pool[key][self.pool_idx[key] % len(self.pool[key])]
            self.pool_idx[key] += 1
            if self.counts[s] > 0:
                wait = wait + [(s, self.counts[s])]
            self.counts[s] += 16
            self.streams[eng].append((tuple(wait), fn, ((s, 16),)))
            return (s, self.counts[s])
        incs = ()
        ret = None
        if inc is not None:
            self.counts[inc] += inc_n
            ret = self.counts[inc]
            incs = ((inc, inc_n),)
        self.streams[eng].append((tuple(wait), fn, incs))
        return ret

    def dma_fence(self, key):
        return [("_", (s, self.counts[s])) for s in self.pool[key]]

    def val(self, sem):
        return self.counts[sem]

    def barrier(self):
        snap = [(s, self.counts[s]) for s in self.counts]
        for e in self.ENGS:
            if self.streams[e]:
                self.add(e, None, wait=snap)

    def emit(self, nc, stack):
        sems = {n: stack.enter_context(nc.semaphore(n)) for n in self.counts}
        block = stack.enter_context(nc.Block())

        def runner(stream):
            def run(eng):
                for waits, fn, incs in stream:
                    for s, v in waits:
                        eng.wait_ge(sems[s], v)
                    inst = fn(eng) if fn is not None else None
                    for s, n in incs:
                        inst.then_inc(sems[s], n)
            return run

        for e in self.ENGS:
            if self.streams[e]:
                getattr(block, e)(runner(self.streams[e]))


class Arena:
    def __init__(self, ap, words):
        self.ap = ap
        self.words = words
        self.top = 0

    def f32(self, rows, n):
        return self.ap[0:rows, self._take(n)[0]: self.top]

    def bf16(self, rows, n):
        o = self._take((n + 1) // 2)[0]
        return self.ap[0:rows, o:o + (n + 1) // 2].bitcast(BF16)

    def f32_at(self, row0, rows, o, n):
        return self.ap[row0:row0 + rows, o:o + n]

    def bf16_at(self, row0, rows, o, n):
        return self.ap[row0:row0 + rows, o:o + (n + 1) // 2].bitcast(BF16)

    def block(self, n):
        return self._take(n)[0]

    def _take(self, n):
        o = self.top
        self.top += n
        assert self.top <= self.words, f"arena overflow: {self.top} > {self.words}"
        return o, self.top

    def mark(self):
        return self.top

    def reset(self, m):
        self.top = m


SEMS = ("ld", "g", "v", "a", "t", "gp")


class FCtx:
    pass


def _const(nc, C, name, arr):
    if C is not None:
        if name not in C.consts:
            C.consts[name] = nc.inline_tensor(arr, name=name)
        return C.consts[name]
    return nc.inline_tensor(arr, name=name)


def build_p1(n_batches=NB, C=None):
    if C is None:
        nc = bass.Bass("TRN2", target_bir_lowering=False, detect_race_conditions=False)
        S = Sched(SEMS)
        x_in = nc.dram_tensor("x", [n_batches, NSEQ, D], F32, kind="ExternalInput").ap()
        w_dr = {n: nc.dram_tensor(n, [D, D], F32, kind="ExternalInput").ap()
                for n in ("wq", "wk", "wv", "wo")}
        b_dr = {n: nc.dram_tensor(n, [D, 1], F32, kind="ExternalInput").ap()
                for n in ("bq", "bk", "bv", "bo")}
        x1_out = nc.dram_tensor("x1_fm", [n_batches, D, NSEQ], F32, kind="ExternalOutput").ap()
        y_out = nc.dram_tensor("yL_fm", [n_batches, D, NSEQ], BF16, kind="ExternalOutput").ap()
    else:
        nc, S = C.nc, C.S
        x_in = C.io["x"]
        w_dr = {n: C.io["l0a_" + n] for n in ("wq", "wk", "wv", "wo")}
        b_dr = {n: C.io["l0a_" + n] for n in ("bq", "bk", "bv", "bo")}
        x1_out = C.io["x1_fm"]
        y_out = C.io["yL_fm"]

    xfm_scr = nc.dram_tensor("xfm_scr", [D, NSEQ], F32, kind="Internal").ap()
    q2_scr = nc.dram_tensor("q2_scr", [HEADS, NSEQ], BF16, kind="Internal").ap()
    k2_scr = nc.dram_tensor("k2_scr", [HEADS, NSEQ], BF16, kind="Internal").ap()
    ou_scr = nc.dram_tensor("p1_ou", [D, NSEQ], BF16, kind="Internal").ap()
    wprep = {n: nc.dram_tensor(f"p1_wp_{n}", [128, DC * D], BF16, kind="Internal").ap()
             for n in ("wq", "wk", "wv", "wo")}

    ident16_c = _const(nc, C, "ident16", np.eye(128, dtype=BF))
    ident32_c = _const(nc, C, "ident32", np.eye(128, dtype=np.float32))

    with ExitStack() as stack:
        if C is None:
            AR = Arena(stack.enter_context(
                nc.sbuf_tensor("AR", [128, 53160], F32)).ap(), 53160)
        else:
            AR = C.AR
            AR.reset(0)

        id16 = AR.bf16(128, 128)
        id32 = AR.f32(128, 128)
        bcol = AR.f32(128, 24)
        scol = AR.f32(128, 8)
        ones_col = AR.bf16(128, 2)
        epscol = AR.f32(128, 1)
        o_or = AR.block(64)
        ones_row = AR.bf16_at(0, 1, o_or, 128)
        aug1 = AR.block(1024)
        kaug = [AR.bf16_at(0, 2, aug1, 2048), AR.bf16_at(64, 2, aug1, 2048)]
        q2row16 = AR.bf16_at(32, 1, aug1, 2048)
        aug2 = AR.block(1024)
        qaug = [AR.bf16_at(0, 2, aug2, 2048), AR.bf16_at(64, 2, aug2, 2048)]
        rrow = AR.bf16(1, 1024)
        brv16 = AR.bf16(1, 1024)
        sq512 = [AR.bf16(128, 512), AR.bf16(128, 512)]

        wslot = [AR.bf16(128, DC * D), AR.bf16(128, DC * D)]

        Qb = AR.bf16(128, 8 * NSEQ)
        Kb = AR.bf16(128, 8 * NSEQ)

        Vt = AR.bf16(128, NT * HEADS * (HD + 1))

        mC = AR.mark()
        yfm = AR.bf16(128, DC * NSEQ)
        endC_y = AR.mark()
        AR.reset(mC)
        Lbuf = AR.f32(128, 1024)
        Ubuf = AR.f32(128, 1024)
        Pbuf = [AR.bf16(128, 1024), AR.bf16(128, 1024)]
        Rsb = AR.bf16(64, 1024)
        oust = [AR.bf16(64, 1024), AR.bf16(64, 1024)]
        endC_3 = AR.mark()
        AR.reset(mC)
        Oun = AR.bf16(128, 8 * 512)
        x1st = [AR.f32(128, 512), AR.f32(128, 512)]
        xres = [AR.f32(128, 512), AR.f32(128, 512)]
        x1c = AR.bf16(128, 512)
        endC_4 = AR.mark()
        AR.reset(max(endC_y, endC_3, endC_4))

        mD = AR.mark()
        xt = [AR.f32(128, D), AR.f32(128, D)]
        ytile = [AR.bf16(128, D), AR.bf16(128, D)]
        xstage = [AR.f32(128, D), AR.f32(128, D)]
        sqdump = AR.f32(128, D)
        endD_1 = AR.mark()
        AR.reset(mD)
        Mt = AR.bf16(128, NSEQ)
        Rt = AR.bf16(128, NSEQ)
        r1 = AR.block(1024)
        sum16 = AR.bf16_at(0, 1, r1, 2048)
        ss16 = AR.bf16_at(32, 1, r1, 2048)
        m16 = AR.bf16(1, 2048)
        rstd16 = AR.bf16(1, 2048)
        var16 = AR.bf16(1, 2048)
        ssd16 = AR.bf16(1, 2048)
        ylt = [AR.bf16(128, 512), AR.bf16(128, 512)]
        yltmp = AR.f32(128, 512)
        endD_5 = AR.mark()
        AR.reset(max(endD_1, endD_5))

        wtmp32 = xt
        wtmp16 = ytile
        wstg = sqdump.bitcast(BF16)[:, 0:128]

        e_ = stack.enter_context
        if C is None:
            psA = e_(nc.psum_tensor("psA", [128, 1024], F32)).ap()
            psB = e_(nc.psum_tensor("psB", [128, 1024], F32)).ap()
            psO = e_(nc.psum_tensor("psO", [128, 1024], F32)).ap()
            pt16 = e_(nc.psum_tensor("pt16", [128, 2048], BF16)).ap()
        else:
            psA, psB, psO, pt16 = C.psA, C.psB, C.psO, C.pt16

        l_id = S.add("sync", lambda e: e.dma_start(out=id16, in_=ident16_c.ap()),
                     inc="ld", inc_n=16)
        l_id2 = S.add("sync", lambda e: e.dma_start(out=id32, in_=ident32_c.ap()),
                      inc="ld", inc_n=16)
        id_tok = [("_", l_id), ("_", l_id2)]
        S.add("vector", lambda e: e.memset(ones_col[:, 0:1], 1.0), inc="v")
        S.add("vector", lambda e: e.memset(epscol, EPS), inc="v")
        S.add("vector", lambda e: e.memset(ones_row, 1.0), inc="v")
        S.add("vector", lambda e: e.memset(kaug[0], 1.0), inc="v")
        S.add("vector", lambda e: e.memset(kaug[1], 1.0), inc="v")
        S.add("vector", lambda e: e.memset(qaug[0], 1.0), inc="v")
        S.add("vector", lambda e: e.memset(qaug[1], 1.0), inc="v")

        def set_vt_ones(e):
            ap = Vt.rearrange("p (nt h j) -> p nt h j", nt=NT, h=HEADS, j=HD + 1)
            return e.memset(ap[:, :, :, HD:HD + 1], 1.0)
        S.add("vector", set_vt_ones, inc="v")

        wstate = dict(cast_done=[0, 0], tr_done=[0, 0], ps_done=[0, 0], stg_ld=0)

        def emit_wt(w_dram, dst_dram, scale):
            for r in range(DC):
                buf = r % 2
                L = S.add("sync", lambda e, wd=w_dram, r=r, t=wtmp32[buf]:
                          e.dma_start(out=t, in_=wd[r * 128:(r + 1) * 128, :]),
                          wait=[("v", wstate["cast_done"][buf])], inc="ld", inc_n=16)
                if scale == 1.0:
                    cv = S.add("vector", lambda e, i=wtmp32[buf], o=wtmp16[buf]:
                               e.tensor_copy(out=o, in_=i),
                               wait=[("ld", L), ("t", wstate["tr_done"][buf])], inc="v")
                else:
                    cv = S.add("vector", lambda e, i=wtmp32[buf], o=wtmp16[buf], s=scale:
                               e.tensor_scalar_mul(out=o, in0=i, scalar1=s),
                               wait=[("ld", L), ("t", wstate["tr_done"][buf])], inc="v")
                wstate["cast_done"][buf] = cv
                last_t = 0
                for c in range(DC):
                    slot = c % 2
                    dst_ps = pt16[:, slot * 1024: slot * 1024 + 128]
                    tt = S.add("tensor", lambda e, i=wtmp16[buf], c=c, dst=dst_ps:
                               e.transpose(out=dst, in_=i[:, c * 128:(c + 1) * 128],
                                           identity=id16),
                               wait=[("v", cv), ("v", wstate["ps_done"][slot]),
                                     ] + id_tok, inc="t")
                    ev = S.add("vector", lambda e, dst=dst_ps:
                               e.tensor_copy(out=wstg, in_=dst),
                               wait=[("t", tt), ("ld", wstate["stg_ld"])], inc="v")
                    wstate["stg_ld"] = S.add(
                        "sync", lambda e, dd=dst_dram, c=c, r=r:
                        e.dma_start(out=dd[:, c * D + r * 128: c * D + (r + 1) * 128], in_=wstg),
                        wait=[("v", ev)], inc="ld", inc_n=16)
                    wstate["ps_done"][slot] = ev
                    last_t = tt
                wstate["tr_done"][buf] = last_t

        emit_wt(w_dr["wq"], wprep["wq"], -2.0)
        emit_wt(w_dr["wk"], wprep["wk"], 1.0)
        emit_wt(w_dr["wv"], wprep["wv"], 1.0)
        emit_wt(w_dr["wo"], wprep["wo"], 1.0)

        for j, n in enumerate(("bq", "bk", "bo")):
            for c in range(DC):
                S.add("sync", lambda e, bd=b_dr[n], c=c, col=j * 8 + c:
                      e.dma_start(out=bcol[:, col:col + 1],
                                  in_=bd[c * 128:(c + 1) * 128, 0:1]),
                      inc="ld", inc_n=16)
        brv_tok = S.add("gpsimd", lambda e:
              e.dma_start(out=brv16, in_=b_dr["bv"].rearrange("d one -> one d")),
              inc="g", inc_n=16)
        bias_ld_f = S.dma_fence("ld")
        bias_v = S.add("vector", lambda e: e.tensor_scalar_mul(out=bcol[:, 0:8], in0=bcol[:, 0:8],
                                                               scalar1=-2.0),
                       wait=bias_ld_f, inc="v")

        st = dict(
            xt_free_v=[0, 0], xt_free_t=[0, 0],
            ytile_free=[0, 0], xstage_free=[0, 0],
            pt_free=[0, 0],
            psA_v=0, psA_a=0, psB_v=0, psB_a=0, psO_v=0,
            wslot_rd=[0, 0],
            q2row_g=0, q2_g=[0] * HEADS,
            kaug_rd=[0, 0], qaug_rd=[0, 0],
            P_rd=[0, 0],
            sq_rd=[0, 0],
            oust_ld=[0, 0],
            x1st_ld=[0, 0], xres_v=[0, 0],
            ylt_ld=[0, 0],
        )

        for b in range(n_batches):
            S.barrier()
            for k in ("psA_v", "psB_v", "psO_v"):
                st[k] = 0
            st["psA_a"] = st["psB_a"] = 0

            wv_ld = S.add("sync", lambda e: e.dma_start(out=wslot[0], in_=wprep["wv"]),
                          inc="ld", inc_n=16)
            wk_ld = S.add("sync", lambda e: e.dma_start(out=wslot[1], in_=wprep["wk"]),
                          inc="ld", inc_n=16)

            for nt in range(NT):
                i = nt % 2
                L = S.add("sync", lambda e, b=b, nt=nt, t=xt[i]:
                          e.dma_start(out=t, in_=x_in[b, nt * 128:(nt + 1) * 128, :]),
                          wait=[("v", st["xt_free_v"][i]), ("t", st["xt_free_t"][i])],
                          inc="ld", inc_n=16)
                a1 = S.add("scalar", lambda e, t=xt[i]:
                           e.activation(out=sqdump, in_=t, func=AF.Square,
                                        accum_out=scol[:, 1:2]),
                           wait=[("ld", L)], inc="a")
                v1 = S.add("vector", lambda e, t=xt[i]:
                           e.reduce_sum(out=scol[:, 0:1], in_=t, axis=AX.X),
                           wait=[("ld", L)], inc="v")
                v2 = S.add("vector", lambda e:
                           e.tensor_scalar_mul(out=scol[:, 2:3], in0=scol[:, 0:1], scalar1=1.0 / D),
                           wait=[("v", v1)], inc="v")
                S.add("vector", lambda e:
                      e.tensor_scalar_mul(out=scol[:, 3:4], in0=scol[:, 1:2], scalar1=1.0 / D),
                      wait=[("a", a1)], inc="v")
                v4 = S.add("vector", lambda e:
                           e.tensor_mul(out=scol[:, 4:5], in0=scol[:, 2:3], in1=scol[:, 2:3]),
                           wait=[("v", v2)], inc="v")
                v5 = S.add("vector", lambda e:
                           e.tensor_sub(out=scol[:, 5:6], in0=scol[:, 3:4], in1=scol[:, 4:5]),
                           wait=[("v", v4)], inc="v")
                a2 = S.add("scalar", lambda e:
                           e.activation(out=scol[:, 6:7], in_=scol[:, 5:6], func=AF.Ln, bias=epscol),
                           wait=[("v", v5)], inc="a")
                a3 = S.add("scalar", lambda e:
                           e.activation(out=scol[:, 7:8], in_=scol[:, 6:7], func=AF.Exp, scale=-0.5),
                           wait=[("a", a2)], inc="a")
                v6 = S.add("vector", lambda e, t=xt[i], o=ytile[i]:
                           e.tensor_scalar(out=o, in0=t, scalar1=scol[:, 2:3],
                                           scalar2=scol[:, 7:8], op0=ALU.subtract, op1=ALU.mult),
                           wait=[("a", a3), ("t", st["ytile_free"][i])], inc="v")
                last_tt = 0
                for c in range(DC):
                    slot = c % 2
                    dst = pt16[:, slot * 1024: slot * 1024 + 128]
                    tt = S.add("tensor", lambda e, yt=ytile[i], c=c, dst=dst:
                               e.transpose(out=dst, in_=yt[:, c * 128:(c + 1) * 128],
                                           identity=id16),
                               wait=[("v", v6), ("v", st["pt_free"][slot])], inc="t")
                    ev = S.add("vector", lambda e, dst=dst, c=c, nt=nt:
                               e.tensor_copy(out=yfm[:, c * NSEQ + nt * 128: c * NSEQ + (nt + 1) * 128],
                                             in_=dst),
                               wait=[("t", tt)], inc="v")
                    st["pt_free"][slot] = ev
                    last_tt = tt
                st["ytile_free"][i] = last_tt
                last_xtr = 0
                for c in range(DC):
                    if c % 2 == 0:
                        ps, pv, pa = psA, st["psA_v"], st["psA_a"]
                    else:
                        ps, pv, pa = psB, st["psB_v"], st["psB_a"]
                    tt = S.add("tensor", lambda e, t=xt[i], c=c, ps=ps:
                               e.transpose(out=ps[0:128, 0:128], in_=t[:, c * 128:(c + 1) * 128],
                                           identity=id32),
                               wait=[("ld", L), ("v", pv), ("a", pa)], inc="t")
                    ev = S.add("vector", lambda e, ps=ps, c=c, o=xstage[i]:
                               e.tensor_copy(out=o[:, c * 128:(c + 1) * 128], in_=ps[0:128, 0:128]),
                               wait=[("t", tt), ("ld", st["xstage_free"][i])], inc="v")
                    if c % 2 == 0:
                        st["psA_v"] = ev
                    else:
                        st["psB_v"] = ev
                    last_xtr = tt
                st["xt_free_v"][i] = v6
                st["xt_free_t"][i] = last_xtr
                sl = S.add("sync", lambda e, nt=nt, o=xstage[i]:
                           e.dma_start(
                               out=xfm_scr.rearrange("(c p) (nt j) -> p c nt j", p=128, nt=NT)[:, :, nt, :],
                               in_=o.rearrange("p (c j) -> p c j", c=DC)),
                           wait=[("v", S.val("v"))], inc="ld", inc_n=16)
                st["xstage_free"][i] = sl
            yfm_v = S.val("v")

            for nt in range(NT):
                for ob in range(2):
                    par = (nt * 2 + ob) % 2
                    if par == 0:
                        ps, pv, pa = psA, st["psA_v"], st["psA_a"]
                    else:
                        ps, pv, pa = psB, st["psB_v"], st["psB_a"]
                    for ic in range(DC):
                        S.add("tensor", lambda e, nt=nt, ob=ob, ic=ic, ps=ps:
                              e.matmul(out=ps[0:128, 0:512],
                                       lhsT=yfm[:, ic * NSEQ + nt * 128: ic * NSEQ + (nt + 1) * 128],
                                       rhs=wslot[0][:, ic * D + ob * 512: ic * D + (ob + 1) * 512],
                                       start=(ic == 0), stop=False),
                              wait=([("v", yfm_v), ("v", pv), ("a", pa), ("ld", wv_ld)]
                                    if ic == 0 else []),
                              inc="t")
                    mmv = S.add("tensor", lambda e, ob=ob, ps=ps:
                                e.matmul(out=ps[0:128, 0:512], lhsT=ones_row[0:1, :],
                                         rhs=brv16[0:1, ob * 512:(ob + 1) * 512],
                                         start=False, stop=True),
                                wait=[("_", brv_tok)], inc="t")
                    ev = S.add("vector", lambda e, nt=nt, ob=ob, ps=ps:
                               e.tensor_copy(
                                   out=Vt[:, (nt * HEADS + ob * 8) * 65: (nt * HEADS + ob * 8 + 8) * 65]
                                   .rearrange("p (h j) -> p h j", j=65)[:, :, 0:64],
                                   in_=ps[0:128, 0:512].rearrange("p (h j) -> p h j", j=64)),
                               wait=[("t", mmv)], inc="v")
                    if par == 0:
                        st["psA_v"] = ev
                    else:
                        st["psB_v"] = ev
            vproj_t = S.val("t")

            wq_ld = S.add("sync", lambda e: e.dma_start(out=wslot[0], in_=wprep["wq"]),
                          wait=[("t", vproj_t)], inc="ld", inc_n=16)

            for p in range(8):
                for (widx, wld, dstb, bcol0) in ((0, wq_ld, Qb, 0), (1, wk_ld, Kb, 8)):
                    for nb in range(4):
                        par = nb % 2
                        if par == 0:
                            ps, pv, pa = psA, st["psA_v"], st["psA_a"]
                        else:
                            ps, pv, pa = psB, st["psB_v"], st["psB_a"]
                        for ic in range(DC):
                            S.add("tensor", lambda e, p=p, nb=nb, ic=ic, ps=ps, widx=widx:
                                  e.matmul(out=ps[0:128, 0:512],
                                           lhsT=wslot[widx][:, ic * D + p * 128: ic * D + (p + 1) * 128],
                                           rhs=yfm[:, ic * NSEQ + nb * 512: ic * NSEQ + (nb + 1) * 512],
                                           start=(ic == 0), stop=(ic == DC - 1)),
                                  wait=([("v", yfm_v), ("v", pv), ("a", pa), ("ld", wld)]
                                        if ic == 0 else []),
                                  inc="t")
                        mm = S.val("t")
                        ev = S.add("vector", lambda e, p=p, nb=nb, ps=ps, dstb=dstb, col=bcol0 + p:
                                   e.tensor_scalar_add(
                                       out=dstb[:, p * NSEQ + nb * 512: p * NSEQ + (nb + 1) * 512],
                                       in0=ps[0:128, 0:512], scalar1=bcol[:, col:col + 1]),
                                   wait=[("t", mm), ("v", bias_v)], inc="v")
                        if par == 0:
                            st["psA_v"] = ev
                        else:
                            st["psB_v"] = ev
            qk_v = S.val("v")
            wo_ld = S.add("sync", lambda e: e.dma_start(out=wslot[1], in_=wprep["wo"]),
                          wait=[("t", S.val("t"))], inc="ld", inc_n=16)

            for h in range(HEADS):
                p, hh = h // 2, h % 2
                base = hh * 64
                for (src, scl, scr_t) in ((Kb, 1.0, k2_scr), (Qb, 0.25, q2_scr)):
                    for nb in range(4):
                        sqi = nb % 2
                        sv = S.add("vector", lambda e, p=p, base=base, nb=nb, sqi=sqi, src=src:
                                   e.tensor_mul(out=sq512[sqi][0:64, :],
                                                in0=src[base:base + 64,
                                                        p * NSEQ + nb * 512: p * NSEQ + (nb + 1) * 512],
                                                in1=src[base:base + 64,
                                                        p * NSEQ + nb * 512: p * NSEQ + (nb + 1) * 512]),
                                   wait=[("t", st["sq_rd"][sqi])], inc="v")
                        mm = S.add("tensor", lambda e, nb=nb, sqi=sqi:
                                   e.matmul(out=psO[0:1, (nb % 2) * 512:(nb % 2) * 512 + 512],
                                            lhsT=ones_col[0:64, 0:1], rhs=sq512[sqi][0:64, :],
                                            start=True, stop=True),
                                   wait=[("v", sv), ("v", st["psO_v"])], inc="t")
                        st["sq_rd"][sqi] = mm
                        st["psO_v"] = S.add(
                            "vector", lambda e, nb=nb, scl=scl:
                            e.tensor_scalar_mul(out=q2row16[0:1, nb * 512:(nb + 1) * 512],
                                                in0=psO[0:1, (nb % 2) * 512:(nb % 2) * 512 + 512],
                                                scalar1=scl),
                            wait=[("t", mm)] + ([("g", st["q2row_g"])] if nb == 0 else []),
                            inc="v")
                    st["q2row_g"] = S.add("gpsimd", lambda e, h=h, scr_t=scr_t:
                                          e.dma_start(out=scr_t[h:h + 1, :], in_=q2row16),
                                          wait=[("v", st["psO_v"])], inc="g", inc_n=16)
            aug_rows_f = S.dma_fence("g")

            def emit_aug(h):
                i = h % 2
                g1 = S.add("gpsimd", lambda e, h=h, i=i:
                           e.dma_start(out=kaug[i][0:1, :], in_=k2_scr[h:h + 1, :]),
                           wait=aug_rows_f + [("t", st["kaug_rd"][i])],
                           inc="g", inc_n=16)
                g2 = S.add("gpsimd", lambda e, h=h, i=i:
                           e.dma_start(out=qaug[i][1:2, :], in_=q2_scr[h:h + 1, :]),
                           wait=[("t", st["qaug_rd"][i])], inc="g", inc_n=16)
                st["q2_g"][h] = g2

            emit_aug(0)
            for h in range(HEADS):
                i = h % 2
                p, hh = h // 2, h % 2
                base = hh * 64
                if h + 1 < HEADS:
                    emit_aug(h + 1)
                aug_g = st["q2_g"][h]
                for qh in range(2):
                    pvmm = 0
                    for kt in range(NT):
                        par = kt % 2
                        if par == 0:
                            ps, pv, pa = psA, st["psA_v"], st["psA_a"]
                        else:
                            ps, pv, pa = psB, st["psB_v"], st["psB_a"]
                        last_mm2 = 0
                        for half in range(2):
                            S.add("tensor", lambda e, p=p, base=base, kt=kt, qh=qh, half=half, ps=ps:
                                  e.matmul(out=ps[0:128, half * 512:(half + 1) * 512],
                                           lhsT=Kb[base:base + 64,
                                                   p * NSEQ + kt * 128: p * NSEQ + (kt + 1) * 128],
                                           rhs=Qb[base:base + 64,
                                                  p * NSEQ + qh * 1024 + half * 512:
                                                  p * NSEQ + qh * 1024 + (half + 1) * 512],
                                           start=True, stop=False),
                                  wait=([("v", pv), ("a", pa), ("v", qk_v)] if half == 0 else []),
                                  inc="t")
                            last_mm2 = S.add(
                                "tensor", lambda e, kt=kt, qh=qh, half=half, ps=ps, i=i:
                                e.matmul(out=ps[0:128, half * 512:(half + 1) * 512],
                                         lhsT=kaug[i][0:2, kt * 128:(kt + 1) * 128],
                                         rhs=qaug[i][0:2, qh * 1024 + half * 512:
                                                     qh * 1024 + (half + 1) * 512],
                                         start=False, stop=True),
                                wait=([("g", aug_g)] if (kt == 0 and half == 0) else []),
                                inc="t")
                        st["kaug_rd"][i] = last_mm2
                        st["qaug_rd"][i] = last_mm2
                        aln = S.add("scalar", lambda e, ps=ps:
                                    e.activation(out=Lbuf, in_=ps[0:128, 0:1024], func=AF.Ln),
                                    wait=[("t", last_mm2)], inc="a")
                        if par == 0:
                            st["psA_a"] = aln
                        else:
                            st["psB_a"] = aln
                        S.add("scalar", lambda e:
                              e.activation(out=Ubuf, in_=Lbuf, func=AF.Exp, scale=0.5),
                              inc="a")
                        pb = kt % 2
                        aexp = S.add("scalar", lambda e, pb=pb:
                                     e.activation(out=Pbuf[pb], in_=Ubuf, func=AF.Exp, scale=-1.0),
                                     wait=[("t", st["P_rd"][pb])], inc="a")
                        for half in range(2):
                            pvmm = S.add("tensor", lambda e, kt=kt, h=h, half=half, pb=pb:
                                         e.matmul(out=psO[0:65, half * 512:(half + 1) * 512],
                                                  lhsT=Vt[:, (kt * HEADS + h) * 65:
                                                          (kt * HEADS + h + 1) * 65],
                                                  rhs=Pbuf[pb][:, half * 512:(half + 1) * 512],
                                                  start=(kt == 0), stop=(kt == NT - 1)),
                                         wait=([("a", aexp)] if half == 0 else []) +
                                              ([("v", st["psO_v"])] if (kt == 0 and half == 0) else []),
                                         inc="t")
                        st["P_rd"][pb] = pvmm
                    def recip_fn(e):
                        with nc.allow_low_precision(reason="softmax denom recip to bf16 "
                                                    "(same grade as bf16 matmul operand cast)"):
                            return e.reciprocal(out=rrow, in_=psO[64:65, 0:1024])
                    vr = S.add("vector", recip_fn, wait=[("t", pvmm)], inc="v")
                    rmm = S.add("tensor", lambda e:
                                e.matmul(out=psA[0:64, 0:512], lhsT=ones_row[0:1, 0:64],
                                         rhs=rrow[0:1, 0:512], start=True, stop=True),
                                wait=[("v", vr), ("a", st["psA_a"]), ("v", st["psA_v"])],
                                inc="t")
                    rmm2 = S.add("tensor", lambda e:
                                 e.matmul(out=psA[0:64, 512:1024], lhsT=ones_row[0:1, 0:64],
                                          rhs=rrow[0:1, 512:1024], start=True, stop=True),
                                 inc="t")
                    vR = S.add("vector", lambda e:
                               e.tensor_copy(out=Rsb, in_=psA[0:64, 0:1024]),
                               wait=[("t", rmm2)], inc="v")
                    st["psA_v"] = vR
                    oi = qh
                    ouev = S.add("vector", lambda e, oi=oi:
                                 e.tensor_mul(out=oust[oi], in0=psO[0:64, 0:1024], in1=Rsb),
                                 wait=[("v", vR), ("ld", st["oust_ld"][oi])], inc="v")
                    st["psO_v"] = ouev
                    st["oust_ld"][oi] = S.add(
                        "sync", lambda e, h=h, qh=qh, oi=oi:
                        e.dma_start(out=ou_scr[h * 64:(h + 1) * 64, qh * 1024:(qh + 1) * 1024],
                                    in_=oust[oi]),
                        wait=[("v", ouev)], inc="ld", inc_n=16)

            ou_done_f = S.dma_fence("ld")
            for nb in range(4):
                lo = S.add("sync", lambda e, nb=nb:
                           e.dma_start(out=Oun.rearrange("p (pr j) -> p pr j", pr=8),
                                       in_=ou_scr.rearrange("(pr p) (nb j) -> p pr nb j",
                                                            p=128, nb=4)[:, :, nb, :]),
                           wait=ou_done_f + [("t", S.val("t"))], inc="ld", inc_n=16)
                for g2 in range(2):
                    for pr in range(8):
                        for oo in range(4):
                            oc = g2 * 4 + oo
                            ps = (psA, psB)[oo // 2]
                            half = oo % 2
                            S.add("tensor",
                                  lambda e, pr=pr, oc=oc, nb=nb, ps=ps, half=half:
                                  e.matmul(out=ps[0:128, half * 512:(half + 1) * 512],
                                           lhsT=wslot[1][:, pr * D + oc * 128: pr * D + (oc + 1) * 128],
                                           rhs=Oun[:, pr * 512:(pr + 1) * 512],
                                           start=(pr == 0), stop=(pr == 7)),
                                  wait=([("ld", lo), ("ld", wo_ld),
                                         ("v", st["psA_v"] if oo // 2 == 0 else st["psB_v"]),
                                         ("a", st["psA_a"] if oo // 2 == 0 else st["psB_a"])]
                                        if pr == 0 and half == 0 else []),
                                  inc="t")
                    mm_g = S.val("t")
                    for oo in range(4):
                        oc = g2 * 4 + oo
                        ps = (psA, psB)[oo // 2]
                        half = oo % 2
                        i = oc % 2
                        lx = S.add("sync", lambda e, oc=oc, nb=nb, i=i:
                                   e.dma_start(out=xres[i],
                                               in_=xfm_scr[oc * 128:(oc + 1) * 128,
                                                           nb * 512:(nb + 1) * 512]),
                                   wait=[("v", st["xres_v"][i])], inc="ld", inc_n=16)
                        ev = S.add("vector", lambda e, oc=oc, i=i, ps=ps, half=half:
                                   e.scalar_tensor_tensor(out=x1st[i], in0=ps[0:128, half * 512:(half + 1) * 512],
                                                          scalar=bcol[:, 16 + oc:17 + oc],
                                                          in1=xres[i],
                                                          op0=ALU.add, op1=ALU.add),
                                   wait=[("t", mm_g), ("ld", lx), ("ld", st["x1st_ld"][i])],
                                   inc="v")
                        st["xres_v"][i] = ev
                        if oo // 2 == 0:
                            st["psA_v"] = ev
                        else:
                            st["psB_v"] = ev
                        sl = S.add("sync", lambda e, b=b, oc=oc, nb=nb, i=i:
                                   e.dma_start(out=x1_out[b, oc * 128:(oc + 1) * 128,
                                                          nb * 512:(nb + 1) * 512],
                                               in_=x1st[i]),
                                   wait=[("v", ev)], inc="ld", inc_n=16)
                        st["x1st_ld"][i] = sl
                        cv = S.add("vector", lambda e, i=i:
                                   e.tensor_copy(out=x1c, in_=x1st[i]),
                                   wait=[("t", st.get("x1c_rd", 0))], inc="v")
                        mm1 = S.add("tensor", lambda e, nb=nb, oc=oc:
                                    e.matmul(out=psO[0:1, 0:512], lhsT=ones_col[0:128, 0:1],
                                             rhs=x1c, start=(oc == 0), stop=(oc == 7)),
                                    wait=[("v", cv)] +
                                         ([("v", st["psO_v"])] if oc == 0 else []),
                                    inc="t")
                        sv = S.add("vector", lambda e:
                                   e.tensor_mul(out=sq512[0][0:128, :], in0=x1c, in1=x1c),
                                   wait=[("t", st["sq_rd"][0])], inc="v")
                        mm2 = S.add("tensor", lambda e, nb=nb, oc=oc:
                                    e.matmul(out=psO[0:1, 512:1024], lhsT=ones_col[0:128, 0:1],
                                             rhs=sq512[0][0:128, :], start=(oc == 0), stop=(oc == 7)),
                                    wait=[("v", sv)], inc="t")
                        st["sq_rd"][0] = mm2
                        st["x1c_rd"] = mm2
                s_ev = S.add("vector", lambda e, nb=nb:
                             e.tensor_copy(out=sum16[0:1, nb * 512:(nb + 1) * 512],
                                           in_=psO[0:1, 0:512]),
                             wait=[("t", S.val("t"))], inc="v")
                st["psO_v"] = S.add("vector", lambda e, nb=nb:
                                    e.tensor_copy(out=ss16[0:1, nb * 512:(nb + 1) * 512],
                                                  in_=psO[0:1, 512:1024]),
                                    inc="v")

            rv1 = S.add("vector", lambda e: e.tensor_scalar_mul(out=m16, in0=sum16,
                                                                scalar1=1.0 / D),
                        wait=[("v", S.val("v"))], inc="v")
            rv2 = S.add("vector", lambda e: e.tensor_mul(out=var16, in0=m16, in1=m16),
                        wait=[("v", rv1)], inc="v")
            rv3 = S.add("vector", lambda e: e.tensor_scalar_mul(out=ssd16, in0=ss16,
                                                                scalar1=1.0 / D),
                        inc="v")
            vv = S.add("vector", lambda e: e.tensor_sub(out=var16, in0=ssd16, in1=var16),
                       wait=[("v", rv3), ("v", rv2)], inc="v")
            aL = S.add("scalar", lambda e: e.activation(out=ssd16, in_=var16, func=AF.Ln,
                                                        bias=epscol[0:1, 0:1]),
                       wait=[("v", vv)], inc="a")
            ar_ = S.add("scalar", lambda e: e.activation(out=rstd16, in_=ssd16, func=AF.Exp,
                                                         scale=-0.5), wait=[("a", aL)], inc="a")
            for nb in range(4):
                par = nb % 2
                ps = (psA, psB)[par]
                pv = st["psA_v"] if par == 0 else st["psB_v"]
                pa = st["psA_a"] if par == 0 else st["psB_a"]
                m1 = S.add("tensor", lambda e, nb=nb, ps=ps:
                           e.matmul(out=ps[0:128, 0:512], lhsT=ones_row[0:1, :],
                                    rhs=m16[0:1, nb * 512:(nb + 1) * 512], start=True, stop=True),
                           wait=[("a", ar_), ("v", pv), ("a", pa)], inc="t")
                m2 = S.add("tensor", lambda e, nb=nb, ps=ps:
                           e.matmul(out=ps[0:128, 512:1024], lhsT=ones_row[0:1, :],
                                    rhs=rstd16[0:1, nb * 512:(nb + 1) * 512], start=True, stop=True),
                           inc="t")
                ev1 = S.add("vector", lambda e, nb=nb, ps=ps:
                            e.tensor_copy(out=Mt[:, nb * 512:(nb + 1) * 512], in_=ps[0:128, 0:512]),
                            wait=[("t", m2)], inc="v")
                ev2 = S.add("vector", lambda e, nb=nb, ps=ps:
                            e.tensor_copy(out=Rt[:, nb * 512:(nb + 1) * 512], in_=ps[0:128, 512:1024]),
                            inc="v")
                if par == 0:
                    st["psA_v"] = ev2
                else:
                    st["psB_v"] = ev2
            bc_v = S.val("v")
            for oc in range(DC):
                for nb in range(4):
                    k = oc * 4 + nb
                    i = k % 2
                    lx = S.add("sync", lambda e, b=b, oc=oc, nb=nb, i=i:
                               e.dma_start(out=xres[i],
                                           in_=x1_out[b, oc * 128:(oc + 1) * 128,
                                                      nb * 512:(nb + 1) * 512]),
                               wait=[("v", st["xres_v"][i])], inc="ld", inc_n=16)
                    v1_ = S.add("vector", lambda e, i=i, nb=nb:
                                e.tensor_sub(out=yltmp, in0=xres[i],
                                             in1=Mt[:, nb * 512:(nb + 1) * 512]),
                                wait=[("ld", lx), ("v", bc_v)], inc="v")
                    st["xres_v"][i] = v1_
                    v2_ = S.add("vector", lambda e, i=i, nb=nb:
                                e.tensor_mul(out=ylt[i], in0=yltmp,
                                             in1=Rt[:, nb * 512:(nb + 1) * 512]),
                                wait=[("ld", st["ylt_ld"][i])], inc="v")
                    sl = S.add("sync", lambda e, b=b, oc=oc, nb=nb, i=i:
                               e.dma_start(out=y_out[b, oc * 128:(oc + 1) * 128,
                                                     nb * 512:(nb + 1) * 512],
                                           in_=ylt[i]),
                               wait=[("v", v2_)], inc="ld", inc_n=16)
                    st["ylt_ld"][i] = sl

        if C is None:
            S.add("sync", None, wait=S.dma_fence("ld") + S.dma_fence("g"))
            S.emit(nc, stack)
    return nc


def build_p3(nloc=NSEQ // NCORES, nsteps=32, C=None):
    if C is None:
        nc = bass.Bass("TRN2", target_bir_lowering=False, detect_race_conditions=False)
        S = Sched(SEMS)
        y_in = nc.dram_tensor("y_re", [nsteps, D, nloc], BF16, kind="ExternalInput").ap()
        wn = ("wf", "wi", "wc", "uf", "ui", "uc")
        w_dr = {n: nc.dram_tensor(n, [D, D], F32, kind="ExternalInput").ap() for n in wn}
        b_dr = {n: nc.dram_tensor(n, [D, 1], F32, kind="ExternalInput").ap()
                for n in ("bf", "bi", "bc", "in_c")}
        c_out = nc.dram_tensor("c_fin", [D, nloc], F32, kind="ExternalOutput").ap()
        y_flat = None
    else:
        nc, S = C.nc, C.S
        y_in = None
        y_flat = (C.io["y_all"].rearrange("a b c -> (a b c)")
                  .rearrange("(x one) -> x one", one=1))
        w_dr = {n: C.io["l0f_" + n] for n in ("wf", "wi", "wc", "uf", "ui", "uc")}
        b_dr = {n: C.io["l0f_" + n] for n in ("bf", "bi", "bc", "in_c")}
        c_out = C.io["c_mine"]

    ident16_c = _const(nc, C, "ident16", np.eye(128, dtype=BF))
    NG = 24
    NL = nloc

    with ExitStack() as stack:
        if C is None:
            AR = Arena(stack.enter_context(
                nc.sbuf_tensor("AR", [128, 53160], F32)).ap(), 53160)
        else:
            AR = C.AR
            AR.reset(0)
        e_ = stack.enter_context

        id16 = AR.bf16(128, 128)
        bcol = AR.f32(128, NG)
        iccol = AR.f32(128, 8)
        GT = AR.bf16(128, DC * 3 * D)
        UT = AR.bf16(128, DC * 3 * D)
        cbuf = AR.f32(128, DC * NL)
        c16 = AR.bf16(128, DC * NL)
        gates = AR.f32(128, NG * NL)
        tmpc = AR.f32(128, NL)
        tmpd = AR.f32(128, NL)
        yt = [AR.bf16(128, DC * NL), AR.bf16(128, DC * NL)]
        wtmp32 = AR.f32(128, D)
        wtmp16 = AR.bf16(128, D)
        if C is not None:
            pidb = AR.f32(128, 1).bitcast(mybir.dt.uint32)
            iota_i = AR.f32(128, 256).bitcast(mybir.dt.int32)
            idx_i = AR.f32(128, 256).bitcast(mybir.dt.int32)
            pidf = AR.f32(128, 1)
            iotaf = AR.f32(128, 256)
            idxf = AR.f32(128, 256)

        if C is None:
            psG = [e_(nc.psum_tensor(f"psG{i}", [128, 512], F32)).ap() for i in range(4)]
            pt16 = e_(nc.psum_tensor("pt16", [128, 2048], BF16)).ap()
        else:
            psG = [C.psA[:, 0:512], C.psA[:, 512:1024],
                   C.psB[:, 0:512], C.psB[:, 512:1024]]
            pt16 = C.pt16

        l_id = S.add("sync", lambda e: e.dma_start(out=id16, in_=ident16_c.ap()),
                     inc="ld", inc_n=16)
        wstate = dict(cast_done=0, tr_done=0, ps_done=[0, 0])

        def emit_wt(w_dram, dst_sb, col0):
            for r in range(DC):
                L = S.add("sync", lambda e, wd=w_dram, r=r:
                          e.dma_start(out=wtmp32, in_=wd[r * 128:(r + 1) * 128, :]),
                          wait=[("v", wstate["cast_done"])], inc="ld", inc_n=16)
                cv = S.add("vector", lambda e:
                           e.tensor_copy(out=wtmp16, in_=wtmp32),
                           wait=[("_", L), ("t", wstate["tr_done"])], inc="v")
                wstate["cast_done"] = cv
                last_t = 0
                for c in range(DC):
                    slot = c % 2
                    dst_ps = pt16[:, slot * 1024: slot * 1024 + 128]
                    tt = S.add("tensor", lambda e, c=c, dst=dst_ps:
                               e.transpose(out=dst, in_=wtmp16[:, c * 128:(c + 1) * 128],
                                           identity=id16),
                               wait=[("v", cv), ("v", wstate["ps_done"][slot]),
                                     ("_", l_id)], inc="t")
                    ev = S.add("vector", lambda e, dst=dst_ps, o=dst_sb, c=c, r=r, col0=col0:
                               e.tensor_copy(out=o[:, c * 3 * D + col0 + r * 128:
                                                   c * 3 * D + col0 + (r + 1) * 128],
                                             in_=dst),
                               wait=[("t", tt)], inc="v")
                    wstate["ps_done"][slot] = ev
                    last_t = tt
                wstate["tr_done"] = last_t

        emit_wt(w_dr["wf"], GT, 0)
        emit_wt(w_dr["wi"], GT, D)
        emit_wt(w_dr["wc"], GT, 2 * D)
        emit_wt(w_dr["uf"], UT, 0)
        emit_wt(w_dr["ui"], UT, D)
        emit_wt(w_dr["uc"], UT, 2 * D)

        for j, n in enumerate(("bf", "bi", "bc")):
            for c in range(DC):
                S.add("sync", lambda e, bd=b_dr[n], c=c, col=j * 8 + c:
                      e.dma_start(out=bcol[:, col:col + 1],
                                  in_=bd[c * 128:(c + 1) * 128, 0:1]),
                      inc="ld", inc_n=16)
        for c in range(DC):
            S.add("sync", lambda e, c=c:
                  e.dma_start(out=iccol[:, c:c + 1],
                              in_=b_dr["in_c"][c * 128:(c + 1) * 128, 0:1]),
                  inc="ld", inc_n=16)
        pre_f = S.dma_fence("ld")
        ic_a = 0
        for ch in range(DC):
            ic_a = S.add("scalar", lambda e, ch=ch:
                         e.activation(out=cbuf[:, ch * NL:(ch + 1) * NL],
                                      in_=cbuf[:, ch * NL:(ch + 1) * NL],
                                      func=AF.Identity, scale=0.0,
                                      bias=iccol[:, ch:ch + 1]),
                         wait=pre_f if ch == 0 else [], inc="a")
        c16_v = 0
        for ch in range(DC):
            c16_v = S.add("vector", lambda e, ch=ch:
                          e.tensor_copy(out=c16[:, ch * NL:(ch + 1) * NL],
                                        in_=cbuf[:, ch * NL:(ch + 1) * NL]),
                          wait=[("a", ic_a)], inc="v")

        st = dict(yt_rd=[0, 0], ps_a=[0, 0, 0, 0], gates_v=0, ut_mm=0)

        idx_wait = []
        if C is not None:
            Lp = S.add("sync", lambda e: e.dma_start(
                out=pidb[:, 0:1],
                in_=nc.partition_id_tensor.ap().broadcast_to((128, 1))),
                inc="ld", inc_n=16)
            g1 = S.add("gpsimd", lambda e: e.iota(
                iota_i[:, :], pattern=[[128 * NSEQ // NL, 256]], base=0,
                channel_multiplier=NSEQ // NL), inc="gp")
            v1 = S.add("vector", lambda e: e.tensor_copy(out=pidf, in_=pidb[:, 0:1]),
                       wait=[("_", Lp)], inc="v")
            v2 = S.add("vector", lambda e: e.tensor_copy(out=iotaf, in_=iota_i[:, :]),
                       wait=[("gp", g1)], inc="v")
            v3 = S.add("vector", lambda e: e.tensor_scalar(
                out=idxf, in0=iotaf, scalar1=pidf, scalar2=float(NL),
                op0=ALU.add, op1=ALU.mult), wait=[("v", v1), ("v", v2)], inc="v")
            v4 = S.add("vector", lambda e: e.tensor_copy(out=idx_i[:, :], in_=idxf),
                       wait=[("v", v3)], inc="v")
            idx_wait = [("v", v4)] + C.y_ready

        prep_v = S.val("v")
        ytok = [None, None]
        for t in range(nsteps):
            i = t % 2
            if C is None:
                ytok[i] = [S.add("sync", lambda e, t=t, i=i:
                                 e.dma_start(out=yt[i].rearrange("p (c n) -> p c n", c=DC),
                                             in_=y_in[t].rearrange("(c p) n -> p c n", p=128)),
                                 wait=[("t", st["yt_rd"][i])], inc="ld", inc_n=16)]
            else:
                toks = []
                for ch in range(DC):
                    w = ([("t", st["yt_rd"][i])] + idx_wait) if ch == 0 else []
                    k = t * DC + ch
                    toks.append(S.add(
                        "gpsimd", lambda e, k=k, ch=ch, i=i:
                        e.indirect_dma_start(
                            out=yt[i][:, ch * NL:(ch + 1) * NL], out_offset=None,
                            in_=y_flat,
                            in_offset=bass.IndirectOffsetOnAxis(ap=idx_i[:, k:k + 1],
                                                                axis=0)),
                        wait=w, inc="g", inc_n=16))
                ytok[i] = toks
            last_mm = 0
            for g in range(NG):
                ps = psG[g % 4]
                for ic in range(DC):
                    S.add("tensor", lambda e, g=g, ic=ic, ps=ps, i=i:
                          e.matmul(out=ps[0:128, 0:NL],
                                   lhsT=GT[:, ic * 3 * D + g * 128: ic * 3 * D + (g + 1) * 128],
                                   rhs=yt[i][:, ic * NL:(ic + 1) * NL],
                                   start=(ic == 0), stop=False),
                          wait=([("_", tk) for tk in ytok[i]] +
                                [("a", st["ps_a"][g % 4]),
                                 ("v", prep_v)] if ic == 0 else []),
                          inc="t")
                for ic in range(DC):
                    last_mm = S.add("tensor", lambda e, g=g, ic=ic, ps=ps:
                                    e.matmul(out=ps[0:128, 0:NL],
                                             lhsT=UT[:, ic * 3 * D + g * 128: ic * 3 * D + (g + 1) * 128],
                                             rhs=c16[:, ic * NL:(ic + 1) * NL],
                                             start=False, stop=(ic == DC - 1)),
                                    wait=([("v", c16_v)] if ic == 0 else []),
                                    inc="t")
                fn = AF.Tanh if g >= 16 else AF.Sigmoid
                st["ps_a"][g % 4] = S.add(
                    "scalar", lambda e, g=g, ps=ps, fn=fn:
                    e.activation(out=gates[:, g * NL:(g + 1) * NL], in_=ps[0:128, 0:NL],
                                 func=fn, bias=bcol[:, g:g + 1]),
                    wait=[("t", last_mm), ("v", st["gates_v"])], inc="a")
            st["yt_rd"][i] = last_mm
            st["ut_mm"] = last_mm
            gate_a = S.val("a")
            for ch in range(DC):
                S.add("vector", lambda e, ch=ch:
                      e.tensor_mul(out=tmpc, in0=gates[:, ch * NL:(ch + 1) * NL],
                                   in1=cbuf[:, ch * NL:(ch + 1) * NL]),
                      wait=[("a", gate_a)] if ch == 0 else [], inc="v")
                td = S.add("vector", lambda e, ch=ch:
                           e.tensor_mul(out=tmpd, in0=gates[:, (8 + ch) * NL:(9 + ch) * NL],
                                        in1=gates[:, (16 + ch) * NL:(17 + ch) * NL]),
                           inc="v")
                ta = S.add("vector", lambda e, ch=ch:
                           e.tensor_add(out=cbuf[:, ch * NL:(ch + 1) * NL], in0=tmpc, in1=tmpd),
                           wait=[("v", td)], inc="v")
                c16_v = S.add("vector", lambda e, ch=ch:
                              e.tensor_copy(out=c16[:, ch * NL:(ch + 1) * NL],
                                            in_=cbuf[:, ch * NL:(ch + 1) * NL]),
                              wait=([("t", st["ut_mm"])] if ch == 0 else []) + [("v", ta)],
                              inc="v")
            st["gates_v"] = S.val("v")

        fin = S.add("sync", lambda e:
                    e.dma_start(out=c_out.rearrange("(c p) n -> p c n", p=128),
                                in_=cbuf.rearrange("p (c n) -> p c n", c=DC)),
                    wait=[("v", S.val("v"))], inc="ld", inc_n=16)
        if C is None:
            S.add("sync", None, wait=S.dma_fence("ld"))
            S.emit(nc, stack)
    return nc


def build_p4(n_batches=NB, C=None):
    if C is None:
        nc = bass.Bass("TRN2", target_bir_lowering=False, detect_race_conditions=False)
        S = Sched(SEMS)
        x1_in = nc.dram_tensor("x1_fm", [n_batches, D, NSEQ], F32, kind="ExternalInput").ap()
        y_in = nc.dram_tensor("yL_fm", [n_batches, D, NSEQ], BF16, kind="ExternalInput").ap()
        c_in = nc.dram_tensor("c_full", [D, NSEQ], F32, kind="ExternalInput").ap()
        wo_dr = nc.dram_tensor("wo", [D, D], F32, kind="ExternalInput").ap()
        uo_dr = nc.dram_tensor("uo", [D, D], F32, kind="ExternalInput").ap()
        bo_dr = nc.dram_tensor("bo", [D, 1], F32, kind="ExternalInput").ap()
        x2_out = nc.dram_tensor("x2_fm", [n_batches, D, NSEQ], F32, kind="ExternalOutput").ap()
        c_all = None
    else:
        nc, S = C.nc, C.S
        x1_in = C.io["x1_fm"]
        y_in = C.io["yL_fm"]
        c_in = None
        c_all = C.io["c_all"]
        wo_dr = C.io["l0f_wo"]
        uo_dr = C.io["l0f_uo"]
        bo_dr = C.io["l0f_bo"]
        x2_out = C.io["x2_fm"]

    ident16_c = _const(nc, C, "ident16", np.eye(128, dtype=BF))

    with ExitStack() as stack:
        if C is None:
            AR = Arena(stack.enter_context(
                nc.sbuf_tensor("AR", [128, 53160], F32)).ap(), 53160)
        else:
            AR = C.AR
            AR.reset(0)
        e_ = stack.enter_context

        id16 = AR.bf16(128, 128)
        bocol = AR.f32(128, 8)
        WoT = AR.bf16(128, DC * D)
        UoT = AR.bf16(128, DC * D)
        c16f = AR.bf16(128, DC * NSEQ)
        CU = AR.f32(128, DC * NSEQ)
        yb = AR.bf16(128, DC * NSEQ)
        ctmp = AR.f32(128, 512)
        x1t = [AR.f32(128, 512), AR.f32(128, 512)]
        ht_in = [AR.f32(128, 512), AR.f32(128, 512)]
        ht_out = [AR.f32(128, 512), AR.f32(128, 512)]
        x2t = [AR.f32(128, 512), AR.f32(128, 512)]
        wtmp32 = AR.f32(128, D)
        wtmp16 = AR.bf16(128, D)

        if C is None:
            psA = e_(nc.psum_tensor("psA", [128, 1024], F32)).ap()
            psB = e_(nc.psum_tensor("psB", [128, 1024], F32)).ap()
            pt16 = e_(nc.psum_tensor("pt16", [128, 2048], BF16)).ap()
        else:
            psA, psB, pt16 = C.psA, C.psB, C.pt16

        l_id = S.add("sync", lambda e: e.dma_start(out=id16, in_=ident16_c.ap()),
                     inc="ld", inc_n=16)
        wstate = dict(cast_done=0, tr_done=0, ps_done=[0, 0])

        def emit_wt(w_dram, dst_sb):
            for r in range(DC):
                L = S.add("sync", lambda e, wd=w_dram, r=r:
                          e.dma_start(out=wtmp32, in_=wd[r * 128:(r + 1) * 128, :]),
                          wait=[("v", wstate["cast_done"])], inc="ld", inc_n=16)
                cv = S.add("vector", lambda e:
                           e.tensor_copy(out=wtmp16, in_=wtmp32),
                           wait=[("_", L), ("t", wstate["tr_done"])], inc="v")
                wstate["cast_done"] = cv
                last_t = 0
                for c in range(DC):
                    slot = c % 2
                    dst_ps = pt16[:, slot * 1024: slot * 1024 + 128]
                    tt = S.add("tensor", lambda e, c=c, dst=dst_ps:
                               e.transpose(out=dst, in_=wtmp16[:, c * 128:(c + 1) * 128],
                                           identity=id16),
                               wait=[("v", cv), ("v", wstate["ps_done"][slot]),
                                     ("_", l_id)], inc="t")
                    ev = S.add("vector", lambda e, dst=dst_ps, o=dst_sb, c=c, r=r:
                               e.tensor_copy(out=o[:, c * D + r * 128: c * D + (r + 1) * 128],
                                             in_=dst),
                               wait=[("t", tt)], inc="v")
                    wstate["ps_done"][slot] = ev
                    last_t = tt
                wstate["tr_done"] = last_t

        emit_wt(wo_dr, WoT)
        emit_wt(uo_dr, UoT)
        for c in range(DC):
            S.add("sync", lambda e, c=c:
                  e.dma_start(out=bocol[:, c:c + 1], in_=bo_dr[c * 128:(c + 1) * 128, 0:1]),
                  inc="ld", inc_n=16)
        for c in range(DC):
            for nb in range(4):
                if C is None:
                    ltoks = [S.add("sync", lambda e, c=c, nb=nb:
                                   e.dma_start(out=ctmp,
                                               in_=c_in[c * 128:(c + 1) * 128,
                                                        nb * 512:(nb + 1) * 512]),
                                   wait=[("v", wstate.get("ctmp_v", 0))],
                                   inc="ld", inc_n=16)]
                else:
                    ltoks = [S.add("sync", lambda e, c=c, r=2 * nb:
                                   e.dma_start(out=ctmp[:, 0:256],
                                               in_=c_all[r, c * 128:(c + 1) * 128, :]),
                                   wait=[("v", wstate.get("ctmp_v", 0))] + C.c_ready,
                                   inc="ld", inc_n=16),
                             S.add("sync", lambda e, c=c, r=2 * nb + 1:
                                   e.dma_start(out=ctmp[:, 256:512],
                                               in_=c_all[r, c * 128:(c + 1) * 128, :]),
                                   inc="ld", inc_n=16)]
                wstate["ctmp_v"] = S.add("vector", lambda e, c=c, nb=nb:
                                         e.tensor_copy(out=c16f[:, c * NSEQ + nb * 512:
                                                                c * NSEQ + (nb + 1) * 512],
                                                       in_=ctmp),
                                         wait=[("_", t) for t in ltoks], inc="v")
        c16_v = S.val("v")
        prep_v = S.val("v")

        st = dict(ps_v=[0, 0], ps_a=[0, 0], x1_v=[0, 0], x2_ld=[0, 0], cu_rd=0)
        for oc in range(DC):
            for nb in range(4):
                par = (oc * 4 + nb) % 2
                ps = (psA, psB)[par]
                for ic in range(DC):
                    S.add("tensor", lambda e, oc=oc, nb=nb, ic=ic, ps=ps:
                          e.matmul(out=ps[0:128, 0:512],
                                   lhsT=UoT[:, ic * D + oc * 128: ic * D + (oc + 1) * 128],
                                   rhs=c16f[:, ic * NSEQ + nb * 512: ic * NSEQ + (nb + 1) * 512],
                                   start=(ic == 0), stop=(ic == DC - 1)),
                          wait=([("v", c16_v), ("v", st["ps_v"][par]), ("v", prep_v)]
                                if ic == 0 else []),
                          inc="t")
                mm = S.val("t")
                st["ps_v"][par] = S.add("vector", lambda e, oc=oc, nb=nb, ps=ps:
                                        e.tensor_copy(out=CU[:, oc * NSEQ + nb * 512:
                                                             oc * NSEQ + (nb + 1) * 512],
                                                      in_=ps[0:128, 0:512]),
                                        wait=[("t", mm)], inc="v")
        cu_v = S.val("v")

        for b in range(n_batches):
            ytok = S.add("sync", lambda e, b=b:
                         e.dma_start(out=yb.rearrange("p (c n) -> p c n", c=DC),
                                     in_=y_in[b].rearrange("(c p) n -> p c n", p=128)),
                         wait=[("t", st.get("yb_rd", 0))], inc="ld", inc_n=16)
            for oc in range(DC):
                for nb in range(4):
                    k = oc * 4 + nb
                    par = k % 2
                    i = k % 2
                    ps = (psA, psB)[par]
                    lx = S.add("sync", lambda e, b=b, oc=oc, nb=nb, i=i:
                               e.dma_start(out=x1t[i],
                                           in_=x1_in[b, oc * 128:(oc + 1) * 128,
                                                     nb * 512:(nb + 1) * 512]),
                               wait=[("v", st["x1_v"][i])], inc="ld", inc_n=16)
                    for ic in range(DC):
                        S.add("tensor", lambda e, oc=oc, nb=nb, ic=ic, ps=ps:
                              e.matmul(out=ps[0:128, 0:512],
                                       lhsT=WoT[:, ic * D + oc * 128: ic * D + (oc + 1) * 128],
                                       rhs=yb[:, ic * NSEQ + nb * 512: ic * NSEQ + (nb + 1) * 512],
                                       start=(ic == 0), stop=(ic == DC - 1)),
                              wait=([("_", ytok), ("v", st["ps_v"][par]),
                                     ("a", st["ps_a"][par]), ("v", cu_v)] if ic == 0 else []),
                              inc="t")
                    mm = S.val("t")
                    st["yb_rd"] = mm
                    sv = S.add("vector", lambda e, oc=oc, nb=nb, ps=ps, i=i:
                               e.scalar_tensor_tensor(out=ht_in[i], in0=ps[0:128, 0:512],
                                                      scalar=1.0,
                                                      in1=CU[:, oc * NSEQ + nb * 512:
                                                             oc * NSEQ + (nb + 1) * 512],
                                                      op0=ALU.mult, op1=ALU.add),
                               wait=[("t", mm), ("a", st["ps_a"][par])], inc="v")
                    st["ps_v"][par] = sv
                    av = S.add("scalar", lambda e, oc=oc, i=i:
                               e.activation(out=ht_out[i], in_=ht_in[i], func=AF.Tanh,
                                            bias=bocol[:, oc:oc + 1]),
                               wait=[("v", sv), ("v", st.get("htout_v", [0, 0])[i])], inc="a")
                    st["ps_a"][par] = av
                    x2v = S.add("vector", lambda e, i=i:
                                e.tensor_add(out=x2t[i], in0=ht_out[i], in1=x1t[i]),
                                wait=[("a", av), ("_", lx), ("_", st["x2_ld"][i])], inc="v")
                    st.setdefault("htout_v", [0, 0])[i] = x2v
                    st["x1_v"][i] = x2v
                    st["x2_ld"][i] = S.add("sync", lambda e, b=b, oc=oc, nb=nb, i=i:
                                           e.dma_start(out=x2_out[b, oc * 128:(oc + 1) * 128,
                                                                  nb * 512:(nb + 1) * 512],
                                                       in_=x2t[i]),
                                           wait=[("v", x2v)], inc="ld", inc_n=16)

        if C is None:
            S.add("sync", None, wait=S.dma_fence("ld") + S.dma_fence("g"))
            S.emit(nc, stack)
    return nc


def _rope_tables():
    inv = 10000.0 ** (-np.arange(0, HD, 2, dtype=np.float64) / HD)
    t = np.arange(NSEQ, dtype=np.float64)
    f = np.concatenate([inv, inv])
    ang = t[None, :] * f[:, None]
    cos = np.cos(ang)
    sin = np.sin(ang)
    return cos, sin


def _perm_block():
    P = np.zeros((64, 64), np.float64)
    for a in range(32):
        P[a + 32, a] = -1.0
    for a in range(32, 64):
        P[a - 32, a] = 1.0
    W = np.zeros((128, 128), np.float64)
    W[0:64, 0:64] = P
    W[64:128, 64:128] = P
    return W


def build_p5(n_batches=NB, C=None):
    if C is None:
        nc = bass.Bass("TRN2", target_bir_lowering=False, detect_race_conditions=False)
        S = Sched(SEMS)
        x2_in = nc.dram_tensor("x2_fm", [n_batches, D, NSEQ], F32, kind="ExternalInput").ap()
        w_dr = {n: nc.dram_tensor(n, [D, D], F32, kind="ExternalInput").ap()
                for n in ("wq", "wk", "wv", "wo")}
        b_dr = {n: nc.dram_tensor(n, [D, 1], F32, kind="ExternalInput").ap()
                for n in ("bq", "bk", "bv", "bo")}
        x3_out = nc.dram_tensor("x3_fm", [n_batches, D, NSEQ], F32, kind="ExternalOutput").ap()
    else:
        nc, S = C.nc, C.S
        x2_in = C.io["x2_fm"]
        w_dr = {n: C.io["l1a_" + n] for n in ("wq", "wk", "wv", "wo")}
        b_dr = {n: C.io["l1a_" + n] for n in ("bq", "bk", "bv", "bo")}
        x3_out = C.io["x3_fm"]
    ou_scr = nc.dram_tensor("p5_ou", [D, NSEQ], BF16, kind="Internal").ap()
    wprep = {n: nc.dram_tensor(f"p5_wp_{n}", [128, DC * D], BF16, kind="Internal").ap()
             for n in ("wq", "wk", "wv", "wo")}

    ident16_c = _const(nc, C, "ident16", np.eye(128, dtype=BF))
    cos64, sin64 = _rope_tables()
    cosq_c = _const(nc, C, "cosq", np.vstack([cos64, cos64]).astype(BF))
    sinq_c = _const(nc, C, "sinq", np.vstack([sin64, sin64]).astype(BF))
    perm_c = _const(nc, C, "permT", _perm_block().astype(BF))
    mask = np.zeros((128, 1920), np.float32)
    for p in range(128):
        u = np.arange(1920) - 896
        mask[p] = np.where(u >= p, 0.0, -1e30)
    mask_c = _const(nc, C, "cmask", mask.astype(BF))

    with ExitStack() as stack:
        if C is None:
            AR = Arena(stack.enter_context(
                nc.sbuf_tensor("AR", [128, 53160], F32)).ap(), 53160)
        else:
            AR = C.AR
            AR.reset(0)
        e_ = stack.enter_context

        id16 = AR.bf16(128, 128)
        permT = AR.bf16(128, 128)
        COSQ = AR.bf16(128, NSEQ)
        SINQ = AR.bf16(128, NSEQ)
        MSK = AR.bf16(128, 1920)
        bcol = AR.f32(128, 24)
        epscol = AR.f32(128, 1)
        ones_col = AR.bf16(128, 2)
        o_or = AR.block(64)
        ones_row = AR.bf16_at(0, 1, o_or, 128)
        rrow = AR.bf16(1, 1024)
        brv16 = AR.bf16(1, 1024)
        sq512 = [AR.bf16(128, 512)]
        wslot = [AR.bf16(128, DC * D), AR.bf16(128, DC * D)]
        mQ = AR.mark()
        wtmp32 = AR.f32(128, D)
        wtmp16 = AR.bf16(128, D)
        AR.reset(mQ)
        Qb = AR.bf16(128, 8 * NSEQ)
        Kb = AR.bf16(128, 8 * NSEQ)
        Vt = AR.bf16(128, NT * HEADS * (HD + 1))
        mC = AR.mark()
        y2 = AR.bf16(128, DC * NSEQ)
        endC_y = AR.mark()
        AR.reset(mC)
        Lbuf = AR.f32(128, 1024)
        Pbuf = [AR.bf16(128, 1024), AR.bf16(128, 1024)]
        Rsb = AR.bf16(64, 1024)
        oust = [AR.bf16(64, 1024), AR.bf16(64, 1024)]
        Oun = AR.bf16(128, 8 * 512)
        tmp1 = AR.bf16(128, 512)
        tmp2 = AR.bf16(128, 512)
        endC_3 = AR.mark()
        AR.reset(max(endC_y, endC_3))
        Mt = AR.bf16(128, NSEQ)
        Rt = AR.bf16(128, NSEQ)
        r1 = AR.block(1024)
        sum16 = AR.bf16_at(0, 1, r1, 2048)
        ss16 = AR.bf16_at(32, 1, r1, 2048)
        rm = AR.block(1024)
        m16 = AR.bf16_at(0, 1, rm, 2048)
        var16 = AR.bf16_at(32, 1, rm, 2048)
        rr = AR.block(1024)
        rstd16 = AR.bf16_at(0, 1, rr, 2048)
        ssd16 = AR.bf16_at(32, 1, rr, 2048)
        x2t = [AR.f32(128, 512), AR.f32(128, 512)]
        x2c = AR.bf16(128, 512)
        x3st = [AR.f32(128, 512), AR.f32(128, 512)]
        ytmp = x3st[0]

        if C is None:
            psA = e_(nc.psum_tensor("psA", [128, 1024], F32)).ap()
            psB = e_(nc.psum_tensor("psB", [128, 1024], F32)).ap()
            psO = e_(nc.psum_tensor("psO", [128, 1024], F32)).ap()
            pt16 = e_(nc.psum_tensor("pt16", [128, 2048], BF16)).ap()
        else:
            psA, psB, psO, pt16 = C.psA, C.psB, C.psO, C.pt16

        toks = []
        for name, sb_, c_ in (("id", id16, ident16_c), ("pm", permT, perm_c),
                              ("cq", COSQ, cosq_c), ("sq", SINQ, sinq_c),
                              ("mk", MSK, mask_c)):
            toks.append(S.add("sync", lambda e, sb_=sb_, c_=c_:
                              e.dma_start(out=sb_, in_=c_.ap()), inc="ld", inc_n=16))
        l_id = toks[0]
        tab_toks = [("_", t) for t in toks]
        S.add("vector", lambda e: e.memset(ones_col[:, 0:1], 1.0), inc="v")
        S.add("vector", lambda e: e.memset(epscol, EPS), inc="v")
        S.add("vector", lambda e: e.memset(ones_row, 1.0), inc="v")

        def set_vt_ones(e):
            ap = Vt.rearrange("p (nt h j) -> p nt h j", nt=NT, h=HEADS, j=HD + 1)
            return e.memset(ap[:, :, :, HD:HD + 1], 1.0)
        S.add("vector", set_vt_ones, inc="v")

        wstate = dict(cast_done=0, tr_done=0, ps_done=[0, 0], stg_ld=0)
        wstg = Lbuf.bitcast(BF16)[:, 0:128]

        def emit_wt(w_dram, dst_dram):
            for r in range(DC):
                L = S.add("sync", lambda e, wd=w_dram, r=r:
                          e.dma_start(out=wtmp32, in_=wd[r * 128:(r + 1) * 128, :]),
                          wait=[("v", wstate["cast_done"])], inc="ld", inc_n=16)
                cv = S.add("vector", lambda e:
                           e.tensor_copy(out=wtmp16, in_=wtmp32),
                           wait=[("_", L), ("t", wstate["tr_done"])], inc="v")
                wstate["cast_done"] = cv
                last_t = 0
                for c in range(DC):
                    slot = c % 2
                    dst_ps = pt16[:, slot * 1024: slot * 1024 + 128]
                    tt = S.add("tensor", lambda e, c=c, dst=dst_ps:
                               e.transpose(out=dst, in_=wtmp16[:, c * 128:(c + 1) * 128],
                                           identity=id16),
                               wait=[("v", cv), ("v", wstate["ps_done"][slot]),
                                     ("_", l_id)], inc="t")
                    ev = S.add("vector", lambda e, dst=dst_ps:
                               e.tensor_copy(out=wstg, in_=dst),
                               wait=[("t", tt), ("_", wstate["stg_ld"])], inc="v")
                    wstate["stg_ld"] = S.add(
                        "sync", lambda e, dd=dst_dram, c=c, r=r:
                        e.dma_start(out=dd[:, c * D + r * 128: c * D + (r + 1) * 128], in_=wstg),
                        wait=[("v", ev)], inc="ld", inc_n=16)
                    wstate["ps_done"][slot] = ev
                    last_t = tt
                wstate["tr_done"] = last_t

        for n in ("wq", "wk", "wv", "wo"):
            emit_wt(w_dr[n], wprep[n])
        for j, n in enumerate(("bq", "bk", "bo")):
            for c in range(DC):
                S.add("sync", lambda e, bd=b_dr[n], c=c, col=j * 8 + c:
                      e.dma_start(out=bcol[:, col:col + 1],
                                  in_=bd[c * 128:(c + 1) * 128, 0:1]),
                      inc="ld", inc_n=16)
        brv_tok = S.add("gpsimd", lambda e:
                        e.dma_start(out=brv16, in_=b_dr["bv"].rearrange("d one -> one d")),
                        inc="g", inc_n=16)
        bias_f = S.dma_fence("ld")
        prep_v = S.val("v")

        st = dict(
            psA_v=0, psA_a=0, psB_v=0, psB_a=0, psO_v=0,
            P_rd=[0, 0], sq_rd=[0, 0], oust_ld=[0, 0],
            x2t_v=[0, 0], x3_ld=[0, 0], x2c_rd=0,
            mt_rd=0, y2_rd=0, qk_rd=0, vt_rd=0, tmp_rd=0,
        )

        for b in range(n_batches):
            S.barrier()
            for k in ("psA_v", "psB_v", "psO_v"):
                st[k] = 0
            st["psA_a"] = st["psB_a"] = 0
            wv_ld = S.add("sync", lambda e: e.dma_start(out=wslot[0], in_=wprep["wv"]),
                          inc="ld", inc_n=16)
            wk_ld = S.add("sync", lambda e: e.dma_start(out=wslot[1], in_=wprep["wk"]),
                          inc="ld", inc_n=16)

            for nb in range(4):
                for oc in range(DC):
                    i = oc % 2
                    lx = S.add("sync", lambda e, b=b, oc=oc, nb=nb, i=i:
                               e.dma_start(out=x2t[i],
                                           in_=x2_in[b, oc * 128:(oc + 1) * 128,
                                                     nb * 512:(nb + 1) * 512]),
                               wait=[("v", st["x2t_v"][i])], inc="ld", inc_n=16)
                    cv = S.add("vector", lambda e, i=i:
                               e.tensor_copy(out=x2c, in_=x2t[i]),
                               wait=[("_", lx), ("t", st["x2c_rd"])], inc="v")
                    st["x2t_v"][i] = cv
                    mm1 = S.add("tensor", lambda e, oc=oc:
                                e.matmul(out=psO[0:1, 0:512], lhsT=ones_col[0:128, 0:1],
                                         rhs=x2c, start=(oc == 0), stop=(oc == DC - 1)),
                                wait=[("v", cv)] + ([("v", st["psO_v"])] if oc == 0 else []),
                                inc="t")
                    sv = S.add("vector", lambda e:
                               e.tensor_mul(out=sq512[0][0:128, :], in0=x2c, in1=x2c),
                               wait=[("t", st["sq_rd"][0])], inc="v")
                    mm2 = S.add("tensor", lambda e, oc=oc:
                                e.matmul(out=psO[0:1, 512:1024], lhsT=ones_col[0:128, 0:1],
                                         rhs=sq512[0][0:128, :], start=(oc == 0),
                                         stop=(oc == DC - 1)),
                                wait=[("v", sv)], inc="t")
                    st["sq_rd"][0] = mm2
                    st["x2c_rd"] = mm2
                s_ev = S.add("vector", lambda e, nb=nb:
                             e.tensor_copy(out=sum16[0:1, nb * 512:(nb + 1) * 512],
                                           in_=psO[0:1, 0:512]),
                             wait=[("t", S.val("t"))], inc="v")
                st["psO_v"] = S.add("vector", lambda e, nb=nb:
                                    e.tensor_copy(out=ss16[0:1, nb * 512:(nb + 1) * 512],
                                                  in_=psO[0:1, 512:1024]),
                                    inc="v")
            rv1 = S.add("vector", lambda e: e.tensor_scalar_mul(out=m16, in0=sum16,
                                                                scalar1=1.0 / D),
                        wait=[("v", S.val("v"))], inc="v")
            rv2 = S.add("vector", lambda e: e.tensor_mul(out=var16, in0=m16, in1=m16),
                        wait=[("v", rv1)], inc="v")
            rv3 = S.add("vector", lambda e: e.tensor_scalar_mul(out=ssd16, in0=ss16,
                                                                scalar1=1.0 / D),
                        inc="v")
            vv = S.add("vector", lambda e: e.tensor_sub(out=var16, in0=ssd16, in1=var16),
                       wait=[("v", rv3), ("v", rv2)], inc="v")
            aL = S.add("scalar", lambda e: e.activation(out=ssd16, in_=var16, func=AF.Ln,
                                                        bias=epscol[0:1, 0:1]),
                       wait=[("v", vv)], inc="a")
            ar_ = S.add("scalar", lambda e: e.activation(out=rstd16, in_=ssd16, func=AF.Exp,
                                                         scale=-0.5), wait=[("a", aL)], inc="a")
            for nb in range(4):
                par = nb % 2
                ps = (psA, psB)[par]
                pv = st["psA_v"] if par == 0 else st["psB_v"]
                pa = st["psA_a"] if par == 0 else st["psB_a"]
                m1 = S.add("tensor", lambda e, nb=nb, ps=ps:
                           e.matmul(out=ps[0:128, 0:512], lhsT=ones_row[0:1, :],
                                    rhs=m16[0:1, nb * 512:(nb + 1) * 512], start=True, stop=True),
                           wait=[("a", ar_), ("v", pv), ("a", pa), ("v", st["mt_rd"]),
                                 ("v", prep_v)], inc="t")
                m2 = S.add("tensor", lambda e, nb=nb, ps=ps:
                           e.matmul(out=ps[0:128, 512:1024], lhsT=ones_row[0:1, :],
                                    rhs=rstd16[0:1, nb * 512:(nb + 1) * 512], start=True,
                                    stop=True),
                           inc="t")
                ev1 = S.add("vector", lambda e, nb=nb, ps=ps:
                            e.tensor_copy(out=Mt[:, nb * 512:(nb + 1) * 512], in_=ps[0:128, 0:512]),
                            wait=[("t", m2)], inc="v")
                ev2 = S.add("vector", lambda e, nb=nb, ps=ps:
                            e.tensor_copy(out=Rt[:, nb * 512:(nb + 1) * 512],
                                          in_=ps[0:128, 512:1024]),
                            inc="v")
                if par == 0:
                    st["psA_v"] = ev2
                else:
                    st["psB_v"] = ev2
            bc_v = S.val("v")
            for oc in range(DC):
                for nb in range(4):
                    k = oc * 4 + nb
                    i = k % 2
                    lx = S.add("sync", lambda e, b=b, oc=oc, nb=nb, i=i:
                               e.dma_start(out=x2t[i],
                                           in_=x2_in[b, oc * 128:(oc + 1) * 128,
                                                     nb * 512:(nb + 1) * 512]),
                               wait=[("v", st["x2t_v"][i])], inc="ld", inc_n=16)
                    v1_ = S.add("vector", lambda e, i=i, nb=nb:
                                e.tensor_sub(out=ytmp, in0=x2t[i],
                                             in1=Mt[:, nb * 512:(nb + 1) * 512]),
                                wait=[("_", lx), ("v", bc_v)], inc="v")
                    st["x2t_v"][i] = v1_
                    v2_ = S.add("vector", lambda e, oc=oc, nb=nb:
                                e.tensor_mul(out=y2[:, oc * NSEQ + nb * 512:
                                                    oc * NSEQ + (nb + 1) * 512],
                                             in0=ytmp, in1=Rt[:, nb * 512:(nb + 1) * 512]),
                                wait=[("t", st["y2_rd"])], inc="v")
            st["mt_rd"] = S.val("v")
            y2_v = S.val("v")

            for nt in range(NT):
                for ob in range(2):
                    par = (nt * 2 + ob) % 2
                    ps = (psA, psB)[par]
                    pv = st["psA_v"] if par == 0 else st["psB_v"]
                    pa = st["psA_a"] if par == 0 else st["psB_a"]
                    for ic in range(DC):
                        S.add("tensor", lambda e, nt=nt, ob=ob, ic=ic, ps=ps:
                              e.matmul(out=ps[0:128, 0:512],
                                       lhsT=y2[:, ic * NSEQ + nt * 128: ic * NSEQ + (nt + 1) * 128],
                                       rhs=wslot[0][:, ic * D + ob * 512: ic * D + (ob + 1) * 512],
                                       start=(ic == 0), stop=False),
                              wait=([("v", y2_v), ("v", pv), ("a", pa), ("_", wv_ld)]
                                    if ic == 0 else []),
                              inc="t")
                    mmv = S.add("tensor", lambda e, ob=ob, ps=ps:
                                e.matmul(out=ps[0:128, 0:512], lhsT=ones_row[0:1, :],
                                         rhs=brv16[0:1, ob * 512:(ob + 1) * 512],
                                         start=False, stop=True),
                                wait=[("_", brv_tok)], inc="t")
                    ev = S.add("vector", lambda e, nt=nt, ob=ob, ps=ps:
                               e.tensor_copy(
                                   out=Vt[:, (nt * HEADS + ob * 8) * 65:
                                          (nt * HEADS + ob * 8 + 8) * 65]
                                   .rearrange("p (h j) -> p h j", j=65)[:, :, 0:64],
                                   in_=ps[0:128, 0:512].rearrange("p (h j) -> p h j", j=64)),
                               wait=[("t", mmv), ("t", st["vt_rd"])], inc="v")
                    if par == 0:
                        st["psA_v"] = ev
                    else:
                        st["psB_v"] = ev
            vproj_t = S.val("t")
            wq_ld = S.add("sync", lambda e: e.dma_start(out=wslot[0], in_=wprep["wq"]),
                          wait=[("t", vproj_t)], inc="ld", inc_n=16)

            for p in range(8):
                for (widx, wld, dstb, bcol0, CT, ST, kscale) in (
                        (0, wq_ld, Qb, 0, COSQ, SINQ, None), (1, wk_ld, Kb, 8, COSQ, SINQ, 0.125)):
                    for nb in range(4):
                        par = nb % 2
                        ps = (psA, psB)[par]
                        pv = st["psA_v"] if par == 0 else st["psB_v"]
                        pa = st["psA_a"] if par == 0 else st["psB_a"]
                        for ic in range(DC):
                            S.add("tensor", lambda e, p=p, nb=nb, ic=ic, ps=ps, widx=widx:
                                  e.matmul(out=ps[0:128, 0:512],
                                           lhsT=wslot[widx][:, ic * D + p * 128: ic * D + (p + 1) * 128],
                                           rhs=y2[:, ic * NSEQ + nb * 512: ic * NSEQ + (nb + 1) * 512],
                                           start=(ic == 0), stop=(ic == DC - 1)),
                                  wait=([("v", y2_v), ("v", pv), ("a", pa), ("_", wld)]
                                        if ic == 0 else []),
                                  inc="t")
                        mm = S.val("t")
                        if kscale is None:
                            ev = S.add("vector", lambda e, p=p, nb=nb, ps=ps, dstb=dstb, col=bcol0 + p:
                                       e.tensor_scalar_add(
                                           out=dstb[:, p * NSEQ + nb * 512: p * NSEQ + (nb + 1) * 512],
                                           in0=ps[0:128, 0:512], scalar1=bcol[:, col:col + 1]),
                                       wait=[("t", mm), ("t", st["qk_rd"])], inc="v")
                        else:
                            ev = S.add("vector", lambda e, p=p, nb=nb, ps=ps, dstb=dstb, col=bcol0 + p, ks=kscale:
                                       e.tensor_scalar(
                                           out=dstb[:, p * NSEQ + nb * 512: p * NSEQ + (nb + 1) * 512],
                                           in0=ps[0:128, 0:512], scalar1=bcol[:, col:col + 1],
                                           scalar2=ks, op0=ALU.add, op1=ALU.mult),
                                       wait=[("t", mm), ("t", st["qk_rd"])], inc="v")
                        if par == 0:
                            st["psA_v"] = ev
                        else:
                            st["psB_v"] = ev
                    for nb in range(4):
                        par = nb % 2
                        ps = (psA, psB)[par]
                        pv = st["psA_v"] if par == 0 else st["psB_v"]
                        pa = st["psA_a"] if par == 0 else st["psB_a"]
                        sl = slice(p * NSEQ + nb * 512, p * NSEQ + (nb + 1) * 512)
                        tsl = slice(nb * 512, (nb + 1) * 512)
                        pm = S.add("tensor", lambda e, ps=ps, dstb=dstb, sl=sl:
                                   e.matmul(out=ps[0:128, 0:512], lhsT=permT,
                                            rhs=dstb[:, sl], start=True, stop=True),
                                   wait=[("v", pv), ("a", pa), ("_", l_id)] + tab_toks[:2],
                                   inc="t")
                        t1 = S.add("vector", lambda e, dstb=dstb, sl=sl, tsl=tsl, CT=CT:
                                   e.tensor_mul(out=tmp1, in0=dstb[:, sl], in1=CT[:, tsl]),
                                   wait=[("v", st["tmp_rd"])] + tab_toks, inc="v")
                        t2 = S.add("vector", lambda e, ps=ps, tsl=tsl, ST=ST:
                                   e.tensor_mul(out=tmp2, in0=ps[0:128, 0:512], in1=ST[:, tsl]),
                                   wait=[("t", pm)], inc="v")
                        t3 = S.add("vector", lambda e, dstb=dstb, sl=sl:
                                   e.tensor_add(out=dstb[:, sl], in0=tmp1, in1=tmp2),
                                   inc="v")
                        st["tmp_rd"] = t3
                        if par == 0:
                            st["psA_v"] = t3
                        else:
                            st["psB_v"] = t3
            qk_v = S.val("v")
            wo_ld = S.add("sync", lambda e: e.dma_start(out=wslot[1], in_=wprep["wo"]),
                          wait=[("t", S.val("t"))], inc="ld", inc_n=16)

            for h in range(HEADS):
                p, hh = h // 2, h % 2
                base = hh * 64
                for qh in range(2):
                    kts = list(range(0, qh * 8 + 8))
                    pvmm = 0
                    for kt in kts:
                        diag = kt >= qh * 8
                        par = kt % 2
                        ps = (psA, psB)[par]
                        pv = st["psA_v"] if par == 0 else st["psB_v"]
                        pa = st["psA_a"] if par == 0 else st["psB_a"]
                        last_mm = 0
                        for half in range(2):
                            last_mm = S.add(
                                "tensor", lambda e, p=p, base=base, kt=kt, qh=qh, half=half, ps=ps:
                                e.matmul(out=ps[0:128, half * 512:(half + 1) * 512],
                                         lhsT=Kb[base:base + 64,
                                                 p * NSEQ + kt * 128: p * NSEQ + (kt + 1) * 128],
                                         rhs=Qb[base:base + 64,
                                                p * NSEQ + qh * 1024 + half * 512:
                                                p * NSEQ + qh * 1024 + (half + 1) * 512],
                                         start=True, stop=True),
                                wait=([("v", pv), ("a", pa), ("v", qk_v)] if half == 0 else []),
                                inc="t")
                        pb = kt % 2
                        if diag:
                            r = kt * 128 - qh * 1024
                            mv = S.add("vector", lambda e, ps=ps, r=r:
                                       e.tensor_add(out=Lbuf, in0=ps[0:128, 0:1024],
                                                    in1=MSK[:, 896 - r: 896 - r + 1024]),
                                       wait=[("t", last_mm), ("a", st.get("lbuf_a", 0))],
                                       inc="v")
                            if par == 0:
                                st["psA_v"] = mv
                            else:
                                st["psB_v"] = mv
                            aexp = S.add("scalar", lambda e, pb=pb:
                                         e.activation(out=Pbuf[pb], in_=Lbuf, func=AF.Exp),
                                         wait=[("v", mv), ("t", st["P_rd"][pb])], inc="a")
                            st["lbuf_a"] = aexp
                        else:
                            aexp = S.add("scalar", lambda e, ps=ps, pb=pb:
                                         e.activation(out=Pbuf[pb], in_=ps[0:128, 0:1024],
                                                      func=AF.Exp),
                                         wait=[("t", last_mm), ("t", st["P_rd"][pb])], inc="a")
                            if par == 0:
                                st["psA_a"] = aexp
                            else:
                                st["psB_a"] = aexp
                        for half in range(2):
                            pvmm = S.add("tensor",
                                         lambda e, kt=kt, h=h, half=half, pb=pb,
                                         k0=kts[0], k1=kts[-1]:
                                         e.matmul(out=psO[0:65, half * 512:(half + 1) * 512],
                                                  lhsT=Vt[:, (kt * HEADS + h) * 65:
                                                          (kt * HEADS + h + 1) * 65],
                                                  rhs=Pbuf[pb][:, half * 512:(half + 1) * 512],
                                                  start=(kt == k0), stop=(kt == k1)),
                                         wait=([("a", aexp)] if half == 0 else []) +
                                              ([("v", st["psO_v"])]
                                               if (kt == kts[0] and half == 0) else []),
                                         inc="t")
                        st["P_rd"][pb] = pvmm
                        st["vt_rd"] = pvmm
                    def recip_fn(e):
                        with nc.allow_low_precision(reason="softmax denom recip bf16"):
                            return e.reciprocal(out=rrow, in_=psO[64:65, 0:1024])
                    vr = S.add("vector", recip_fn, wait=[("t", pvmm)], inc="v")
                    rmm = S.add("tensor", lambda e:
                                e.matmul(out=psA[0:64, 0:512], lhsT=ones_row[0:1, 0:64],
                                         rhs=rrow[0:1, 0:512], start=True, stop=True),
                                wait=[("v", vr), ("a", st["psA_a"]), ("v", st["psA_v"])],
                                inc="t")
                    rmm2 = S.add("tensor", lambda e:
                                 e.matmul(out=psA[0:64, 512:1024], lhsT=ones_row[0:1, 0:64],
                                          rhs=rrow[0:1, 512:1024], start=True, stop=True),
                                 inc="t")
                    vR = S.add("vector", lambda e:
                               e.tensor_copy(out=Rsb, in_=psA[0:64, 0:1024]),
                               wait=[("t", rmm2)], inc="v")
                    st["psA_v"] = vR
                    oi = qh
                    ouev = S.add("vector", lambda e, oi=oi:
                                 e.tensor_mul(out=oust[oi], in0=psO[0:64, 0:1024], in1=Rsb),
                                 wait=[("v", vR), ("_", st["oust_ld"][oi])], inc="v")
                    st["psO_v"] = ouev
                    st["oust_ld"][oi] = S.add(
                        "sync", lambda e, h=h, qh=qh, oi=oi:
                        e.dma_start(out=ou_scr[h * 64:(h + 1) * 64, qh * 1024:(qh + 1) * 1024],
                                    in_=oust[oi]),
                        wait=[("v", ouev)], inc="ld", inc_n=16)
            st["qk_rd"] = S.val("t")

            ou_f = S.dma_fence("ld")
            for nb in range(4):
                lo = S.add("sync", lambda e, nb=nb:
                           e.dma_start(out=Oun.rearrange("p (pr j) -> p pr j", pr=8),
                                       in_=ou_scr.rearrange("(pr p) (nb j) -> p pr nb j",
                                                            p=128, nb=4)[:, :, nb, :]),
                           wait=ou_f + [("t", S.val("t"))], inc="ld", inc_n=16)
                for g2 in range(2):
                    for pr in range(8):
                        for oo in range(4):
                            oc = g2 * 4 + oo
                            ps = (psA, psB)[oo // 2]
                            half = oo % 2
                            S.add("tensor",
                                  lambda e, pr=pr, oc=oc, ps=ps, half=half:
                                  e.matmul(out=ps[0:128, half * 512:(half + 1) * 512],
                                           lhsT=wslot[1][:, pr * D + oc * 128: pr * D + (oc + 1) * 128],
                                           rhs=Oun[:, pr * 512:(pr + 1) * 512],
                                           start=(pr == 0), stop=(pr == 7)),
                                  wait=([("_", lo), ("_", wo_ld),
                                         ("v", st["psA_v"] if oo // 2 == 0 else st["psB_v"]),
                                         ("a", st["psA_a"] if oo // 2 == 0 else st["psB_a"])]
                                        if pr == 0 and half == 0 else []),
                                  inc="t")
                    mm_g = S.val("t")
                    for oo in range(4):
                        oc = g2 * 4 + oo
                        ps = (psA, psB)[oo // 2]
                        half = oo % 2
                        i = oc % 2
                        lx = S.add("sync", lambda e, b=b, oc=oc, nb=nb, i=i:
                                   e.dma_start(out=x2t[i],
                                               in_=x2_in[b, oc * 128:(oc + 1) * 128,
                                                         nb * 512:(nb + 1) * 512]),
                                   wait=[("v", st["x2t_v"][i])], inc="ld", inc_n=16)
                        ev = S.add("vector", lambda e, oc=oc, i=i, ps=ps, half=half:
                                   e.scalar_tensor_tensor(out=x3st[i],
                                                          in0=ps[0:128, half * 512:(half + 1) * 512],
                                                          scalar=bcol[:, 16 + oc:17 + oc],
                                                          in1=x2t[i],
                                                          op0=ALU.add, op1=ALU.add),
                                   wait=[("t", mm_g), ("_", lx), ("_", st["x3_ld"][i])],
                                   inc="v")
                        st["x2t_v"][i] = ev
                        if oo // 2 == 0:
                            st["psA_v"] = ev
                        else:
                            st["psB_v"] = ev
                        st["x3_ld"][i] = S.add(
                            "sync", lambda e, b=b, oc=oc, nb=nb, i=i:
                            e.dma_start(out=x3_out[b, oc * 128:(oc + 1) * 128,
                                                   nb * 512:(nb + 1) * 512],
                                        in_=x3st[i]),
                            wait=[("v", S.val("v"))], inc="ld", inc_n=16)
            st["y2_rd"] = S.val("t")

        if C is None:
            S.add("sync", None, wait=S.dma_fence("ld") + S.dma_fence("g"))
            S.emit(nc, stack)
    return nc


P6_HEADS = 8
P6_INNER = 512
P6_HID = 2730
P6_W = 512
HJ = (P6_HID + 127) // 128


def _xpos_tables():
    hd = HD
    half_idx = np.arange(0, hd, 2, dtype=np.float64)
    inv = 1.0 / 10000.0 ** (half_idx / hd)
    t = np.arange(NSEQ, dtype=np.float64)
    fr = np.concatenate([inv, inv])
    ang = fr[:, None] * t[None, :]
    base = (half_idx + 0.4 * hd) / (1.4 * hd)
    base2 = np.concatenate([base, base])
    power = (t - NSEQ // 2) / float(P6_W // 2)
    scale = base2[:, None] ** power[None, :]
    cos, sin = np.cos(ang), np.sin(ang)
    return [np.vstack([a, a]).astype(BF) for a in
            (cos * scale * 8.0, sin * scale * 8.0, cos / scale, sin / scale)]


def build_p6(n_batches=NB, n_layers=2, attn_only=False, C=None):
    if C is None:
        nc = bass.Bass("TRN2", target_bir_lowering=False, detect_race_conditions=False)
        S = Sched(SEMS)
        x3_in = nc.dram_tensor("x3_fm", [n_batches, D, NSEQ], F32, kind="ExternalInput").ap()
        wqkv_dr = nc.dram_tensor("wqkv", [2, D, 3 * P6_INNER], F32, kind="ExternalInput").ap()
        wol_dr = nc.dram_tensor("wo_l", [2, P6_INNER, D], F32, kind="ExternalInput").ap()
        w1_dr = nc.dram_tensor("w1", [2, D, 2 * P6_HID], F32, kind="ExternalInput").ap()
        w2_dr = nc.dram_tensor("w2", [2, P6_HID, D], F32, kind="ExternalInput").ap()
        qs_dr = nc.dram_tensor("q_scale", [2, HD], F32, kind="ExternalInput").ap()
        ks_dr = nc.dram_tensor("k_scale", [2, HD], F32, kind="ExternalInput").ap()
        lng_dr = nc.dram_tensor("ln_g", [2, D], F32, kind="ExternalInput").ap()
        lnb_dr = nc.dram_tensor("ln_b", [2, D], F32, kind="ExternalInput").ap()
        b1_dr = nc.dram_tensor("b1", [2, 2 * P6_HID], F32, kind="ExternalInput").ap()
        b2_dr = nc.dram_tensor("b2", [2, D], F32, kind="ExternalInput").ap()
        x_out = nc.dram_tensor("xout", [n_batches, NSEQ, D], F32, kind="ExternalOutput").ap()
    else:
        nc, S = C.nc, C.S
        x3_in = C.io["x3_fm"]
        wqkv_dr = C.io["l1f_wqkv"]
        wol_dr = C.io["l1f_wo"]
        w1_dr = C.io["l1f_w1"]
        w2_dr = C.io["l1f_w2"]
        qs_dr = C.io["l1f_qs"]
        ks_dr = C.io["l1f_ks"]
        lng_dr = C.io["l1f_lng"]
        lnb_dr = C.io["l1f_lnb"]
        b1_dr = C.io["l1f_b1"]
        b2_dr = C.io["l1f_b2"]
        x_out = C.io["xout"]

    xi32 = nc.dram_tensor("xi32", [n_batches, D, NSEQ], F32, kind="Internal").ap()
    xi16 = nc.dram_tensor("xi16", [n_batches, D, NSEQ], BF16, kind="Internal").ap()
    ou_scr = nc.dram_tensor("p6_ou", [P6_INNER, NSEQ], BF16, kind="Internal").ap()
    nr_scr = nc.dram_tensor("nr_scr", [16, NSEQ], BF16, kind="Internal").ap()
    wq16 = nc.dram_tensor("wq16", [2, D, 3 * P6_INNER], BF16, kind="Internal").ap()
    wol16 = nc.dram_tensor("wol16", [2, P6_INNER, D], BF16, kind="Internal").ap()
    w116 = nc.dram_tensor("w116", [2, D, 2 * P6_HID], BF16, kind="Internal").ap()
    w216 = nc.dram_tensor("w216", [2, P6_HID, D], BF16, kind="Internal").ap()

    ident16_c = _const(nc, C, "ident16", np.eye(128, dtype=BF))
    ident32_c = _const(nc, C, "ident32", np.eye(128, dtype=np.float32))
    CSq_h, SSq_h, CSk_h, SSk_h = _xpos_tables()
    csq_c = _const(nc, C, "csq", CSq_h)
    ssq_c = _const(nc, C, "ssq", SSq_h)
    csk_c = _const(nc, C, "csk", CSk_h)
    ssk_c = _const(nc, C, "ssk", SSk_h)
    perm_c = _const(nc, C, "permT", _perm_block().astype(BF))
    u = np.arange(1408) - 384
    cm = np.where((u[None, :] >= np.arange(128)[:, None]) &
                  (u[None, :] <= np.arange(128)[:, None] + P6_W), 0.0, -1e30)
    cmask_c = _const(nc, C, "cmask6", cm.astype(BF))
    hind = np.zeros((128, 8 * 16), np.float64)
    for cc in range(8):
        hind[0:64, cc * 16 + 2 * cc] = 1.0
        hind[64:128, cc * 16 + 2 * cc + 1] = 1.0
    hind_c = _const(nc, C, "hind", hind.astype(BF))

    with ExitStack() as stack:
        if C is None:
            AR = Arena(stack.enter_context(
                nc.sbuf_tensor("AR", [128, 53160], F32)).ap(), 53160)
        else:
            AR = C.AR
            AR.reset(0)
        e_ = stack.enter_context

        id16 = AR.bf16(128, 128)
        id32 = AR.f32(128, 128)
        permT = AR.bf16(128, 128)
        CSQ = AR.bf16(128, NSEQ)
        SSQ = AR.bf16(128, NSEQ)
        CSK = AR.bf16(128, NSEQ)
        SSK = AR.bf16(128, NSEQ)
        CMSK = AR.bf16(128, 1408)
        HIND = AR.bf16(128, 8 * 16)
        qscol = AR.f32(128, 1)
        kscol = AR.f32(128, 1)
        lgcol = AR.f32(128, 8)
        lbcol = AR.f32(128, 8)
        b2col = AR.f32(128, 8)
        b1acol = AR.f32(128, HJ)
        b1gcol = AR.f32(128, HJ)
        epscol = AR.f32(128, 1)
        ones_col = AR.bf16(128, 2)
        o_or = AR.block(64)
        ones_row = AR.bf16_at(0, 1, o_or, 128)
        rrow = AR.bf16(1, 512)
        nrA = AR.bf16(1, NSEQ)
        nrB = AR.bf16(1, NSEQ)
        r1 = AR.block(1024)
        sum16 = AR.bf16_at(0, 1, r1, 2048)
        ss16 = AR.bf16_at(32, 1, r1, 2048)
        rm_ = AR.block(1024)
        m16 = AR.bf16_at(0, 1, rm_, 2048)
        var16 = AR.bf16_at(32, 1, rm_, 2048)
        rr_ = AR.block(1024)
        rstd16 = AR.bf16_at(0, 1, rr_, 2048)
        ssd16 = AR.bf16_at(32, 1, rr_, 2048)
        Mt = AR.bf16(128, NSEQ)
        Rt = AR.bf16(128, NSEQ)
        NR = AR.bf16(16, 1024)

        mR1 = AR.mark()
        Qf = AR.bf16(128, 4 * NSEQ)
        Kf = AR.bf16(128, 4 * NSEQ)
        e1 = AR.mark()
        AR.reset(mR1)
        LNH = AR.bf16(128, DC * NSEQ)
        e2 = AR.mark()
        AR.reset(max(e1, e2))

        mR2 = AR.mark()
        Vt = AR.bf16(128, NT * P6_HEADS * (HD + 1))
        Lb6 = AR.f32(128, 512)
        Pb6 = [AR.bf16(128, 512), AR.bf16(128, 512)]
        Rsb = AR.bf16(64, 512)
        oust = [AR.bf16(64, 512), AR.bf16(64, 512)]
        tmp1 = AR.bf16(128, 512)
        tmp2 = AR.bf16(128, 512)
        e1 = AR.mark()
        AR.reset(mR2)
        hg = AR.bf16(128, HJ * 1024)
        ffG = AR.bf16(128, 512)
        ffA = AR.bf16(128, 512)
        e2 = AR.mark()
        AR.reset(max(e1, e2))

        mR3 = AR.mark()
        wqkv_sb = AR.bf16(128, DC * 3 * P6_INNER)
        e1 = AR.mark()
        AR.reset(mR3)
        w1a_sb = [AR.bf16(128, DC * 128), AR.bf16(128, DC * 128)]
        w1g_sb = [AR.bf16(128, DC * 128), AR.bf16(128, DC * 128)]
        w2_sb = [AR.bf16(128, HJ * 128), AR.bf16(128, HJ * 128)]
        e2 = AR.mark()
        AR.reset(max(e1, e2))

        wo_sb = AR.bf16(128, 4 * D)
        Oun = AR.bf16(128, 4 * 512)
        sq512 = AR.bf16(128, 512)
        x2t = [AR.f32(128, 512), AR.f32(128, 512)]
        x2c = AR.bf16(128, 512)
        xot = [AR.f32(128, 512), AR.f32(128, 512)]
        xo16 = [AR.bf16(128, 512), AR.bf16(128, 512)]
        vls = [AR.bf16(128, 128), AR.bf16(128, 128)]
        ytmp = AR.f32(128, 512)
        ytmp2 = AR.f32(128, 512)
        xstage = [AR.f32(128, D), AR.f32(128, D)]
        wtmp32 = AR.f32(128, 1024)
        wtmp16 = AR.bf16(128, 1024)

        if C is None:
            psA = e_(nc.psum_tensor("psA", [128, 1024], F32)).ap()
            psB = e_(nc.psum_tensor("psB", [128, 1024], F32)).ap()
            psO = e_(nc.psum_tensor("psO", [128, 1024], F32)).ap()
            pt16 = e_(nc.psum_tensor("pt16", [128, 2048], BF16)).ap()
        else:
            psA, psB, psO, pt16 = C.psA, C.psB, C.psO, C.pt16

        toks = []
        for sb_, c_ in ((id16, ident16_c), (id32, ident32_c), (permT, perm_c),
                        (CSQ, csq_c), (SSQ, ssq_c), (CSK, csk_c), (SSK, ssk_c),
                        (CMSK, cmask_c), (HIND, hind_c)):
            toks.append(S.add("sync", lambda e, sb_=sb_, c_=c_:
                              e.dma_start(out=sb_, in_=c_.ap()), inc="ld", inc_n=16))
        tab_f = [("_", t) for t in toks]
        S.add("vector", lambda e: e.memset(ones_col[:, 0:1], 1.0), inc="v")
        S.add("vector", lambda e: e.memset(epscol, EPS), inc="v")
        S.add("vector", lambda e: e.memset(ones_row, 1.0), inc="v")

        def set_vt_ones(e):
            ap = Vt.rearrange("p (nt h j) -> p nt h j", nt=NT, h=P6_HEADS, j=HD + 1)
            return e.memset(ap[:, :, :, HD:HD + 1], 1.0)
        vto = S.add("vector", set_vt_ones, inc="v")

        wstate = dict(cast_done=0, st_ld=0)

        def emit_cast(src, dst, rows, cols):
            nt_ = (rows + 127) // 128
            cw = 1024
            ncw = (cols + cw - 1) // cw
            for r in range(nt_):
                pr = min(128, rows - r * 128)
                for cb in range(ncw):
                    w = min(cw, cols - cb * cw)
                    L = S.add("sync", lambda e, r=r, cb=cb, pr=pr, w=w, src=src:
                              e.dma_start(out=wtmp32[0:pr, 0:w],
                                          in_=src[r * 128:r * 128 + pr, cb * cw:cb * cw + w]),
                              wait=[("v", wstate["cast_done"])], inc="ld", inc_n=16)
                    cv = S.add("vector", lambda e, pr=pr, w=w:
                               e.tensor_copy(out=wtmp16[0:pr, 0:w], in_=wtmp32[0:pr, 0:w]),
                               wait=[("_", L), ("_", wstate["st_ld"])], inc="v")
                    wstate["cast_done"] = cv
                    wstate["st_ld"] = S.add(
                        "sync", lambda e, r=r, cb=cb, pr=pr, w=w, dst=dst:
                        e.dma_start(out=dst[r * 128:r * 128 + pr, cb * cw:cb * cw + w],
                                    in_=wtmp16[0:pr, 0:w]),
                        wait=[("v", cv)], inc="ld", inc_n=16)

        for l in range(2):
            emit_cast(wqkv_dr[l], wq16[l], D, 3 * P6_INNER)
            emit_cast(wol_dr[l], wol16[l], P6_INNER, D)
            emit_cast(w1_dr[l], w116[l], D, 2 * P6_HID)
            emit_cast(w2_dr[l], w216[l], P6_HID, D)
        prep_f = S.dma_fence("ld")

        st = dict(
            psA_v=0, psA_a=0, psB_v=0, psB_a=0, psO_v=0, psO_a=0,
            P_rd=[0, 0], sq_rd=0, oust_ld=[0, 0],
            x2t_v=[0, 0], x2c_rd=0, xot_ld=[0, 0], xo16_ld=[0, 0], xo16_rd=[0, 0],
            mt_rd=0, lnh_rd=0, qf_rd=0, vt_rd=0, tmp1_rd=0, tmp2_rd=0, hg_rd=0,
            w1_ld=[0, 0], w1_rd=[0, 0], w2_ld=[0, 0], w2_rd=[0, 0],
            nr_rd=0, nrow_rd=0, rrow_rd=0,
            lb6_a=0, oun_rd=0, vls_rd=[0, 0],
            xstage_ld=[0, 0], ytmp2_rd=0,
        )

        def ln_rows_and_bcast(shifted_src=None):
            rv1 = S.add("vector", lambda e: e.tensor_scalar_mul(out=m16, in0=sum16,
                                                                scalar1=1.0 / D),
                        wait=[("v", S.val("v"))], inc="v")
            rv2 = S.add("vector", lambda e: e.tensor_mul(out=var16, in0=m16, in1=m16),
                        wait=[("v", rv1)], inc="v")
            rv3 = S.add("vector", lambda e: e.tensor_scalar_mul(out=ssd16, in0=ss16,
                                                                scalar1=1.0 / D),
                        inc="v")
            vv = S.add("vector", lambda e: e.tensor_sub(out=var16, in0=ssd16, in1=var16),
                       wait=[("v", rv3), ("v", rv2)], inc="v")
            aL = S.add("scalar", lambda e: e.activation(out=ssd16, in_=var16, func=AF.Ln,
                                                        bias=epscol[0:1, 0:1]),
                       wait=[("v", vv)], inc="a")
            ar_ = S.add("scalar", lambda e: e.activation(out=rstd16, in_=ssd16, func=AF.Exp,
                                                         scale=-0.5), wait=[("a", aL)], inc="a")
            for nb in range(4):
                par = nb % 2
                ps = (psA, psB)[par]
                pv = st["psA_v"] if par == 0 else st["psB_v"]
                pa = st["psA_a"] if par == 0 else st["psB_a"]
                S.add("tensor", lambda e, nb=nb, ps=ps:
                      e.matmul(out=ps[0:128, 0:512], lhsT=ones_row[0:1, :],
                               rhs=m16[0:1, nb * 512:(nb + 1) * 512], start=True, stop=True),
                      wait=[("a", ar_), ("v", pv), ("a", pa), ("v", st["mt_rd"])], inc="t")
                m2 = S.add("tensor", lambda e, nb=nb, ps=ps:
                           e.matmul(out=ps[0:128, 512:1024], lhsT=ones_row[0:1, :],
                                    rhs=rstd16[0:1, nb * 512:(nb + 1) * 512], start=True,
                                    stop=True),
                           inc="t")
                S.add("vector", lambda e, nb=nb, ps=ps:
                      e.tensor_copy(out=Mt[:, nb * 512:(nb + 1) * 512], in_=ps[0:128, 0:512]),
                      wait=[("t", m2)], inc="v")
                ev2 = S.add("vector", lambda e, nb=nb, ps=ps:
                            e.tensor_copy(out=Rt[:, nb * 512:(nb + 1) * 512],
                                          in_=ps[0:128, 512:1024]),
                            inc="v")
                if par == 0:
                    st["psA_v"] = ev2
                else:
                    st["psB_v"] = ev2
            return S.val("v")

        def fm_ln_stats(load_tile):
            for nb in range(4):
                for oc in range(DC):
                    cw = load_tile(oc, nb)
                    mm1 = S.add("tensor", lambda e, oc=oc:
                                e.matmul(out=psO[0:1, 0:512], lhsT=ones_col[0:128, 0:1],
                                         rhs=x2c, start=(oc == 0), stop=(oc == DC - 1)),
                                wait=cw + ([("v", st["psO_v"]), ("a", st["psO_a"])]
                                           if oc == 0 else []),
                                inc="t")
                    sv = S.add("vector", lambda e:
                               e.tensor_mul(out=sq512, in0=x2c, in1=x2c),
                               wait=cw + [("t", st["sq_rd"])], inc="v")
                    mm2 = S.add("tensor", lambda e, oc=oc:
                                e.matmul(out=psO[0:1, 512:1024], lhsT=ones_col[0:128, 0:1],
                                         rhs=sq512, start=(oc == 0), stop=(oc == DC - 1)),
                                wait=[("v", sv)], inc="t")
                    st["sq_rd"] = mm2
                    st["x2c_rd"] = mm2
                S.add("vector", lambda e, nb=nb:
                      e.tensor_copy(out=sum16[0:1, nb * 512:(nb + 1) * 512], in_=psO[0:1, 0:512]),
                      wait=[("t", S.val("t"))], inc="v")
                st["psO_v"] = S.add("vector", lambda e, nb=nb:
                                    e.tensor_copy(out=ss16[0:1, nb * 512:(nb + 1) * 512],
                                                  in_=psO[0:1, 512:1024]),
                                    inc="v")

        def load_shift16(dstt, src16, oc, nb, width=512, war=()):
            shift = oc >= 4
            if shift and nb == 0:
                ms = S.add("vector", lambda e, dstt=dstt: e.memset(dstt[:, 0:1], 0.0),
                           wait=list(war), inc="v")
                L = S.add("gpsimd", lambda e, oc=oc, src16=src16, dstt=dstt, width=width:
                          e.dma_start(out=dstt[:, 1:width],
                                      in_=src16[oc * 128:(oc + 1) * 128, 0:width - 1]),
                          wait=list(war), inc="g", inc_n=16)
                return [("_", L), ("v", ms)]
            off = nb * width - (1 if shift else 0)
            L = S.add("gpsimd", lambda e, oc=oc, off=off, src16=src16, dstt=dstt, width=width:
                      e.dma_start(out=dstt[:, 0:width],
                                  in_=src16[oc * 128:(oc + 1) * 128, off:off + width]),
                      wait=list(war), inc="g", inc_n=16)
            return [("_", L)]

        first = True
        for b in range(n_batches):
            S.barrier()
            for k in ("psA_v", "psB_v", "psO_v"):
                st[k] = 0
            st["psA_a"] = st["psB_a"] = st["psO_a"] = 0
            if first:
                S.add("sync", None, wait=prep_f + tab_f)
                first = False

            def entry_tile(oc, nb):
                i = oc % 2
                lx = S.add("sync", lambda e, b=b, oc=oc, nb=nb, i=i:
                           e.dma_start(out=x2t[i],
                                       in_=x3_in[b, oc * 128:(oc + 1) * 128,
                                                 nb * 512:(nb + 1) * 512]),
                           wait=[("v", st["x2t_v"][i])], inc="ld", inc_n=16)
                cv = S.add("vector", lambda e, i=i:
                           e.tensor_copy(out=x2c, in_=x2t[i]),
                           wait=[("_", lx), ("t", st["x2c_rd"])], inc="v")
                st["x2t_v"][i] = cv
                return [("v", cv)]

            fm_ln_stats(entry_tile)
            bc_v = ln_rows_and_bcast()
            for oc in range(DC):
                for nb in range(4):
                    k = oc * 4 + nb
                    i = k % 2
                    lx = S.add("sync", lambda e, b=b, oc=oc, nb=nb, i=i:
                               e.dma_start(out=x2t[i],
                                           in_=x3_in[b, oc * 128:(oc + 1) * 128,
                                                     nb * 512:(nb + 1) * 512]),
                               wait=[("v", st["x2t_v"][i])], inc="ld", inc_n=16)
                    v1_ = S.add("vector", lambda e, i=i, nb=nb:
                                e.tensor_sub(out=ytmp, in0=x2t[i],
                                             in1=Mt[:, nb * 512:(nb + 1) * 512]),
                                wait=[("_", lx), ("v", bc_v)], inc="v")
                    st["x2t_v"][i] = v1_
                    v2_ = S.add("vector", lambda e, i=i, nb=nb:
                                e.tensor_mul(out=xot[i], in0=ytmp,
                                             in1=Rt[:, nb * 512:(nb + 1) * 512]),
                                wait=[("_", st["xot_ld"][i])], inc="v")
                    v3_ = S.add("vector", lambda e, i=i:
                                e.tensor_copy(out=xo16[i], in_=xot[i]),
                                wait=[("_", st["xo16_ld"][i])], inc="v")
                    st["xot_ld"][i] = S.add("sync", lambda e, b=b, oc=oc, nb=nb, i=i:
                                            e.dma_start(out=xi32[b, oc * 128:(oc + 1) * 128,
                                                                 nb * 512:(nb + 1) * 512],
                                                        in_=xot[i]),
                                            wait=[("v", v2_)], inc="ld", inc_n=16)
                    st["xo16_ld"][i] = S.add("sync", lambda e, b=b, oc=oc, nb=nb, i=i:
                                             e.dma_start(out=xi16[b, oc * 128:(oc + 1) * 128,
                                                                  nb * 512:(nb + 1) * 512],
                                                         in_=xo16[i]),
                                             wait=[("v", v3_)], inc="ld", inc_n=16)
            st["mt_rd"] = S.val("v")

        for l in range(n_layers):
            S.barrier()
            lt = []
            lt.append(S.add("gpsimd", lambda e, l=l:
                            e.dma_start(out=qscol[0:64, 0:1],
                                        in_=qs_dr[l].rearrange("(d one) -> d one", one=1)),
                            inc="g", inc_n=16))
            lt.append(S.add("gpsimd", lambda e, l=l:
                            e.dma_start(out=qscol[64:128, 0:1],
                                        in_=qs_dr[l].rearrange("(d one) -> d one", one=1)),
                            inc="g", inc_n=16))
            lt.append(S.add("gpsimd", lambda e, l=l:
                            e.dma_start(out=kscol[0:64, 0:1],
                                        in_=ks_dr[l].rearrange("(d one) -> d one", one=1)),
                            inc="g", inc_n=16))
            lt.append(S.add("gpsimd", lambda e, l=l:
                            e.dma_start(out=kscol[64:128, 0:1],
                                        in_=ks_dr[l].rearrange("(d one) -> d one", one=1)),
                            inc="g", inc_n=16))
            for c in range(DC):
                for (col, dr) in ((lgcol, lng_dr), (lbcol, lnb_dr), (b2col, b2_dr)):
                    lt.append(S.add("sync", lambda e, l=l, c=c, col=col, dr=dr:
                                    e.dma_start(out=col[:, c:c + 1],
                                                in_=dr[l, c * 128:(c + 1) * 128]
                                                .rearrange("(d one) -> d one", one=1)),
                                    inc="ld", inc_n=16))
            for j in range(HJ):
                pr = min(128, P6_HID - j * 128)
                lt.append(S.add("sync", lambda e, l=l, j=j, pr=pr:
                                e.dma_start(out=b1acol[0:pr, j:j + 1],
                                            in_=b1_dr[l, j * 128: j * 128 + pr]
                                            .rearrange("(d one) -> d one", one=1)),
                                inc="ld", inc_n=16))
                lt.append(S.add("sync", lambda e, l=l, j=j, pr=pr:
                                e.dma_start(out=b1gcol[0:pr, j:j + 1],
                                            in_=b1_dr[l, P6_HID + j * 128: P6_HID + j * 128 + pr]
                                            .rearrange("(d one) -> d one", one=1)),
                                inc="ld", inc_n=16))
            lay_f = [("_", t) for t in lt]

            for b in range(n_batches):
                S.barrier()
                for k in ("psA_v", "psB_v", "psO_v"):
                    st[k] = 0
                st["psA_a"] = st["psB_a"] = st["psO_a"] = 0
                S.add("tensor", None, wait=lay_f)
                wqkv_ld = S.add("sync", lambda e, l=l:
                                e.dma_start(out=wqkv_sb.rearrange("p (c n) -> p c n", c=DC),
                                            in_=wq16[l].rearrange("(c p) n -> p c n", p=128)),
                                inc="ld", inc_n=16)
                wol_ld = S.add("sync", lambda e, l=l:
                               e.dma_start(out=wo_sb.rearrange("p (c n) -> p c n", c=4),
                                           in_=wol16[l].rearrange("(c p) n -> p c n", p=128)),
                               inc="ld", inc_n=16)
                vto = S.add("vector", set_vt_ones, inc="v")

                for ot in range(8):
                    dstb = Qf if ot < 4 else Kf
                    for nb in range(4):
                        par = nb % 2
                        ps = (psA, psB)[par]
                        pv = st["psA_v"] if par == 0 else st["psB_v"]
                        pa = st["psA_a"] if par == 0 else st["psB_a"]
                        for ic in range(DC):
                            i = ic % 2
                            ltile = load_shift16(xo16[i], xi16[b], ic, nb,
                                                 war=[("t", st["xo16_rd"][i])])
                            mmw = ltile + ([("v", pv), ("a", pa), ("_", wqkv_ld)]
                                           if ic == 0 else [])
                            st["xo16_rd"][i] = S.add(
                                "tensor", lambda e, ot=ot, ic=ic, ps=ps, i=i:
                                e.matmul(out=ps[0:128, 0:512],
                                         lhsT=wqkv_sb[:, ic * 1536 + ot * 128:
                                                      ic * 1536 + (ot + 1) * 128],
                                         rhs=xo16[i],
                                         start=(ic == 0), stop=(ic == DC - 1)),
                                wait=mmw, inc="t")
                        mm = S.val("t")
                        ev = S.add("vector", lambda e, ot=ot, nb=nb, ps=ps, dstb=dstb:
                                   e.tensor_copy(
                                       out=dstb[:, (ot % 4) * NSEQ + nb * 512:
                                                (ot % 4) * NSEQ + (nb + 1) * 512],
                                       in_=ps[0:128, 0:512]),
                                   wait=[("t", mm), ("t", st["qf_rd"])], inc="v")
                        if par == 0:
                            st["psA_v"] = ev
                        else:
                            st["psB_v"] = ev
                qk_v0 = S.val("v")

                for nt in range(NT):
                    par = nt % 2
                    ps = (psA, psB)[par]
                    pv = st["psA_v"] if par == 0 else st["psB_v"]
                    pa = st["psA_a"] if par == 0 else st["psB_a"]
                    for ic in range(DC):
                        i = ic % 2
                        shift = ic >= 4
                        if shift and nt == 0:
                            ms = S.add("vector", lambda e, i=i: e.memset(vls[i][:, 0:1], 0.0),
                                       wait=[("t", st["vls_rd"][i])], inc="v")
                            L = S.add("gpsimd", lambda e, ic=ic, b=b, i=i:
                                      e.dma_start(out=vls[i][:, 1:128],
                                                  in_=xi16[b, ic * 128:(ic + 1) * 128, 0:127]),
                                      wait=[("t", st["vls_rd"][i])], inc="g", inc_n=16)
                            ltile = [("_", L), ("v", ms)]
                        else:
                            o0 = nt * 128 - (1 if shift else 0)
                            L = S.add("gpsimd", lambda e, ic=ic, b=b, i=i, o0=o0:
                                      e.dma_start(out=vls[i],
                                                  in_=xi16[b, ic * 128:(ic + 1) * 128,
                                                           o0:o0 + 128]),
                                      wait=[("t", st["vls_rd"][i])], inc="g", inc_n=16)
                            ltile = [("_", L)]
                        st["vls_rd"][i] = S.add(
                            "tensor", lambda e, nt=nt, ic=ic, ps=ps, i=i:
                            e.matmul(out=ps[0:128, 0:512],
                                     lhsT=vls[i],
                                     rhs=wqkv_sb[:, ic * 1536 + 1024: ic * 1536 + 1536],
                                     start=(ic == 0), stop=(ic == DC - 1)),
                            wait=ltile + ([("v", pv), ("a", pa)] if ic == 0 else []),
                            inc="t")
                    mm = S.val("t")
                    ev = S.add("vector", lambda e, nt=nt, ps=ps:
                               e.tensor_copy(
                                   out=Vt[:, nt * P6_HEADS * 65: (nt + 1) * P6_HEADS * 65]
                                   .rearrange("p (h j) -> p h j", j=65)[:, :, 0:64],
                                   in_=ps[0:128, 0:512].rearrange("p (h j) -> p h j", j=64)),
                               wait=[("t", mm), ("t", st["vt_rd"]), ("v", vto)], inc="v")
                    if par == 0:
                        st["psA_v"] = ev
                    else:
                        st["psB_v"] = ev

                for half in range(2):
                    for sb2 in range(2):
                        nbx = half * 2 + sb2
                        for cc in range(8):
                            src = Qf if cc < 4 else Kf
                            c4 = cc % 4
                            sv = S.add("vector",
                                       lambda e, src=src, c4=c4, nbx=nbx:
                                       e.tensor_mul(
                                           out=sq512,
                                           in0=src[:, c4 * NSEQ + nbx * 512:
                                                   c4 * NSEQ + (nbx + 1) * 512],
                                           in1=src[:, c4 * NSEQ + nbx * 512:
                                                   c4 * NSEQ + (nbx + 1) * 512]),
                                       wait=[("t", st["sq_rd"]), ("v", qk_v0)], inc="v")
                            mm = S.add("tensor", lambda e, cc=cc, sb2=sb2:
                                       e.matmul(out=psO[0:16, sb2 * 512:(sb2 + 1) * 512],
                                                lhsT=HIND[:, cc * 16:(cc + 1) * 16],
                                                rhs=sq512, start=(cc == 0), stop=(cc == 7)),
                                       wait=[("v", sv)] +
                                            ([("v", st["psO_v"]), ("a", st["psO_a"])]
                                             if (cc == 0 and sb2 == 0) else []),
                                       inc="t")
                            st["sq_rd"] = mm
                    lmm = S.val("t")
                    a1 = S.add("scalar", lambda e:
                               e.activation(out=NR, in_=psO[0:16, 0:1024], func=AF.Ln),
                               wait=[("t", lmm), ("_", st["nr_rd"])],
                               inc="a")
                    st["psO_a"] = a1
                    a2 = S.add("scalar", lambda e:
                               e.activation(out=NR, in_=NR, func=AF.Exp, scale=-0.5),
                               inc="a")
                    st["nr_rd"] = S.add("gpsimd", lambda e, half=half:
                                        e.dma_start(out=nr_scr[:, half * 1024:(half + 1) * 1024],
                                                    in_=NR),
                                        wait=[("a", a2)], inc="g", inc_n=16)
                    st["psO_v"] = 0 if False else st["psO_v"]
                nr_g = st["nr_rd"]

                for qk in range(2):
                    src = (Qf, Kf)[qk]
                    scol_ = (qscol, kscol)[qk]
                    CT = (CSQ, CSK)[qk]
                    ST_ = (SSQ, SSK)[qk]
                    for c4 in range(4):
                        la = S.add("gpsimd", lambda e, qk=qk, c4=c4:
                                   e.dma_start(out=nrA, in_=nr_scr[qk * 8 + 2 * c4:
                                                                   qk * 8 + 2 * c4 + 1, :]),
                                   wait=[("_", nr_g), ("t", st["nrow_rd"])], inc="g", inc_n=16)
                        lb_ = S.add("gpsimd", lambda e, qk=qk, c4=c4:
                                    e.dma_start(out=nrB, in_=nr_scr[qk * 8 + 2 * c4 + 1:
                                                                    qk * 8 + 2 * c4 + 2, :]),
                                    inc="g", inc_n=16)
                        for nb in range(4):
                            sl = slice(c4 * NSEQ + nb * 512, c4 * NSEQ + (nb + 1) * 512)
                            tsl = slice(nb * 512, (nb + 1) * 512)
                            r1_ = S.add("tensor", lambda e, nb=nb:
                                        e.matmul(out=psA[0:64, 0:512],
                                                 lhsT=ones_row[0:1, 0:64],
                                                 rhs=nrA[0:1, nb * 512:(nb + 1) * 512],
                                                 start=True, stop=True),
                                        wait=[("_", la), ("_", lb_), ("v", st["psA_v"]),
                                              ("a", st["psA_a"])], inc="t")
                            r2_ = S.add("tensor", lambda e, nb=nb:
                                        e.matmul(out=psA[64:128, 0:512],
                                                 lhsT=ones_row[0:1, 0:64],
                                                 rhs=nrB[0:1, nb * 512:(nb + 1) * 512],
                                                 start=True, stop=True),
                                        inc="t")
                            st["nrow_rd"] = r2_
                            sv = S.add("vector", lambda e, src=src, sl=sl, scol_=scol_:
                                       e.scalar_tensor_tensor(out=tmp1, in0=src[:, sl],
                                                              scalar=scol_[:, 0:1],
                                                              in1=psA[0:128, 0:512],
                                                              op0=ALU.mult, op1=ALU.mult),
                                       wait=[("t", r2_), ("t", st["tmp1_rd"]),
                                             ("v", qk_v0)] + lay_f[0:1], inc="v")
                            st["psA_v"] = sv
                            pm = S.add("tensor", lambda e:
                                       e.matmul(out=psB[0:128, 0:512], lhsT=permT,
                                                rhs=tmp1, start=True, stop=True),
                                       wait=[("v", sv), ("v", st["psB_v"]),
                                            ("a", st["psB_a"])], inc="t")
                            st["tmp1_rd"] = pm
                            t2 = S.add("vector", lambda e, tsl=tsl, ST_=ST_:
                                       e.tensor_mul(out=tmp2, in0=psB[0:128, 0:512],
                                                    in1=ST_[:, tsl]),
                                       wait=[("t", pm), ("v", st["tmp2_rd"])], inc="v")
                            st["psB_v"] = t2
                            t1 = S.add("vector", lambda e, tsl=tsl, CT=CT:
                                       e.tensor_mul(out=ytmp.bitcast(BF16)[:, 0:512],
                                                    in0=tmp1, in1=CT[:, tsl]),
                                       inc="v")
                            t3 = S.add("vector", lambda e, src=src, sl=sl:
                                       e.tensor_add(out=src[:, sl],
                                                    in0=ytmp.bitcast(BF16)[:, 0:512],
                                                    in1=tmp2),
                                       inc="v")
                            st["tmp2_rd"] = t3
                qk_v = S.val("v")

                for h in range(P6_HEADS):
                    c4, hh = h // 2, h % 2
                    base = hh * 64
                    for w in range(4):
                        kt0 = max(0, w * 4 - 4)
                        kts = list(range(kt0, w * 4 + 4))
                        pvmm = 0
                        for jx, kt in enumerate(kts):
                            par = jx % 2
                            pss = psA[0:128, 0:512] if par == 0 else psA[0:128, 512:1024]
                            pv = st["psA_v"] if par == 0 else st.get("psA2_v", 0)
                            pa = st["psA_a"] if par == 0 else st.get("psA2_a", 0)
                            smm = S.add("tensor",
                                        lambda e, c4=c4, base=base, kt=kt, w=w, pss=pss:
                                        e.matmul(out=pss,
                                                 lhsT=Kf[base:base + 64,
                                                         c4 * NSEQ + kt * 128:
                                                         c4 * NSEQ + (kt + 1) * 128],
                                                 rhs=Qf[base:base + 64,
                                                        c4 * NSEQ + w * 512:
                                                        c4 * NSEQ + (w + 1) * 512],
                                                 start=True, stop=True),
                                        wait=[("v", pv), ("a", pa), ("v", qk_v)], inc="t")
                            r = kt * 128 - w * 512
                            mv = S.add("vector", lambda e, pss=pss, r=r:
                                       e.tensor_add(out=Lb6, in0=pss,
                                                    in1=CMSK[:, 384 - r: 384 - r + 512]),
                                       wait=[("t", smm), ("a", st["lb6_a"])], inc="v")
                            if par == 0:
                                st["psA_v"] = mv
                            else:
                                st["psA2_v"] = mv
                            pb = jx % 2
                            aexp = S.add("scalar", lambda e, pb=pb:
                                         e.activation(out=Pb6[pb], in_=Lb6, func=AF.Exp),
                                         wait=[("v", mv), ("t", st["P_rd"][pb])], inc="a")
                            st["lb6_a"] = aexp
                            pvmm = S.add("tensor", lambda e, kt=kt, h=h, pb=pb,
                                         k0=kts[0], k1=kts[-1]:
                                         e.matmul(out=psO[0:65, 0:512],
                                                  lhsT=Vt[:, (kt * P6_HEADS + h) * 65:
                                                          (kt * P6_HEADS + h + 1) * 65],
                                                  rhs=Pb6[pb],
                                                  start=(kt == k0), stop=(kt == k1)),
                                         wait=([("v", st["psO_v"])] if kt == kts[0] else []) +
                                              [("a", aexp)],
                                         inc="t")
                            st["P_rd"][pb] = pvmm
                            st["vt_rd"] = pvmm
                        def recip6(e):
                            with nc.allow_low_precision(reason="softmax denom bf16"):
                                return e.reciprocal(out=rrow, in_=psO[64:65, 0:512])
                        vr = S.add("vector", recip6,
                                   wait=[("t", pvmm), ("t", st["rrow_rd"])], inc="v")
                        rmm = S.add("tensor", lambda e:
                                    e.matmul(out=psO[0:64, 512:1024],
                                             lhsT=ones_row[0:1, 0:64],
                                             rhs=rrow, start=True, stop=True),
                                    wait=[("v", vr)], inc="t")
                        st["rrow_rd"] = rmm
                        vR = S.add("vector", lambda e:
                                   e.tensor_copy(out=Rsb, in_=psO[0:64, 512:1024]),
                                   wait=[("t", rmm)], inc="v")
                        oi = w % 2
                        ouev = S.add("vector", lambda e, oi=oi:
                                     e.tensor_mul(out=oust[oi], in0=psO[0:64, 0:512],
                                                  in1=Rsb),
                                     wait=[("v", vR), ("_", st["oust_ld"][oi])], inc="v")
                        st["psO_v"] = ouev
                        st["oust_ld"][oi] = S.add(
                            "sync", lambda e, h=h, w=w, oi=oi:
                            e.dma_start(out=ou_scr[h * 64:(h + 1) * 64,
                                                   w * 512:(w + 1) * 512],
                                        in_=oust[oi]),
                            wait=[("v", ouev)], inc="ld", inc_n=16)
                st["qf_rd"] = S.val("t")

                ou_f = S.dma_fence("ld")
                for nb in range(4):
                    lo = S.add("sync", lambda e, nb=nb:
                               e.dma_start(out=Oun.rearrange("p (pr j) -> p pr j", pr=4),
                                           in_=ou_scr.rearrange("(pr p) (nb j) -> p pr nb j",
                                                                p=128, nb=4)[:, :, nb, :]),
                               wait=ou_f + [("t", st["oun_rd"])], inc="ld", inc_n=16)
                    for g2 in range(2):
                        for pr in range(4):
                            for oo in range(4):
                                oc = g2 * 4 + oo
                                ps = (psA, psB)[oo // 2]
                                hf = oo % 2
                                S.add("tensor",
                                      lambda e, pr=pr, oc=oc, ps=ps, hf=hf:
                                      e.matmul(out=ps[0:128, hf * 512:(hf + 1) * 512],
                                               lhsT=wo_sb[:, pr * D + oc * 128:
                                                          pr * D + (oc + 1) * 128],
                                               rhs=Oun[:, pr * 512:(pr + 1) * 512],
                                               start=(pr == 0), stop=(pr == 3)),
                                      wait=([("_", lo), ("_", wol_ld),
                                             ("v", st["psA_v"] if oo // 2 == 0 else st["psB_v"]),
                                             ("v", st.get("psA2_v", 0) if oo // 2 == 0 else 0),
                                             ("a", st["psA_a"] if oo // 2 == 0 else st["psB_a"])]
                                            if pr == 0 and hf == 0 else []),
                                      inc="t")
                        mm_g = S.val("t")
                        st["oun_rd"] = mm_g
                        for oo in range(4):
                            oc = g2 * 4 + oo
                            ps = (psA, psB)[oo // 2]
                            hf = oo % 2
                            i = oc % 2
                            lx = S.add("sync", lambda e, b=b, oc=oc, nb=nb, i=i:
                                       e.dma_start(out=x2t[i],
                                                   in_=xi32[b, oc * 128:(oc + 1) * 128,
                                                            nb * 512:(nb + 1) * 512]),
                                       wait=[("v", st["x2t_v"][i])], inc="ld", inc_n=16)
                            ev = S.add("vector", lambda e, i=i, ps=ps, hf=hf:
                                       e.tensor_add(out=xot[i],
                                                    in0=ps[0:128, hf * 512:(hf + 1) * 512],
                                                    in1=x2t[i]),
                                       wait=[("t", mm_g), ("_", lx),
                                             ("_", st["xot_ld"][i])], inc="v")
                            st["x2t_v"][i] = ev
                            if oo // 2 == 0:
                                st["psA_v"] = ev
                                st["psA2_v"] = ev
                            else:
                                st["psB_v"] = ev
                            cv = S.add("vector", lambda e, i=i:
                                       e.tensor_copy(out=xo16[i], in_=xot[i]),
                                       wait=[("_", st["xo16_ld"][i]),
                                             ("t", st["xo16_rd"][i])], inc="v")
                            st["xot_ld"][i] = S.add(
                                "sync", lambda e, b=b, oc=oc, nb=nb, i=i:
                                e.dma_start(out=xi32[b, oc * 128:(oc + 1) * 128,
                                                     nb * 512:(nb + 1) * 512],
                                            in_=xot[i]),
                                wait=[("v", cv)], inc="ld", inc_n=16)
                            st["xo16_ld"][i] = S.add(
                                "sync", lambda e, b=b, oc=oc, nb=nb, i=i:
                                e.dma_start(out=xi16[b, oc * 128:(oc + 1) * 128,
                                                     nb * 512:(nb + 1) * 512],
                                            in_=xo16[i]),
                                wait=[("v", cv)], inc="ld", inc_n=16)
                wo_res_f = S.dma_fence("ld")
                if attn_only:
                    continue

                def ln2_tile(oc, nb):
                    cw = load_shift16(x2c, xi16[b], oc, nb,
                                      war=[("t", st["x2c_rd"])] + (wo_res_f if oc == 0 and nb == 0 else []))
                    return cw
                fm_ln_stats(ln2_tile)
                bc2 = ln_rows_and_bcast()
                for oc in range(DC):
                    for nb in range(4):
                        k = oc * 4 + nb
                        i = k % 2
                        shift = oc >= 4
                        if shift and nb == 0:
                            msk_ = S.add("vector", lambda e, i=i: e.memset(x2t[i][:, 0:1], 0.0),
                                         wait=[("v", st["x2t_v"][i])], inc="v")
                            lxw = [("v", msk_)]
                            lx = S.add("gpsimd", lambda e, b=b, oc=oc, i=i:
                                       e.dma_start(out=x2t[i][:, 1:512],
                                                   in_=xi32[b, oc * 128:(oc + 1) * 128, 0:511]),
                                       wait=[("v", st["x2t_v"][i])], inc="g", inc_n=16)
                        else:
                            off = nb * 512 - (1 if shift else 0)
                            lx = S.add("gpsimd", lambda e, b=b, oc=oc, off=off, i=i:
                                       e.dma_start(out=x2t[i],
                                                   in_=xi32[b, oc * 128:(oc + 1) * 128,
                                                            off:off + 512]),
                                       wait=[("v", st["x2t_v"][i])], inc="g", inc_n=16)
                            lxw = []
                        v1_ = S.add("vector", lambda e, i=i, nb=nb:
                                    e.tensor_sub(out=ytmp, in0=x2t[i],
                                                 in1=Mt[:, nb * 512:(nb + 1) * 512]),
                                    wait=[("_", lx), ("v", bc2)] + lxw, inc="v")
                        st["x2t_v"][i] = v1_
                        v2_ = S.add("vector", lambda e, oc=oc, nb=nb:
                                    e.scalar_tensor_tensor(out=ytmp2, in0=ytmp,
                                                           scalar=lgcol[:, oc:oc + 1],
                                                           in1=Rt[:, nb * 512:(nb + 1) * 512],
                                                           op0=ALU.mult, op1=ALU.mult),
                                    wait=[("v", st["ytmp2_rd"]) if False else ("v", 0)], inc="v")
                        v3_ = S.add("vector", lambda e, oc=oc, nb=nb:
                                    e.tensor_scalar_add(
                                        out=LNH[:, oc * NSEQ + nb * 512:
                                                oc * NSEQ + (nb + 1) * 512],
                                        in0=ytmp2, scalar1=lbcol[:, oc:oc + 1]),
                                    wait=[("t", st["lnh_rd"])], inc="v")
                lnh_v = S.val("v")

                for nh in range(2):
                    hg_v = 0
                    for j in range(HJ):
                        pr = min(128, P6_HID - j * 128)
                        wi = j % 2
                        wl1 = S.add("sync", lambda e, l=l, j=j, pr=pr, wi=wi:
                                    e.dma_start(
                                        out=w1a_sb[wi].rearrange("p (c n) -> p c n", c=DC)[:, :, 0:pr],
                                        in_=w116[l].rearrange("(c p) n -> p c n", p=128)
                                        [:, :, j * 128: j * 128 + pr]),
                                    wait=[("t", st["w1_rd"][wi])], inc="ld", inc_n=16)
                        wl2 = S.add("sync", lambda e, l=l, j=j, pr=pr, wi=wi:
                                    e.dma_start(
                                        out=w1g_sb[wi].rearrange("p (c n) -> p c n", c=DC)[:, :, 0:pr],
                                        in_=w116[l].rearrange("(c p) n -> p c n", p=128)
                                        [:, :, P6_HID + j * 128: P6_HID + j * 128 + pr]),
                                    inc="ld", inc_n=16)
                        for sb2 in range(2):
                            nb = nh * 2 + sb2
                            for ic in range(DC):
                                S.add("tensor", lambda e, j=j, pr=pr, ic=ic, nb=nb, wi=wi:
                                      e.matmul(out=psA[0:pr, 0:512],
                                               lhsT=w1a_sb[wi][:, ic * 128: ic * 128 + pr],
                                               rhs=LNH[:, ic * NSEQ + nb * 512:
                                                       ic * NSEQ + (nb + 1) * 512],
                                               start=(ic == 0), stop=(ic == DC - 1)),
                                      wait=([("_", wl1), ("v", lnh_v), ("v", st["psA_v"]),
                                             ("v", st.get("psA2_v", 0)), ("a", st["psA_a"])]
                                            if ic == 0 else []),
                                      inc="t")
                                S.add("tensor", lambda e, j=j, pr=pr, ic=ic, nb=nb, wi=wi:
                                      e.matmul(out=psB[0:pr, 0:512],
                                               lhsT=w1g_sb[wi][:, ic * 128: ic * 128 + pr],
                                               rhs=LNH[:, ic * NSEQ + nb * 512:
                                                       ic * NSEQ + (nb + 1) * 512],
                                               start=(ic == 0), stop=(ic == DC - 1)),
                                      wait=([("_", wl2), ("v", st["psB_v"]), ("a", st["psB_a"])]
                                            if ic == 0 else []),
                                      inc="t")
                            mmj = S.val("t")
                            st["w1_rd"][wi] = mmj
                            ag = S.add("scalar", lambda e, j=j, pr=pr:
                                       e.activation(out=ffG[0:pr, :], in_=psB[0:pr, 0:512],
                                                    func=AF.Gelu,
                                                    bias=b1gcol[0:pr, j:j + 1]),
                                       wait=[("t", mmj), ("v", st.get("ffG_rd", 0))], inc="a")
                            st["psB_a"] = ag
                            av = S.add("vector", lambda e, j=j, pr=pr:
                                       e.tensor_scalar_add(out=ffA[0:pr, :],
                                                           in0=psA[0:pr, 0:512],
                                                           scalar1=b1acol[0:pr, j:j + 1]),
                                       wait=[("t", mmj)], inc="v")
                            st["psA_v"] = av
                            st["psA_a"] = 0 if False else st["psA_a"]
                            hv = S.add("vector", lambda e, j=j, pr=pr, sb2=sb2:
                                       e.tensor_mul(out=hg[0:pr, j * 1024 + sb2 * 512:
                                                           j * 1024 + (sb2 + 1) * 512],
                                                    in0=ffA[0:pr, :], in1=ffG[0:pr, :]),
                                       wait=[("a", ag), ("t", st["hg_rd"])], inc="v")
                            st["ffG_rd"] = hv
                            hg_v = hv
                    for oc in range(DC):
                        wi = oc % 2
                        wl_a = S.add("sync", lambda e, l=l, oc=oc, wi=wi:
                                     e.dma_start(
                                         out=w2_sb[wi].rearrange("p (c n) -> p c n", c=HJ)[:, 0:21, :],
                                         in_=w216[l][0:2688, oc * 128:(oc + 1) * 128]
                                         .rearrange("(c p) n -> p c n", p=128)),
                                     wait=[("t", st["w2_rd"][wi])], inc="ld", inc_n=16)
                        wl = S.add("sync", lambda e, l=l, oc=oc, wi=wi:
                                   e.dma_start(
                                       out=w2_sb[wi][0:42, 21 * 128:22 * 128],
                                       in_=w216[l][2688:2730, oc * 128:(oc + 1) * 128]),
                                   inc="ld", inc_n=16)
                        for sb2 in range(2):
                            nb = nh * 2 + sb2
                            par = sb2
                            ps = (psA, psB)[par]
                            pv = st["psA_v"] if par == 0 else st["psB_v"]
                            pa = st["psA_a"] if par == 0 else st["psB_a"]
                            for j in range(HJ):
                                pr = min(128, P6_HID - j * 128)
                                S.add("tensor", lambda e, j=j, pr=pr, oc=oc, nb=nb,
                                      sb2=sb2, wi=wi, ps=ps:
                                      e.matmul(out=ps[0:128, 0:512],
                                               lhsT=w2_sb[wi][0:pr, j * 128:(j + 1) * 128],
                                               rhs=hg[0:pr, j * 1024 + sb2 * 512:
                                                      j * 1024 + (sb2 + 1) * 512],
                                               start=(j == 0), stop=(j == HJ - 1)),
                                      wait=([("_", wl), ("v", hg_v), ("v", pv), ("a", pa)]
                                            if j == 0 else []) +
                                           ([("_", wl_a)] if j == 0 else []),
                                      inc="t")
                            mmo = S.val("t")
                            st["w2_rd"][wi] = mmo
                            st["hg_rd"] = mmo
                            i = (oc * 2 + sb2) % 2
                            lx = S.add("sync", lambda e, b=b, oc=oc, nb=nb, i=i:
                                       e.dma_start(out=x2t[i],
                                                   in_=xi32[b, oc * 128:(oc + 1) * 128,
                                                            nb * 512:(nb + 1) * 512]),
                                       wait=[("v", st["x2t_v"][i])], inc="ld", inc_n=16)
                            ev = S.add("vector", lambda e, oc=oc, i=i, ps=ps:
                                       e.scalar_tensor_tensor(out=xot[i], in0=ps[0:128, 0:512],
                                                              scalar=b2col[:, oc:oc + 1],
                                                              in1=x2t[i],
                                                              op0=ALU.add, op1=ALU.add),
                                       wait=[("t", mmo), ("_", lx), ("_", st["xot_ld"][i])],
                                       inc="v")
                            st["x2t_v"][i] = ev
                            if par == 0:
                                st["psA_v"] = ev
                            else:
                                st["psB_v"] = ev
                            cv = S.add("vector", lambda e, i=i:
                                       e.tensor_copy(out=xo16[i], in_=xot[i]),
                                       wait=[("_", st["xo16_ld"][i]),
                                             ("t", st["xo16_rd"][i])], inc="v")
                            st["xot_ld"][i] = S.add(
                                "sync", lambda e, b=b, oc=oc, nb=nb, i=i:
                                e.dma_start(out=xi32[b, oc * 128:(oc + 1) * 128,
                                                     nb * 512:(nb + 1) * 512],
                                            in_=xot[i]),
                                wait=[("v", ev)], inc="ld", inc_n=16)
                            st["xo16_ld"][i] = S.add(
                                "sync", lambda e, b=b, oc=oc, nb=nb, i=i:
                                e.dma_start(out=xi16[b, oc * 128:(oc + 1) * 128,
                                                     nb * 512:(nb + 1) * 512],
                                            in_=xo16[i]),
                                wait=[("v", cv)], inc="ld", inc_n=16)
                st["lnh_rd"] = S.val("t")

        for b in range(n_batches):
            S.barrier()
            for k in ("psA_v", "psB_v", "psO_v"):
                st[k] = 0
            st["psA_a"] = st["psB_a"] = 0
            for nt in range(NT):
                i = nt % 2
                last_tr = 0
                for c in range(DC):
                    j = c % 2
                    lx1 = S.add("sync", lambda e, b=b, c=c, nt=nt, j=j:
                                e.dma_start(out=x2t[j][:, 0:128],
                                            in_=xi32[b, c * 128:(c + 1) * 128,
                                                     nt * 128:(nt + 1) * 128]),
                                wait=[("v", st["x2t_v"][j])], inc="ld", inc_n=16)
                    lx2 = S.add("sync", lambda e, b=b, c=c, nt=nt, j=j:
                                e.dma_start(out=x2t[j][:, 128:256],
                                            in_=x3_in[b, c * 128:(c + 1) * 128,
                                                      nt * 128:(nt + 1) * 128]),
                                inc="ld", inc_n=16)
                    sv = S.add("vector", lambda e, j=j:
                               e.tensor_add(out=xot[j][:, 0:128], in0=x2t[j][:, 0:128],
                                            in1=x2t[j][:, 128:256]),
                               wait=[("_", lx1), ("_", lx2), ("t", st.get("xot_tr", 0))],
                               inc="v")
                    st["x2t_v"][j] = sv
                    par = c % 2
                    ps = (psA, psB)[par]
                    pv = st["psA_v"] if par == 0 else st["psB_v"]
                    tr = S.add("tensor", lambda e, j=j, ps=ps:
                               e.transpose(out=ps[0:128, 0:128], in_=xot[j][:, 0:128],
                                           identity=id32),
                               wait=[("v", sv), ("v", pv)], inc="t")
                    st["xot_tr"] = tr
                    evv = S.add("vector", lambda e, c=c, i=i, ps=ps:
                                e.tensor_copy(out=xstage[i][:, c * 128:(c + 1) * 128],
                                              in_=ps[0:128, 0:128]),
                                wait=[("t", tr), ("_", st["xstage_ld"][i])], inc="v")
                    if par == 0:
                        st["psA_v"] = evv
                    else:
                        st["psB_v"] = evv
                    last_tr = evv
                st["xstage_ld"][i] = S.add(
                    "sync", lambda e, b=b, nt=nt, i=i:
                    e.dma_start(out=x_out[b, nt * 128:(nt + 1) * 128, :], in_=xstage[i]),
                    wait=[("v", last_tr)], inc="ld", inc_n=16)

        if C is None:
            S.add("sync", None, wait=S.dma_fence("ld") + S.dma_fence("g"))
            S.emit(nc, stack)
    return nc


def build_fused(n_batches=NB):
    nc = bass.Bass("TRN2", target_bir_lowering=False, detect_race_conditions=False)
    S = Sched(SEMS + ("cc",))
    NL = NSEQ // NCORES

    io = {}
    io["x"] = nc.dram_tensor("x", [n_batches, NSEQ, D], F32, kind="ExternalInput").ap()
    for n in ("wq", "wk", "wv", "wo"):
        io["l0a_" + n] = nc.dram_tensor("l0a_" + n, [D, D], F32, kind="ExternalInput").ap()
        io["l1a_" + n] = nc.dram_tensor("l1a_" + n, [D, D], F32, kind="ExternalInput").ap()
    for n in ("bq", "bk", "bv", "bo"):
        io["l0a_" + n] = nc.dram_tensor("l0a_" + n, [D, 1], F32, kind="ExternalInput").ap()
        io["l1a_" + n] = nc.dram_tensor("l1a_" + n, [D, 1], F32, kind="ExternalInput").ap()
    for n in ("wf", "wi", "wc", "uf", "ui", "uc", "wo", "uo"):
        io["l0f_" + n] = nc.dram_tensor("l0f_" + n, [D, D], F32, kind="ExternalInput").ap()
    for n in ("bf", "bi", "bc", "bo", "in_c"):
        io["l0f_" + n] = nc.dram_tensor("l0f_" + n, [D, 1], F32, kind="ExternalInput").ap()
    for n, shp in (("wqkv", [2, D, 3 * P6_INNER]), ("wo", [2, P6_INNER, D]),
                   ("w1", [2, D, 2 * P6_HID]), ("w2", [2, P6_HID, D]),
                   ("qs", [2, HD]), ("ks", [2, HD]), ("lng", [2, D]),
                   ("lnb", [2, D]), ("b1", [2, 2 * P6_HID]), ("b2", [2, D])):
        io["l1f_" + n] = nc.dram_tensor("l1f_" + n, shp, F32, kind="ExternalInput").ap()

    io["x1_fm"] = nc.dram_tensor("x1_fm", [n_batches, D, NSEQ], F32, kind="Internal").ap()
    io["yL_fm"] = nc.dram_tensor("yL_fm", [n_batches, D, NSEQ], BF16, kind="Internal").ap()
    io["y_all"] = nc.dram_tensor("y_all", [NCORES * n_batches, D, NSEQ], BF16,
                                 kind="Internal", addr_space="Shared").ap()
    io["c_mine"] = nc.dram_tensor("c_mine", [D, NL], F32, kind="Internal").ap()
    io["c_all"] = nc.dram_tensor("c_all", [NCORES, D, NL], F32, kind="Internal",
                                 addr_space="Shared").ap()
    io["x2_fm"] = nc.dram_tensor("x2_fm", [n_batches, D, NSEQ], F32, kind="Internal").ap()
    io["x3_fm"] = nc.dram_tensor("x3_fm", [n_batches, D, NSEQ], F32, kind="Internal").ap()
    io["xout"] = nc.dram_tensor("xout", [n_batches, NSEQ, D], F32,
                                kind="ExternalOutput").ap()

    C = FCtx()
    C.nc, C.S, C.io = nc, S, io
    C.consts = {}
    C.y_ready = [("cc", 1)]
    C.c_ready = [("cc", 2)]
    rg = [list(range(NCORES))]

    with ExitStack() as stack:
        C.AR = Arena(stack.enter_context(
            nc.sbuf_tensor("AR", [128, 53160], F32)).ap(), 53160)
        e_ = stack.enter_context
        C.psA = e_(nc.psum_tensor("psA", [128, 1024], F32)).ap()
        C.psB = e_(nc.psum_tensor("psB", [128, 1024], F32)).ap()
        C.psO = e_(nc.psum_tensor("psO", [128, 1024], F32)).ap()
        C.pt16 = e_(nc.psum_tensor("pt16", [128, 2048], BF16)).ap()

        build_p1(n_batches, C=C)
        S.barrier()
        S.add("gpsimd", lambda e: e.collective_compute(
            "AllGather", ALU.bypass, rg,
            ins=[io["yL_fm"].opt()], outs=[io["y_all"].opt()]), inc="cc")
        build_p3(nloc=NL, nsteps=NCORES * n_batches, C=C)
        S.barrier()
        S.add("gpsimd", lambda e: e.collective_compute(
            "AllGather", ALU.bypass, rg,
            ins=[io["c_mine"].opt()], outs=[io["c_all"].opt()]), inc="cc")
        build_p4(n_batches, C=C)
        S.barrier()
        build_p5(n_batches, C=C)
        S.barrier()
        build_p6(n_batches, C=C)
        S.add("sync", None, wait=S.dma_fence("ld") + S.dma_fence("g"))
        S.emit(nc, stack)
    return nc


def _col(v):
    return np.ascontiguousarray(np.asarray(v).reshape(-1, 1))


def kernel(**inputs):
    x = np.asarray(inputs["L0_attn_x"])
    cores = list(range(NCORES))
    nc = build_fused()

    def feed(c):
        f = {"x": np.ascontiguousarray(x[4 * c:4 * c + 4])}
        for n in ("wq", "wk", "wv", "wo"):
            f["l0a_" + n] = inputs["L0_attn_" + n]
            f["l1a_" + n] = inputs["L1_attn_W" + n[1:]]
        for n in ("bq", "bk", "bv", "bo"):
            f["l0a_" + n] = _col(inputs["L0_attn_" + n])
            f["l1a_" + n] = _col(inputs["L1_attn_" + n])
        for n in ("wf", "wi", "wc", "uf", "ui", "uc", "wo", "uo"):
            f["l0f_" + n] = inputs["L0_ffn_" + n]
        for n in ("bf", "bi", "bc", "bo", "in_c"):
            f["l0f_" + n] = _col(inputs["L0_ffn_" + n])
        for n, src in (("wqkv", "wqkv"), ("wo", "wo"), ("w1", "w1"), ("w2", "w2"),
                       ("qs", "q_scale"), ("ks", "k_scale"), ("lng", "ln_g"),
                       ("lnb", "ln_b"), ("b1", "b1"), ("b2", "b2")):
            f["l1f_" + n] = inputs["L1_ffn_" + src]
        return f

    res = run_bass_kernel_spmd(nc, [feed(c) for c in cores], core_ids=cores).results
    return np.concatenate([r["xout"] for r in res], axis=0)


def kernel_unfused(**inputs):
    x = np.asarray(inputs["L0_attn_x"])
    cores = list(range(NCORES))

    nc1 = build_p1()
    in1 = [{"x": np.ascontiguousarray(x[4 * c:4 * c + 4]),
            "wq": inputs["L0_attn_wq"], "wk": inputs["L0_attn_wk"],
            "wv": inputs["L0_attn_wv"], "wo": inputs["L0_attn_wo"],
            "bq": _col(inputs["L0_attn_bq"]), "bk": _col(inputs["L0_attn_bk"]),
            "bv": _col(inputs["L0_attn_bv"]), "bo": _col(inputs["L0_attn_bo"])}
           for c in cores]
    r1 = run_bass_kernel_spmd(nc1, in1, core_ids=cores).results
    x1 = [r["x1_fm"] for r in r1]
    yl = [r["yL_fm"] for r in r1]

    y_all = np.concatenate(yl, axis=0)
    nloc = NSEQ // NCORES
    nc3 = build_p3(nloc=nloc)
    in3 = [{"y_re": np.ascontiguousarray(y_all[:, :, c * nloc:(c + 1) * nloc]),
            "wf": inputs["L0_ffn_wf"], "wi": inputs["L0_ffn_wi"],
            "wc": inputs["L0_ffn_wc"], "uf": inputs["L0_ffn_uf"],
            "ui": inputs["L0_ffn_ui"], "uc": inputs["L0_ffn_uc"],
            "bf": _col(inputs["L0_ffn_bf"]), "bi": _col(inputs["L0_ffn_bi"]),
            "bc": _col(inputs["L0_ffn_bc"]), "in_c": _col(inputs["L0_ffn_in_c"])}
           for c in cores]
    r3 = run_bass_kernel_spmd(nc3, in3, core_ids=cores).results
    c_full = np.concatenate([r["c_fin"] for r in r3], axis=1)

    nc4 = build_p4()
    in4 = [{"x1_fm": x1[c], "yL_fm": yl[c], "c_full": c_full,
            "wo": inputs["L0_ffn_wo"], "uo": inputs["L0_ffn_uo"],
            "bo": _col(inputs["L0_ffn_bo"])}
           for c in cores]
    r4 = run_bass_kernel_spmd(nc4, in4, core_ids=cores).results
    x2 = [r["x2_fm"] for r in r4]

    nc5 = build_p5()
    in5 = [{"x2_fm": x2[c],
            "wq": inputs["L1_attn_Wq"], "wk": inputs["L1_attn_Wk"],
            "wv": inputs["L1_attn_Wv"], "wo": inputs["L1_attn_Wo"],
            "bq": _col(inputs["L1_attn_bq"]), "bk": _col(inputs["L1_attn_bk"]),
            "bv": _col(inputs["L1_attn_bv"]), "bo": _col(inputs["L1_attn_bo"])}
           for c in cores]
    r5 = run_bass_kernel_spmd(nc5, in5, core_ids=cores).results
    x3 = [r["x3_fm"] for r in r5]

    nc6 = build_p6()
    in6 = [{"x3_fm": x3[c], "wqkv": inputs["L1_ffn_wqkv"], "wo_l": inputs["L1_ffn_wo"],
            "w1": inputs["L1_ffn_w1"], "w2": inputs["L1_ffn_w2"],
            "q_scale": inputs["L1_ffn_q_scale"], "k_scale": inputs["L1_ffn_k_scale"],
            "ln_g": inputs["L1_ffn_ln_g"], "ln_b": inputs["L1_ffn_ln_b"],
            "b1": inputs["L1_ffn_b1"], "b2": inputs["L1_ffn_b2"]}
           for c in cores]
    r6 = run_bass_kernel_spmd(nc6, in6, core_ids=cores).results
    return np.concatenate([r["xout"] for r in r6], axis=0)


# version 03
# speedup vs baseline: 1.0475x; 1.0475x over previous
import numpy as np
import ml_dtypes
from contextlib import ExitStack

import concourse.bass as bass
import concourse.mybir as mybir
from concourse.bass_utils import run_bass_kernel_spmd

F32 = mybir.dt.float32
BF16 = mybir.dt.bfloat16
AF = mybir.ActivationFunctionType
ALU = mybir.AluOpType
AX = mybir.AxisListType
BF = ml_dtypes.bfloat16

D = 1024
NSEQ = 2048
NB = 4
NCORES = 8
HEADS = 16
HD = 64
DC = D // 128
NT = NSEQ // 128
EPS = 1e-5


class Sched:
    ENGS = ("sync", "gpsimd", "vector", "scalar", "tensor")
    POOLS = {"ld": ("syncdma", 16), "g": ("gpdma", 8)}

    def __init__(self, sem_names):
        self.streams = {e: [] for e in self.ENGS}
        self.counts = {n: 0 for n in sem_names if n not in self.POOLS}
        self.pool = {}
        self.pool_idx = {}
        for key, (pref, n) in self.POOLS.items():
            self.pool[key] = [f"{pref}{i}" for i in range(n)]
            self.pool_idx[key] = 0
            for s in self.pool[key]:
                self.counts[s] = 0

    def _norm_wait(self, wait):
        out = []
        for item in wait:
            s, v = item
            if isinstance(v, tuple):
                if v[1] > 0:
                    out.append(v)
            elif v:
                out.append((s, v))
        return out

    def add(self, eng, fn=None, wait=(), inc=None, inc_n=1):
        wait = self._norm_wait(wait)
        if inc in self.POOLS:
            key = inc
            s = self.pool[key][self.pool_idx[key] % len(self.pool[key])]
            self.pool_idx[key] += 1
            if self.counts[s] > 0:
                wait = wait + [(s, self.counts[s])]
            self.counts[s] += 16
            self.streams[eng].append((tuple(wait), fn, ((s, 16),)))
            return (s, self.counts[s])
        incs = ()
        ret = None
        if inc is not None:
            self.counts[inc] += inc_n
            ret = self.counts[inc]
            incs = ((inc, inc_n),)
        self.streams[eng].append((tuple(wait), fn, incs))
        return ret

    def dma_fence(self, key):
        return [("_", (s, self.counts[s])) for s in self.pool[key]]

    def val(self, sem):
        return self.counts[sem]

    def barrier(self):
        snap = [(s, self.counts[s]) for s in self.counts]
        for e in self.ENGS:
            if self.streams[e]:
                self.add(e, None, wait=snap)

    def emit(self, nc, stack):
        sems = {n: stack.enter_context(nc.semaphore(n)) for n in self.counts}
        block = stack.enter_context(nc.Block())

        def runner(stream):
            def run(eng):
                for waits, fn, incs in stream:
                    for s, v in waits:
                        eng.wait_ge(sems[s], v)
                    inst = fn(eng) if fn is not None else None
                    for s, n in incs:
                        inst.then_inc(sems[s], n)
            return run

        for e in self.ENGS:
            if self.streams[e]:
                getattr(block, e)(runner(self.streams[e]))


class Arena:
    def __init__(self, ap, words):
        self.ap = ap
        self.words = words
        self.top = 0

    def f32(self, rows, n):
        return self.ap[0:rows, self._take(n)[0]: self.top]

    def bf16(self, rows, n):
        o = self._take((n + 1) // 2)[0]
        return self.ap[0:rows, o:o + (n + 1) // 2].bitcast(BF16)

    def f32_at(self, row0, rows, o, n):
        return self.ap[row0:row0 + rows, o:o + n]

    def bf16_at(self, row0, rows, o, n):
        return self.ap[row0:row0 + rows, o:o + (n + 1) // 2].bitcast(BF16)

    def block(self, n):
        return self._take(n)[0]

    def _take(self, n):
        o = self.top
        self.top += n
        assert self.top <= self.words, f"arena overflow: {self.top} > {self.words}"
        return o, self.top

    def mark(self):
        return self.top

    def reset(self, m):
        self.top = m


SEMS = ("ld", "g", "v", "a", "t", "gp")


class FCtx:
    pass


def _const(nc, C, name, arr):
    if C is not None:
        if name not in C.consts:
            C.consts[name] = nc.inline_tensor(arr, name=name)
        return C.consts[name]
    return nc.inline_tensor(arr, name=name)


def build_p1(n_batches=NB, C=None):
    if C is None:
        nc = bass.Bass("TRN2", target_bir_lowering=False, detect_race_conditions=False)
        S = Sched(SEMS)
        x_in = nc.dram_tensor("x", [n_batches, NSEQ, D], F32, kind="ExternalInput").ap()
        w_dr = {n: nc.dram_tensor(n, [D, D], F32, kind="ExternalInput").ap()
                for n in ("wq", "wk", "wv", "wo")}
        b_dr = {n: nc.dram_tensor(n, [D, 1], F32, kind="ExternalInput").ap()
                for n in ("bq", "bk", "bv", "bo")}
        x1_out = nc.dram_tensor("x1_fm", [n_batches, D, NSEQ], F32, kind="ExternalOutput").ap()
        y_out = nc.dram_tensor("yL_fm", [n_batches, D, NSEQ], BF16, kind="ExternalOutput").ap()
    else:
        nc, S = C.nc, C.S
        x_in = C.io["x"]
        w_dr = {n: C.io["l0a_" + n] for n in ("wq", "wk", "wv", "wo")}
        b_dr = {n: C.io["l0a_" + n] for n in ("bq", "bk", "bv", "bo")}
        x1_out = C.io["x1_fm"]
        y_out = C.io["yL_fm"]

    xfm_scr = nc.dram_tensor("xfm_scr", [D, NSEQ], F32, kind="Internal").ap()
    q2_scr = nc.dram_tensor("q2_scr", [HEADS, NSEQ], BF16, kind="Internal").ap()
    k2_scr = nc.dram_tensor("k2_scr", [HEADS, NSEQ], BF16, kind="Internal").ap()
    ou_scr = nc.dram_tensor("p1_ou", [D, NSEQ], BF16, kind="Internal").ap()
    wprep = {n: nc.dram_tensor(f"p1_wp_{n}", [128, DC * D], BF16, kind="Internal").ap()
             for n in ("wq", "wk", "wv", "wo")}

    ident16_c = _const(nc, C, "ident16", np.eye(128, dtype=BF))
    ident32_c = _const(nc, C, "ident32", np.eye(128, dtype=np.float32))

    with ExitStack() as stack:
        if C is None:
            AR = Arena(stack.enter_context(
                nc.sbuf_tensor("AR", [128, 53160], F32)).ap(), 53160)
        else:
            AR = C.AR
            AR.reset(0)

        id16 = AR.bf16(128, 128)
        id32 = AR.f32(128, 128)
        bcol = AR.f32(128, 24)
        scol = AR.f32(128, 8)
        ones_col = AR.bf16(128, 2)
        epscol = AR.f32(128, 1)
        o_or = AR.block(64)
        ones_row = AR.bf16_at(0, 1, o_or, 128)
        aug1 = AR.block(1024)
        kaug = [AR.bf16_at(0, 2, aug1, 2048), AR.bf16_at(64, 2, aug1, 2048)]
        q2row16 = AR.bf16_at(32, 1, aug1, 2048)
        aug2 = AR.block(1024)
        qaug = [AR.bf16_at(0, 2, aug2, 2048), AR.bf16_at(64, 2, aug2, 2048)]
        rrow = AR.bf16(1, 1024)
        brv16 = AR.bf16(1, 1024)
        sq512 = [AR.bf16(128, 512), AR.bf16(128, 512)]

        wslot = [AR.bf16(128, DC * D), AR.bf16(128, DC * D)]

        Qb = AR.bf16(128, 8 * NSEQ)
        Kb = AR.bf16(128, 8 * NSEQ)

        Vt = AR.bf16(128, NT * HEADS * (HD + 1))

        mC = AR.mark()
        yfm = AR.bf16(128, DC * NSEQ)
        endC_y = AR.mark()
        AR.reset(mC)
        Lbuf = AR.f32(128, 1024)
        Ubuf = AR.f32(128, 1024)
        Pbuf = [AR.bf16(128, 1024), AR.bf16(128, 1024)]
        Rsb = AR.bf16(64, 1024)
        oust = [AR.bf16(64, 1024), AR.bf16(64, 1024)]
        endC_3 = AR.mark()
        AR.reset(mC)
        Oun = AR.bf16(128, 8 * 512)
        x1st = [AR.f32(128, 512), AR.f32(128, 512)]
        xres = [AR.f32(128, 512), AR.f32(128, 512)]
        x1c = AR.bf16(128, 512)
        endC_4 = AR.mark()
        AR.reset(max(endC_y, endC_3, endC_4))

        mD = AR.mark()
        xt = [AR.f32(128, D), AR.f32(128, D)]
        ytile = [AR.bf16(128, D), AR.bf16(128, D)]
        xstage = [AR.f32(128, D), AR.f32(128, D)]
        sqdump = AR.f32(128, D)
        endD_1 = AR.mark()
        AR.reset(mD)
        Mt = AR.bf16(128, NSEQ)
        Rt = AR.bf16(128, NSEQ)
        r1 = AR.block(1024)
        sum16 = AR.bf16_at(0, 1, r1, 2048)
        ss16 = AR.bf16_at(32, 1, r1, 2048)
        m16 = AR.bf16(1, 2048)
        rstd16 = AR.bf16(1, 2048)
        var16 = AR.bf16(1, 2048)
        ssd16 = AR.bf16(1, 2048)
        ylt = [AR.bf16(128, 512), AR.bf16(128, 512)]
        yltmp = AR.f32(128, 512)
        endD_5 = AR.mark()
        AR.reset(max(endD_1, endD_5))

        wtmp32 = xt
        wtmp16 = ytile
        wstg = sqdump.bitcast(BF16)[:, 0:128]

        e_ = stack.enter_context
        if C is None:
            psA = e_(nc.psum_tensor("psA", [128, 1024], F32)).ap()
            psB = e_(nc.psum_tensor("psB", [128, 1024], F32)).ap()
            psO = e_(nc.psum_tensor("psO", [128, 1024], F32)).ap()
            pt16 = e_(nc.psum_tensor("pt16", [128, 2048], BF16)).ap()
        else:
            psA, psB, psO, pt16 = C.psA, C.psB, C.psO, C.pt16

        l_id = S.add("sync", lambda e: e.dma_start(out=id16, in_=ident16_c.ap()),
                     inc="ld", inc_n=16)
        l_id2 = S.add("sync", lambda e: e.dma_start(out=id32, in_=ident32_c.ap()),
                      inc="ld", inc_n=16)
        id_tok = [("_", l_id), ("_", l_id2)]
        S.add("vector", lambda e: e.memset(ones_col[:, 0:1], 1.0), inc="v")
        S.add("vector", lambda e: e.memset(epscol, EPS), inc="v")
        S.add("vector", lambda e: e.memset(ones_row, 1.0), inc="v")
        S.add("vector", lambda e: e.memset(kaug[0], 1.0), inc="v")
        S.add("vector", lambda e: e.memset(kaug[1], 1.0), inc="v")
        S.add("vector", lambda e: e.memset(qaug[0], 1.0), inc="v")
        S.add("vector", lambda e: e.memset(qaug[1], 1.0), inc="v")

        def set_vt_ones(e):
            ap = Vt.rearrange("p (nt h j) -> p nt h j", nt=NT, h=HEADS, j=HD + 1)
            return e.memset(ap[:, :, :, HD:HD + 1], 1.0)
        S.add("vector", set_vt_ones, inc="v")

        wstate = dict(cast_done=[0, 0], tr_done=[0, 0], ps_done=[0, 0], stg_ld=0)

        def emit_wt(w_dram, dst_dram, scale):
            for r in range(DC):
                buf = r % 2
                L = S.add("sync", lambda e, wd=w_dram, r=r, t=wtmp32[buf]:
                          e.dma_start(out=t, in_=wd[r * 128:(r + 1) * 128, :]),
                          wait=[("v", wstate["cast_done"][buf])], inc="ld", inc_n=16)
                if scale == 1.0:
                    cv = S.add("vector", lambda e, i=wtmp32[buf], o=wtmp16[buf]:
                               e.tensor_copy(out=o, in_=i),
                               wait=[("ld", L), ("t", wstate["tr_done"][buf])], inc="v")
                else:
                    cv = S.add("vector", lambda e, i=wtmp32[buf], o=wtmp16[buf], s=scale:
                               e.tensor_scalar_mul(out=o, in0=i, scalar1=s),
                               wait=[("ld", L), ("t", wstate["tr_done"][buf])], inc="v")
                wstate["cast_done"][buf] = cv
                last_t = 0
                for c in range(DC):
                    slot = c % 2
                    dst_ps = pt16[:, slot * 1024: slot * 1024 + 128]
                    tt = S.add("tensor", lambda e, i=wtmp16[buf], c=c, dst=dst_ps:
                               e.transpose(out=dst, in_=i[:, c * 128:(c + 1) * 128],
                                           identity=id16),
                               wait=[("v", cv), ("v", wstate["ps_done"][slot]),
                                     ] + id_tok, inc="t")
                    ev = S.add("vector", lambda e, dst=dst_ps:
                               e.tensor_copy(out=wstg, in_=dst),
                               wait=[("t", tt), ("ld", wstate["stg_ld"])], inc="v")
                    wstate["stg_ld"] = S.add(
                        "sync", lambda e, dd=dst_dram, c=c, r=r:
                        e.dma_start(out=dd[:, c * D + r * 128: c * D + (r + 1) * 128], in_=wstg),
                        wait=[("v", ev)], inc="ld", inc_n=16)
                    wstate["ps_done"][slot] = ev
                    last_t = tt
                wstate["tr_done"][buf] = last_t

        emit_wt(w_dr["wq"], wprep["wq"], -2.0)
        emit_wt(w_dr["wk"], wprep["wk"], 1.0)
        emit_wt(w_dr["wv"], wprep["wv"], 1.0)
        emit_wt(w_dr["wo"], wprep["wo"], 1.0)

        for j, n in enumerate(("bq", "bk", "bo")):
            for c in range(DC):
                S.add("sync", lambda e, bd=b_dr[n], c=c, col=j * 8 + c:
                      e.dma_start(out=bcol[:, col:col + 1],
                                  in_=bd[c * 128:(c + 1) * 128, 0:1]),
                      inc="ld", inc_n=16)
        brv_tok = S.add("gpsimd", lambda e:
              e.dma_start(out=brv16, in_=b_dr["bv"].rearrange("d one -> one d")),
              inc="g", inc_n=16)
        bias_ld_f = S.dma_fence("ld")
        bias_v = S.add("vector", lambda e: e.tensor_scalar_mul(out=bcol[:, 0:8], in0=bcol[:, 0:8],
                                                               scalar1=-2.0),
                       wait=bias_ld_f, inc="v")

        st = dict(
            xt_free_v=[0, 0], xt_free_t=[0, 0],
            ytile_free=[0, 0], xstage_free=[0, 0],
            pt_free=[0, 0],
            psA_v=0, psA_a=0, psB_v=0, psB_a=0, psO_v=0,
            wslot_rd=[0, 0],
            q2row_g=0, q2_g=[0] * HEADS,
            kaug_rd=[0, 0], qaug_rd=[0, 0],
            P_rd=[0, 0],
            sq_rd=[0, 0],
            oust_ld=[0, 0],
            x1st_ld=[0, 0], xres_v=[0, 0],
            ylt_ld=[0, 0],
        )

        for b in range(n_batches):
            S.barrier()
            for k in ("psA_v", "psB_v", "psO_v"):
                st[k] = 0
            st["psA_a"] = st["psB_a"] = 0

            wv_ld = S.add("sync", lambda e: e.dma_start(out=wslot[0], in_=wprep["wv"]),
                          inc="ld", inc_n=16)
            wk_ld = S.add("sync", lambda e: e.dma_start(out=wslot[1], in_=wprep["wk"]),
                          inc="ld", inc_n=16)

            for nt in range(NT):
                i = nt % 2
                L = S.add("sync", lambda e, b=b, nt=nt, t=xt[i]:
                          e.dma_start(out=t, in_=x_in[b, nt * 128:(nt + 1) * 128, :]),
                          wait=[("v", st["xt_free_v"][i]), ("t", st["xt_free_t"][i])],
                          inc="ld", inc_n=16)
                a1 = S.add("scalar", lambda e, t=xt[i]:
                           e.activation(out=sqdump, in_=t, func=AF.Square,
                                        accum_out=scol[:, 1:2]),
                           wait=[("ld", L)], inc="a")
                v1 = S.add("vector", lambda e, t=xt[i]:
                           e.reduce_sum(out=scol[:, 0:1], in_=t, axis=AX.X),
                           wait=[("ld", L)], inc="v")
                v2 = S.add("vector", lambda e:
                           e.tensor_scalar_mul(out=scol[:, 2:3], in0=scol[:, 0:1], scalar1=1.0 / D),
                           wait=[("v", v1)], inc="v")
                S.add("vector", lambda e:
                      e.tensor_scalar_mul(out=scol[:, 3:4], in0=scol[:, 1:2], scalar1=1.0 / D),
                      wait=[("a", a1)], inc="v")
                v4 = S.add("vector", lambda e:
                           e.tensor_mul(out=scol[:, 4:5], in0=scol[:, 2:3], in1=scol[:, 2:3]),
                           wait=[("v", v2)], inc="v")
                v5 = S.add("vector", lambda e:
                           e.tensor_sub(out=scol[:, 5:6], in0=scol[:, 3:4], in1=scol[:, 4:5]),
                           wait=[("v", v4)], inc="v")
                a2 = S.add("scalar", lambda e:
                           e.activation(out=scol[:, 6:7], in_=scol[:, 5:6], func=AF.Ln, bias=epscol),
                           wait=[("v", v5)], inc="a")
                a3 = S.add("scalar", lambda e:
                           e.activation(out=scol[:, 7:8], in_=scol[:, 6:7], func=AF.Exp, scale=-0.5),
                           wait=[("a", a2)], inc="a")
                v6 = S.add("vector", lambda e, t=xt[i], o=ytile[i]:
                           e.tensor_scalar(out=o, in0=t, scalar1=scol[:, 2:3],
                                           scalar2=scol[:, 7:8], op0=ALU.subtract, op1=ALU.mult),
                           wait=[("a", a3), ("t", st["ytile_free"][i])], inc="v")
                last_tt = 0
                for c in range(DC):
                    slot = c % 2
                    dst = pt16[:, slot * 1024: slot * 1024 + 128]
                    tt = S.add("tensor", lambda e, yt=ytile[i], c=c, dst=dst:
                               e.transpose(out=dst, in_=yt[:, c * 128:(c + 1) * 128],
                                           identity=id16),
                               wait=[("v", v6), ("v", st["pt_free"][slot])], inc="t")
                    ev = S.add("vector", lambda e, dst=dst, c=c, nt=nt:
                               e.tensor_copy(out=yfm[:, c * NSEQ + nt * 128: c * NSEQ + (nt + 1) * 128],
                                             in_=dst),
                               wait=[("t", tt)], inc="v")
                    st["pt_free"][slot] = ev
                    last_tt = tt
                st["ytile_free"][i] = last_tt
                last_xtr = 0
                for c in range(DC):
                    if c % 2 == 0:
                        ps, pv, pa = psA, st["psA_v"], st["psA_a"]
                    else:
                        ps, pv, pa = psB, st["psB_v"], st["psB_a"]
                    tt = S.add("tensor", lambda e, t=xt[i], c=c, ps=ps:
                               e.transpose(out=ps[0:128, 0:128], in_=t[:, c * 128:(c + 1) * 128],
                                           identity=id32),
                               wait=[("ld", L), ("v", pv), ("a", pa)], inc="t")
                    ev = S.add("vector", lambda e, ps=ps, c=c, o=xstage[i]:
                               e.tensor_copy(out=o[:, c * 128:(c + 1) * 128], in_=ps[0:128, 0:128]),
                               wait=[("t", tt), ("ld", st["xstage_free"][i])], inc="v")
                    if c % 2 == 0:
                        st["psA_v"] = ev
                    else:
                        st["psB_v"] = ev
                    last_xtr = tt
                st["xt_free_v"][i] = v6
                st["xt_free_t"][i] = last_xtr
                sl = S.add("sync", lambda e, nt=nt, o=xstage[i]:
                           e.dma_start(
                               out=xfm_scr.rearrange("(c p) (nt j) -> p c nt j", p=128, nt=NT)[:, :, nt, :],
                               in_=o.rearrange("p (c j) -> p c j", c=DC)),
                           wait=[("v", S.val("v"))], inc="ld", inc_n=16)
                st["xstage_free"][i] = sl
            yfm_v = S.val("v")

            for nt in range(NT):
                for ob in range(2):
                    par = (nt * 2 + ob) % 2
                    if par == 0:
                        ps, pv, pa = psA, st["psA_v"], st["psA_a"]
                    else:
                        ps, pv, pa = psB, st["psB_v"], st["psB_a"]
                    for ic in range(DC):
                        S.add("tensor", lambda e, nt=nt, ob=ob, ic=ic, ps=ps:
                              e.matmul(out=ps[0:128, 0:512],
                                       lhsT=yfm[:, ic * NSEQ + nt * 128: ic * NSEQ + (nt + 1) * 128],
                                       rhs=wslot[0][:, ic * D + ob * 512: ic * D + (ob + 1) * 512],
                                       start=(ic == 0), stop=False),
                              wait=([("v", yfm_v), ("v", pv), ("a", pa), ("ld", wv_ld)]
                                    if ic == 0 else []),
                              inc="t")
                    mmv = S.add("tensor", lambda e, ob=ob, ps=ps:
                                e.matmul(out=ps[0:128, 0:512], lhsT=ones_row[0:1, :],
                                         rhs=brv16[0:1, ob * 512:(ob + 1) * 512],
                                         start=False, stop=True),
                                wait=[("_", brv_tok)], inc="t")
                    ev = S.add("vector", lambda e, nt=nt, ob=ob, ps=ps:
                               e.tensor_copy(
                                   out=Vt[:, (nt * HEADS + ob * 8) * 65: (nt * HEADS + ob * 8 + 8) * 65]
                                   .rearrange("p (h j) -> p h j", j=65)[:, :, 0:64],
                                   in_=ps[0:128, 0:512].rearrange("p (h j) -> p h j", j=64)),
                               wait=[("t", mmv)], inc="v")
                    if par == 0:
                        st["psA_v"] = ev
                    else:
                        st["psB_v"] = ev
            vproj_t = S.val("t")

            wq_ld = S.add("sync", lambda e: e.dma_start(out=wslot[0], in_=wprep["wq"]),
                          wait=[("t", vproj_t)], inc="ld", inc_n=16)

            for p in range(8):
                for (widx, wld, dstb, bcol0) in ((0, wq_ld, Qb, 0), (1, wk_ld, Kb, 8)):
                    for nb in range(4):
                        par = nb % 2
                        if par == 0:
                            ps, pv, pa = psA, st["psA_v"], st["psA_a"]
                        else:
                            ps, pv, pa = psB, st["psB_v"], st["psB_a"]
                        for ic in range(DC):
                            S.add("tensor", lambda e, p=p, nb=nb, ic=ic, ps=ps, widx=widx:
                                  e.matmul(out=ps[0:128, 0:512],
                                           lhsT=wslot[widx][:, ic * D + p * 128: ic * D + (p + 1) * 128],
                                           rhs=yfm[:, ic * NSEQ + nb * 512: ic * NSEQ + (nb + 1) * 512],
                                           start=(ic == 0), stop=(ic == DC - 1)),
                                  wait=([("v", yfm_v), ("v", pv), ("a", pa), ("ld", wld)]
                                        if ic == 0 else []),
                                  inc="t")
                        mm = S.val("t")
                        ev = S.add("vector", lambda e, p=p, nb=nb, ps=ps, dstb=dstb, col=bcol0 + p:
                                   e.tensor_scalar_add(
                                       out=dstb[:, p * NSEQ + nb * 512: p * NSEQ + (nb + 1) * 512],
                                       in0=ps[0:128, 0:512], scalar1=bcol[:, col:col + 1]),
                                   wait=[("t", mm), ("v", bias_v)], inc="v")
                        if par == 0:
                            st["psA_v"] = ev
                        else:
                            st["psB_v"] = ev
            qk_v = S.val("v")
            wo_ld = S.add("sync", lambda e: e.dma_start(out=wslot[1], in_=wprep["wo"]),
                          wait=[("t", S.val("t"))], inc="ld", inc_n=16)

            for h in range(HEADS):
                p, hh = h // 2, h % 2
                base = hh * 64
                for (src, scl, scr_t) in ((Kb, 1.0, k2_scr), (Qb, 0.25, q2_scr)):
                    for nb in range(4):
                        sqi = nb % 2
                        sv = S.add("vector", lambda e, p=p, base=base, nb=nb, sqi=sqi, src=src:
                                   e.tensor_mul(out=sq512[sqi][0:64, :],
                                                in0=src[base:base + 64,
                                                        p * NSEQ + nb * 512: p * NSEQ + (nb + 1) * 512],
                                                in1=src[base:base + 64,
                                                        p * NSEQ + nb * 512: p * NSEQ + (nb + 1) * 512]),
                                   wait=[("t", st["sq_rd"][sqi])], inc="v")
                        mm = S.add("tensor", lambda e, nb=nb, sqi=sqi:
                                   e.matmul(out=psO[0:1, (nb % 2) * 512:(nb % 2) * 512 + 512],
                                            lhsT=ones_col[0:64, 0:1], rhs=sq512[sqi][0:64, :],
                                            start=True, stop=True),
                                   wait=[("v", sv), ("v", st["psO_v"])], inc="t")
                        st["sq_rd"][sqi] = mm
                        st["psO_v"] = S.add(
                            "vector", lambda e, nb=nb, scl=scl:
                            e.tensor_scalar_mul(out=q2row16[0:1, nb * 512:(nb + 1) * 512],
                                                in0=psO[0:1, (nb % 2) * 512:(nb % 2) * 512 + 512],
                                                scalar1=scl),
                            wait=[("t", mm)] + ([("g", st["q2row_g"])] if nb == 0 else []),
                            inc="v")
                    st["q2row_g"] = S.add("gpsimd", lambda e, h=h, scr_t=scr_t:
                                          e.dma_start(out=scr_t[h:h + 1, :], in_=q2row16),
                                          wait=[("v", st["psO_v"])], inc="g", inc_n=16)
            aug_rows_f = S.dma_fence("g")

            def emit_aug(h):
                i = h % 2
                g1 = S.add("gpsimd", lambda e, h=h, i=i:
                           e.dma_start(out=kaug[i][0:1, :], in_=k2_scr[h:h + 1, :]),
                           wait=aug_rows_f + [("t", st["kaug_rd"][i])],
                           inc="g", inc_n=16)
                g2 = S.add("gpsimd", lambda e, h=h, i=i:
                           e.dma_start(out=qaug[i][1:2, :], in_=q2_scr[h:h + 1, :]),
                           wait=[("t", st["qaug_rd"][i])], inc="g", inc_n=16)
                st["q2_g"][h] = g2

            emit_aug(0)
            for h in range(HEADS):
                i = h % 2
                p, hh = h // 2, h % 2
                base = hh * 64
                if h + 1 < HEADS:
                    emit_aug(h + 1)
                aug_g = st["q2_g"][h]
                for qh in range(2):
                    pvmm = 0

                    def emit_pv(kt, pb, aexp):
                        r = 0
                        for half in range(2):
                            r = S.add("tensor", lambda e, kt=kt, h=h, half=half, pb=pb:
                                      e.matmul(out=psO[0:65, half * 512:(half + 1) * 512],
                                               lhsT=Vt[:, (kt * HEADS + h) * 65:
                                                       (kt * HEADS + h + 1) * 65],
                                               rhs=Pbuf[pb][:, half * 512:(half + 1) * 512],
                                               start=(kt == 0), stop=(kt == NT - 1)),
                                      wait=([("a", aexp)] if half == 0 else []) +
                                           ([("v", st["psO_v"])] if (kt == 0 and half == 0) else []),
                                      inc="t")
                        st["P_rd"][pb] = r
                        return r

                    pend = None
                    for kt in range(NT):
                        par = kt % 2
                        if par == 0:
                            ps, pv, pa = psA, st["psA_v"], st["psA_a"]
                        else:
                            ps, pv, pa = psB, st["psB_v"], st["psB_a"]
                        last_mm2 = 0
                        for half in range(2):
                            S.add("tensor", lambda e, p=p, base=base, kt=kt, qh=qh, half=half, ps=ps:
                                  e.matmul(out=ps[0:128, half * 512:(half + 1) * 512],
                                           lhsT=Kb[base:base + 64,
                                                   p * NSEQ + kt * 128: p * NSEQ + (kt + 1) * 128],
                                           rhs=Qb[base:base + 64,
                                                  p * NSEQ + qh * 1024 + half * 512:
                                                  p * NSEQ + qh * 1024 + (half + 1) * 512],
                                           start=True, stop=False),
                                  wait=([("v", pv), ("a", pa), ("v", qk_v)] if half == 0 else []),
                                  inc="t")
                            last_mm2 = S.add(
                                "tensor", lambda e, kt=kt, qh=qh, half=half, ps=ps, i=i:
                                e.matmul(out=ps[0:128, half * 512:(half + 1) * 512],
                                         lhsT=kaug[i][0:2, kt * 128:(kt + 1) * 128],
                                         rhs=qaug[i][0:2, qh * 1024 + half * 512:
                                                     qh * 1024 + (half + 1) * 512],
                                         start=False, stop=True),
                                wait=([("g", aug_g)] if (kt == 0 and half == 0) else []),
                                inc="t")
                        st["kaug_rd"][i] = last_mm2
                        st["qaug_rd"][i] = last_mm2
                        asq = S.add("scalar", lambda e, ps=ps:
                                    e.activation(out=Ubuf, in_=ps[0:128, 0:1024], func=AF.Sqrt),
                                    wait=[("t", last_mm2)], inc="a")
                        if par == 0:
                            st["psA_a"] = asq
                        else:
                            st["psB_a"] = asq
                        pb = kt % 2
                        aexp = S.add("scalar", lambda e, pb=pb:
                                     e.activation(out=Pbuf[pb], in_=Ubuf, func=AF.Exp, scale=-1.0),
                                     wait=[("t", st["P_rd"][pb])], inc="a")
                        if pend is not None:
                            emit_pv(*pend)
                        pend = (kt, pb, aexp)
                    pvmm = emit_pv(*pend)
                    def recip_fn(e):
                        with nc.allow_low_precision(reason="softmax denom recip to bf16 "
                                                    "(same grade as bf16 matmul operand cast)"):
                            return e.reciprocal(out=rrow, in_=psO[64:65, 0:1024])
                    vr = S.add("vector", recip_fn, wait=[("t", pvmm)], inc="v")
                    rmm = S.add("tensor", lambda e:
                                e.matmul(out=psA[0:64, 0:512], lhsT=ones_row[0:1, 0:64],
                                         rhs=rrow[0:1, 0:512], start=True, stop=True),
                                wait=[("v", vr), ("a", st["psA_a"]), ("v", st["psA_v"])],
                                inc="t")
                    rmm2 = S.add("tensor", lambda e:
                                 e.matmul(out=psA[0:64, 512:1024], lhsT=ones_row[0:1, 0:64],
                                          rhs=rrow[0:1, 512:1024], start=True, stop=True),
                                 inc="t")
                    vR = S.add("vector", lambda e:
                               e.tensor_copy(out=Rsb, in_=psA[0:64, 0:1024]),
                               wait=[("t", rmm2)], inc="v")
                    st["psA_v"] = vR
                    oi = qh
                    ouev = S.add("vector", lambda e, oi=oi:
                                 e.tensor_mul(out=oust[oi], in0=psO[0:64, 0:1024], in1=Rsb),
                                 wait=[("v", vR), ("ld", st["oust_ld"][oi])], inc="v")
                    st["psO_v"] = ouev
                    st["oust_ld"][oi] = S.add(
                        "sync", lambda e, h=h, qh=qh, oi=oi:
                        e.dma_start(out=ou_scr[h * 64:(h + 1) * 64, qh * 1024:(qh + 1) * 1024],
                                    in_=oust[oi]),
                        wait=[("v", ouev)], inc="ld", inc_n=16)

            ou_done_f = S.dma_fence("ld")
            for nb in range(4):
                lo = S.add("sync", lambda e, nb=nb:
                           e.dma_start(out=Oun.rearrange("p (pr j) -> p pr j", pr=8),
                                       in_=ou_scr.rearrange("(pr p) (nb j) -> p pr nb j",
                                                            p=128, nb=4)[:, :, nb, :]),
                           wait=ou_done_f + [("t", S.val("t"))], inc="ld", inc_n=16)
                for g2 in range(2):
                    for pr in range(8):
                        for oo in range(4):
                            oc = g2 * 4 + oo
                            ps = (psA, psB)[oo // 2]
                            half = oo % 2
                            S.add("tensor",
                                  lambda e, pr=pr, oc=oc, nb=nb, ps=ps, half=half:
                                  e.matmul(out=ps[0:128, half * 512:(half + 1) * 512],
                                           lhsT=wslot[1][:, pr * D + oc * 128: pr * D + (oc + 1) * 128],
                                           rhs=Oun[:, pr * 512:(pr + 1) * 512],
                                           start=(pr == 0), stop=(pr == 7)),
                                  wait=([("ld", lo), ("ld", wo_ld),
                                         ("v", st["psA_v"] if oo // 2 == 0 else st["psB_v"]),
                                         ("a", st["psA_a"] if oo // 2 == 0 else st["psB_a"])]
                                        if pr == 0 and half == 0 else []),
                                  inc="t")
                    mm_g = S.val("t")
                    for oo in range(4):
                        oc = g2 * 4 + oo
                        ps = (psA, psB)[oo // 2]
                        half = oo % 2
                        i = oc % 2
                        lx = S.add("sync", lambda e, oc=oc, nb=nb, i=i:
                                   e.dma_start(out=xres[i],
                                               in_=xfm_scr[oc * 128:(oc + 1) * 128,
                                                           nb * 512:(nb + 1) * 512]),
                                   wait=[("v", st["xres_v"][i])], inc="ld", inc_n=16)
                        ev = S.add("vector", lambda e, oc=oc, i=i, ps=ps, half=half:
                                   e.scalar_tensor_tensor(out=x1st[i], in0=ps[0:128, half * 512:(half + 1) * 512],
                                                          scalar=bcol[:, 16 + oc:17 + oc],
                                                          in1=xres[i],
                                                          op0=ALU.add, op1=ALU.add),
                                   wait=[("t", mm_g), ("ld", lx), ("ld", st["x1st_ld"][i])],
                                   inc="v")
                        st["xres_v"][i] = ev
                        if oo // 2 == 0:
                            st["psA_v"] = ev
                        else:
                            st["psB_v"] = ev
                        sl = S.add("sync", lambda e, b=b, oc=oc, nb=nb, i=i:
                                   e.dma_start(out=x1_out[b, oc * 128:(oc + 1) * 128,
                                                          nb * 512:(nb + 1) * 512],
                                               in_=x1st[i]),
                                   wait=[("v", ev)], inc="ld", inc_n=16)
                        st["x1st_ld"][i] = sl
                        cv = S.add("vector", lambda e, i=i:
                                   e.tensor_copy(out=x1c, in_=x1st[i]),
                                   wait=[("t", st.get("x1c_rd", 0))], inc="v")
                        mm1 = S.add("tensor", lambda e, nb=nb, oc=oc:
                                    e.matmul(out=psO[0:1, 0:512], lhsT=ones_col[0:128, 0:1],
                                             rhs=x1c, start=(oc == 0), stop=(oc == 7)),
                                    wait=[("v", cv)] +
                                         ([("v", st["psO_v"])] if oc == 0 else []),
                                    inc="t")
                        sv = S.add("vector", lambda e:
                                   e.tensor_mul(out=sq512[0][0:128, :], in0=x1c, in1=x1c),
                                   wait=[("t", st["sq_rd"][0])], inc="v")
                        mm2 = S.add("tensor", lambda e, nb=nb, oc=oc:
                                    e.matmul(out=psO[0:1, 512:1024], lhsT=ones_col[0:128, 0:1],
                                             rhs=sq512[0][0:128, :], start=(oc == 0), stop=(oc == 7)),
                                    wait=[("v", sv)], inc="t")
                        st["sq_rd"][0] = mm2
                        st["x1c_rd"] = mm2
                s_ev = S.add("vector", lambda e, nb=nb:
                             e.tensor_copy(out=sum16[0:1, nb * 512:(nb + 1) * 512],
                                           in_=psO[0:1, 0:512]),
                             wait=[("t", S.val("t"))], inc="v")
                st["psO_v"] = S.add("vector", lambda e, nb=nb:
                                    e.tensor_copy(out=ss16[0:1, nb * 512:(nb + 1) * 512],
                                                  in_=psO[0:1, 512:1024]),
                                    inc="v")

            rv1 = S.add("vector", lambda e: e.tensor_scalar_mul(out=m16, in0=sum16,
                                                                scalar1=1.0 / D),
                        wait=[("v", S.val("v"))], inc="v")
            rv2 = S.add("vector", lambda e: e.tensor_mul(out=var16, in0=m16, in1=m16),
                        wait=[("v", rv1)], inc="v")
            rv3 = S.add("vector", lambda e: e.tensor_scalar_mul(out=ssd16, in0=ss16,
                                                                scalar1=1.0 / D),
                        inc="v")
            vv = S.add("vector", lambda e: e.tensor_sub(out=var16, in0=ssd16, in1=var16),
                       wait=[("v", rv3), ("v", rv2)], inc="v")
            aL = S.add("scalar", lambda e: e.activation(out=ssd16, in_=var16, func=AF.Ln,
                                                        bias=epscol[0:1, 0:1]),
                       wait=[("v", vv)], inc="a")
            ar_ = S.add("scalar", lambda e: e.activation(out=rstd16, in_=ssd16, func=AF.Exp,
                                                         scale=-0.5), wait=[("a", aL)], inc="a")
            for nb in range(4):
                par = nb % 2
                ps = (psA, psB)[par]
                pv = st["psA_v"] if par == 0 else st["psB_v"]
                pa = st["psA_a"] if par == 0 else st["psB_a"]
                m1 = S.add("tensor", lambda e, nb=nb, ps=ps:
                           e.matmul(out=ps[0:128, 0:512], lhsT=ones_row[0:1, :],
                                    rhs=m16[0:1, nb * 512:(nb + 1) * 512], start=True, stop=True),
                           wait=[("a", ar_), ("v", pv), ("a", pa)], inc="t")
                m2 = S.add("tensor", lambda e, nb=nb, ps=ps:
                           e.matmul(out=ps[0:128, 512:1024], lhsT=ones_row[0:1, :],
                                    rhs=rstd16[0:1, nb * 512:(nb + 1) * 512], start=True, stop=True),
                           inc="t")
                ev1 = S.add("vector", lambda e, nb=nb, ps=ps:
                            e.tensor_copy(out=Mt[:, nb * 512:(nb + 1) * 512], in_=ps[0:128, 0:512]),
                            wait=[("t", m2)], inc="v")
                ev2 = S.add("vector", lambda e, nb=nb, ps=ps:
                            e.tensor_copy(out=Rt[:, nb * 512:(nb + 1) * 512], in_=ps[0:128, 512:1024]),
                            inc="v")
                if par == 0:
                    st["psA_v"] = ev2
                else:
                    st["psB_v"] = ev2
            bc_v = S.val("v")
            for oc in range(DC):
                for nb in range(4):
                    k = oc * 4 + nb
                    i = k % 2
                    lx = S.add("sync", lambda e, b=b, oc=oc, nb=nb, i=i:
                               e.dma_start(out=xres[i],
                                           in_=x1_out[b, oc * 128:(oc + 1) * 128,
                                                      nb * 512:(nb + 1) * 512]),
                               wait=[("v", st["xres_v"][i])], inc="ld", inc_n=16)
                    v1_ = S.add("vector", lambda e, i=i, nb=nb:
                                e.tensor_sub(out=yltmp, in0=xres[i],
                                             in1=Mt[:, nb * 512:(nb + 1) * 512]),
                                wait=[("ld", lx), ("v", bc_v)], inc="v")
                    st["xres_v"][i] = v1_
                    v2_ = S.add("vector", lambda e, i=i, nb=nb:
                                e.tensor_mul(out=ylt[i], in0=yltmp,
                                             in1=Rt[:, nb * 512:(nb + 1) * 512]),
                                wait=[("ld", st["ylt_ld"][i])], inc="v")
                    sl = S.add("sync", lambda e, b=b, oc=oc, nb=nb, i=i:
                               e.dma_start(out=y_out[b, oc * 128:(oc + 1) * 128,
                                                     nb * 512:(nb + 1) * 512],
                                           in_=ylt[i]),
                               wait=[("v", v2_)], inc="ld", inc_n=16)
                    st["ylt_ld"][i] = sl

        if C is None:
            S.add("sync", None, wait=S.dma_fence("ld") + S.dma_fence("g"))
            S.emit(nc, stack)
    return nc


def build_p3(nloc=NSEQ // NCORES, nsteps=32, C=None):
    if C is None:
        nc = bass.Bass("TRN2", target_bir_lowering=False, detect_race_conditions=False)
        S = Sched(SEMS)
        y_in = nc.dram_tensor("y_re", [nsteps, D, nloc], BF16, kind="ExternalInput").ap()
        wn = ("wf", "wi", "wc", "uf", "ui", "uc")
        w_dr = {n: nc.dram_tensor(n, [D, D], F32, kind="ExternalInput").ap() for n in wn}
        b_dr = {n: nc.dram_tensor(n, [D, 1], F32, kind="ExternalInput").ap()
                for n in ("bf", "bi", "bc", "in_c")}
        c_out = nc.dram_tensor("c_fin", [D, nloc], F32, kind="ExternalOutput").ap()
        y_flat = None
    else:
        nc, S = C.nc, C.S
        y_in = None
        y_flat = (C.io["y_all"].rearrange("a b c -> (a b c)")
                  .rearrange("(x one) -> x one", one=1))
        w_dr = {n: C.io["l0f_" + n] for n in ("wf", "wi", "wc", "uf", "ui", "uc")}
        b_dr = {n: C.io["l0f_" + n] for n in ("bf", "bi", "bc", "in_c")}
        c_out = C.io["c_mine"]

    ident16_c = _const(nc, C, "ident16", np.eye(128, dtype=BF))
    NG = 24
    NL = nloc

    with ExitStack() as stack:
        if C is None:
            AR = Arena(stack.enter_context(
                nc.sbuf_tensor("AR", [128, 53160], F32)).ap(), 53160)
        else:
            AR = C.AR
            AR.reset(0)
        e_ = stack.enter_context

        id16 = AR.bf16(128, 128)
        bcol = AR.f32(128, NG)
        iccol = AR.f32(128, 8)
        GT = AR.bf16(128, DC * 3 * D)
        UT = AR.bf16(128, DC * 3 * D)
        cbuf = AR.f32(128, DC * NL)
        c16 = AR.bf16(128, DC * NL)
        gates = AR.f32(128, NG * NL)
        tmpc = AR.f32(128, NL)
        tmpd = AR.f32(128, NL)
        yt = [AR.bf16(128, DC * NL), AR.bf16(128, DC * NL)]
        wtmp32 = AR.f32(128, D)
        wtmp16 = AR.bf16(128, D)
        if C is not None:
            pidb = AR.f32(128, 1).bitcast(mybir.dt.uint32)
            iota_i = AR.f32(128, 256).bitcast(mybir.dt.int32)
            idx_i = AR.f32(128, 256).bitcast(mybir.dt.int32)
            pidf = AR.f32(128, 1)
            iotaf = AR.f32(128, 256)
            idxf = AR.f32(128, 256)

        if C is None:
            psG = [e_(nc.psum_tensor(f"psG{i}", [128, 512], F32)).ap() for i in range(4)]
            pt16 = e_(nc.psum_tensor("pt16", [128, 2048], BF16)).ap()
        else:
            psG = [C.psA[:, 0:512], C.psA[:, 512:1024],
                   C.psB[:, 0:512], C.psB[:, 512:1024]]
            pt16 = C.pt16

        l_id = S.add("sync", lambda e: e.dma_start(out=id16, in_=ident16_c.ap()),
                     inc="ld", inc_n=16)
        wstate = dict(cast_done=0, tr_done=0, ps_done=[0, 0])

        def emit_wt(w_dram, dst_sb, col0):
            for r in range(DC):
                L = S.add("sync", lambda e, wd=w_dram, r=r:
                          e.dma_start(out=wtmp32, in_=wd[r * 128:(r + 1) * 128, :]),
                          wait=[("v", wstate["cast_done"])], inc="ld", inc_n=16)
                cv = S.add("vector", lambda e:
                           e.tensor_copy(out=wtmp16, in_=wtmp32),
                           wait=[("_", L), ("t", wstate["tr_done"])], inc="v")
                wstate["cast_done"] = cv
                last_t = 0
                for c in range(DC):
                    slot = c % 2
                    dst_ps = pt16[:, slot * 1024: slot * 1024 + 128]
                    tt = S.add("tensor", lambda e, c=c, dst=dst_ps:
                               e.transpose(out=dst, in_=wtmp16[:, c * 128:(c + 1) * 128],
                                           identity=id16),
                               wait=[("v", cv), ("v", wstate["ps_done"][slot]),
                                     ("_", l_id)], inc="t")
                    ev = S.add("vector", lambda e, dst=dst_ps, o=dst_sb, c=c, r=r, col0=col0:
                               e.tensor_copy(out=o[:, c * 3 * D + col0 + r * 128:
                                                   c * 3 * D + col0 + (r + 1) * 128],
                                             in_=dst),
                               wait=[("t", tt)], inc="v")
                    wstate["ps_done"][slot] = ev
                    last_t = tt
                wstate["tr_done"] = last_t

        emit_wt(w_dr["wf"], GT, 0)
        emit_wt(w_dr["wi"], GT, D)
        emit_wt(w_dr["wc"], GT, 2 * D)
        emit_wt(w_dr["uf"], UT, 0)
        emit_wt(w_dr["ui"], UT, D)
        emit_wt(w_dr["uc"], UT, 2 * D)

        for j, n in enumerate(("bf", "bi", "bc")):
            for c in range(DC):
                S.add("sync", lambda e, bd=b_dr[n], c=c, col=j * 8 + c:
                      e.dma_start(out=bcol[:, col:col + 1],
                                  in_=bd[c * 128:(c + 1) * 128, 0:1]),
                      inc="ld", inc_n=16)
        for c in range(DC):
            S.add("sync", lambda e, c=c:
                  e.dma_start(out=iccol[:, c:c + 1],
                              in_=b_dr["in_c"][c * 128:(c + 1) * 128, 0:1]),
                  inc="ld", inc_n=16)
        pre_f = S.dma_fence("ld")
        ic_a = 0
        for ch in range(DC):
            ic_a = S.add("scalar", lambda e, ch=ch:
                         e.activation(out=cbuf[:, ch * NL:(ch + 1) * NL],
                                      in_=cbuf[:, ch * NL:(ch + 1) * NL],
                                      func=AF.Identity, scale=0.0,
                                      bias=iccol[:, ch:ch + 1]),
                         wait=pre_f if ch == 0 else [], inc="a")
        c16_v = 0
        for ch in range(DC):
            c16_v = S.add("vector", lambda e, ch=ch:
                          e.tensor_copy(out=c16[:, ch * NL:(ch + 1) * NL],
                                        in_=cbuf[:, ch * NL:(ch + 1) * NL]),
                          wait=[("a", ic_a)], inc="v")

        st = dict(yt_rd=[0, 0], ps_a=[0, 0, 0, 0], gates_v=0, ut_mm=0)

        idx_wait = []
        if C is not None:
            Lp = S.add("sync", lambda e: e.dma_start(
                out=pidb[:, 0:1],
                in_=nc.partition_id_tensor.ap().broadcast_to((128, 1))),
                inc="ld", inc_n=16)
            g1 = S.add("gpsimd", lambda e: e.iota(
                iota_i[:, :], pattern=[[128 * NSEQ // NL, 256]], base=0,
                channel_multiplier=NSEQ // NL), inc="gp")
            v1 = S.add("vector", lambda e: e.tensor_copy(out=pidf, in_=pidb[:, 0:1]),
                       wait=[("_", Lp)], inc="v")
            v2 = S.add("vector", lambda e: e.tensor_copy(out=iotaf, in_=iota_i[:, :]),
                       wait=[("gp", g1)], inc="v")
            v3 = S.add("vector", lambda e: e.tensor_scalar(
                out=idxf, in0=iotaf, scalar1=pidf, scalar2=float(NL),
                op0=ALU.add, op1=ALU.mult), wait=[("v", v1), ("v", v2)], inc="v")
            v4 = S.add("vector", lambda e: e.tensor_copy(out=idx_i[:, :], in_=idxf),
                       wait=[("v", v3)], inc="v")
            idx_wait = [("v", v4)] + C.y_ready

        prep_v = S.val("v")
        ytok = [None, None]
        for t in range(nsteps):
            i = t % 2
            if C is None:
                ytok[i] = [S.add("sync", lambda e, t=t, i=i:
                                 e.dma_start(out=yt[i].rearrange("p (c n) -> p c n", c=DC),
                                             in_=y_in[t].rearrange("(c p) n -> p c n", p=128)),
                                 wait=[("t", st["yt_rd"][i])], inc="ld", inc_n=16)]
            else:
                toks = []
                for ch in range(DC):
                    w = ([("t", st["yt_rd"][i])] + idx_wait) if ch == 0 else []
                    k = t * DC + ch
                    toks.append(S.add(
                        "gpsimd", lambda e, k=k, ch=ch, i=i:
                        e.indirect_dma_start(
                            out=yt[i][:, ch * NL:(ch + 1) * NL], out_offset=None,
                            in_=y_flat,
                            in_offset=bass.IndirectOffsetOnAxis(ap=idx_i[:, k:k + 1],
                                                                axis=0)),
                        wait=w, inc="g", inc_n=16))
                ytok[i] = toks
            last_mm = 0
            for g in range(NG):
                ps = psG[g % 4]
                for ic in range(DC):
                    S.add("tensor", lambda e, g=g, ic=ic, ps=ps, i=i:
                          e.matmul(out=ps[0:128, 0:NL],
                                   lhsT=GT[:, ic * 3 * D + g * 128: ic * 3 * D + (g + 1) * 128],
                                   rhs=yt[i][:, ic * NL:(ic + 1) * NL],
                                   start=(ic == 0), stop=False),
                          wait=([("_", tk) for tk in ytok[i]] +
                                [("a", st["ps_a"][g % 4]),
                                 ("v", prep_v)] if ic == 0 else []),
                          inc="t")
                for ic in range(DC):
                    last_mm = S.add("tensor", lambda e, g=g, ic=ic, ps=ps:
                                    e.matmul(out=ps[0:128, 0:NL],
                                             lhsT=UT[:, ic * 3 * D + g * 128: ic * 3 * D + (g + 1) * 128],
                                             rhs=c16[:, ic * NL:(ic + 1) * NL],
                                             start=False, stop=(ic == DC - 1)),
                                    wait=([("v", c16_v)] if ic == 0 else []),
                                    inc="t")
                fn = AF.Tanh if g >= 16 else AF.Sigmoid
                st["ps_a"][g % 4] = S.add(
                    "scalar", lambda e, g=g, ps=ps, fn=fn:
                    e.activation(out=gates[:, g * NL:(g + 1) * NL], in_=ps[0:128, 0:NL],
                                 func=fn, bias=bcol[:, g:g + 1]),
                    wait=[("t", last_mm), ("v", st["gates_v"])], inc="a")
            st["yt_rd"][i] = last_mm
            st["ut_mm"] = last_mm
            gate_a = S.val("a")
            for ch in range(DC):
                S.add("vector", lambda e, ch=ch:
                      e.tensor_mul(out=tmpc, in0=gates[:, ch * NL:(ch + 1) * NL],
                                   in1=cbuf[:, ch * NL:(ch + 1) * NL]),
                      wait=[("a", gate_a)] if ch == 0 else [], inc="v")
                td = S.add("vector", lambda e, ch=ch:
                           e.tensor_mul(out=tmpd, in0=gates[:, (8 + ch) * NL:(9 + ch) * NL],
                                        in1=gates[:, (16 + ch) * NL:(17 + ch) * NL]),
                           inc="v")
                ta = S.add("vector", lambda e, ch=ch:
                           e.tensor_add(out=cbuf[:, ch * NL:(ch + 1) * NL], in0=tmpc, in1=tmpd),
                           wait=[("v", td)], inc="v")
                c16_v = S.add("vector", lambda e, ch=ch:
                              e.tensor_copy(out=c16[:, ch * NL:(ch + 1) * NL],
                                            in_=cbuf[:, ch * NL:(ch + 1) * NL]),
                              wait=([("t", st["ut_mm"])] if ch == 0 else []) + [("v", ta)],
                              inc="v")
            st["gates_v"] = S.val("v")

        fin = S.add("sync", lambda e:
                    e.dma_start(out=c_out.rearrange("(c p) n -> p c n", p=128),
                                in_=cbuf.rearrange("p (c n) -> p c n", c=DC)),
                    wait=[("v", S.val("v"))], inc="ld", inc_n=16)
        if C is None:
            S.add("sync", None, wait=S.dma_fence("ld"))
            S.emit(nc, stack)
    return nc


def build_p4(n_batches=NB, C=None):
    if C is None:
        nc = bass.Bass("TRN2", target_bir_lowering=False, detect_race_conditions=False)
        S = Sched(SEMS)
        x1_in = nc.dram_tensor("x1_fm", [n_batches, D, NSEQ], F32, kind="ExternalInput").ap()
        y_in = nc.dram_tensor("yL_fm", [n_batches, D, NSEQ], BF16, kind="ExternalInput").ap()
        c_in = nc.dram_tensor("c_full", [D, NSEQ], F32, kind="ExternalInput").ap()
        wo_dr = nc.dram_tensor("wo", [D, D], F32, kind="ExternalInput").ap()
        uo_dr = nc.dram_tensor("uo", [D, D], F32, kind="ExternalInput").ap()
        bo_dr = nc.dram_tensor("bo", [D, 1], F32, kind="ExternalInput").ap()
        x2_out = nc.dram_tensor("x2_fm", [n_batches, D, NSEQ], F32, kind="ExternalOutput").ap()
        c_all = None
    else:
        nc, S = C.nc, C.S
        x1_in = C.io["x1_fm"]
        y_in = C.io["yL_fm"]
        c_in = None
        c_all = C.io["c_all"]
        wo_dr = C.io["l0f_wo"]
        uo_dr = C.io["l0f_uo"]
        bo_dr = C.io["l0f_bo"]
        x2_out = C.io["x2_fm"]

    ident16_c = _const(nc, C, "ident16", np.eye(128, dtype=BF))

    with ExitStack() as stack:
        if C is None:
            AR = Arena(stack.enter_context(
                nc.sbuf_tensor("AR", [128, 53160], F32)).ap(), 53160)
        else:
            AR = C.AR
            AR.reset(0)
        e_ = stack.enter_context

        id16 = AR.bf16(128, 128)
        bocol = AR.f32(128, 8)
        WoT = AR.bf16(128, DC * D)
        UoT = AR.bf16(128, DC * D)
        c16f = AR.bf16(128, DC * NSEQ)
        CU = AR.f32(128, DC * NSEQ)
        yb = AR.bf16(128, DC * NSEQ)
        ctmp = AR.f32(128, 512)
        x1t = [AR.f32(128, 512), AR.f32(128, 512)]
        ht_in = [AR.f32(128, 512), AR.f32(128, 512)]
        ht_out = [AR.f32(128, 512), AR.f32(128, 512)]
        x2t = [AR.f32(128, 512), AR.f32(128, 512)]
        wtmp32 = AR.f32(128, D)
        wtmp16 = AR.bf16(128, D)

        if C is None:
            psA = e_(nc.psum_tensor("psA", [128, 1024], F32)).ap()
            psB = e_(nc.psum_tensor("psB", [128, 1024], F32)).ap()
            pt16 = e_(nc.psum_tensor("pt16", [128, 2048], BF16)).ap()
        else:
            psA, psB, pt16 = C.psA, C.psB, C.pt16

        l_id = S.add("sync", lambda e: e.dma_start(out=id16, in_=ident16_c.ap()),
                     inc="ld", inc_n=16)
        wstate = dict(cast_done=0, tr_done=0, ps_done=[0, 0])

        def emit_wt(w_dram, dst_sb):
            for r in range(DC):
                L = S.add("sync", lambda e, wd=w_dram, r=r:
                          e.dma_start(out=wtmp32, in_=wd[r * 128:(r + 1) * 128, :]),
                          wait=[("v", wstate["cast_done"])], inc="ld", inc_n=16)
                cv = S.add("vector", lambda e:
                           e.tensor_copy(out=wtmp16, in_=wtmp32),
                           wait=[("_", L), ("t", wstate["tr_done"])], inc="v")
                wstate["cast_done"] = cv
                last_t = 0
                for c in range(DC):
                    slot = c % 2
                    dst_ps = pt16[:, slot * 1024: slot * 1024 + 128]
                    tt = S.add("tensor", lambda e, c=c, dst=dst_ps:
                               e.transpose(out=dst, in_=wtmp16[:, c * 128:(c + 1) * 128],
                                           identity=id16),
                               wait=[("v", cv), ("v", wstate["ps_done"][slot]),
                                     ("_", l_id)], inc="t")
                    ev = S.add("vector", lambda e, dst=dst_ps, o=dst_sb, c=c, r=r:
                               e.tensor_copy(out=o[:, c * D + r * 128: c * D + (r + 1) * 128],
                                             in_=dst),
                               wait=[("t", tt)], inc="v")
                    wstate["ps_done"][slot] = ev
                    last_t = tt
                wstate["tr_done"] = last_t

        emit_wt(wo_dr, WoT)
        emit_wt(uo_dr, UoT)
        for c in range(DC):
            S.add("sync", lambda e, c=c:
                  e.dma_start(out=bocol[:, c:c + 1], in_=bo_dr[c * 128:(c + 1) * 128, 0:1]),
                  inc="ld", inc_n=16)
        for c in range(DC):
            for nb in range(4):
                if C is None:
                    ltoks = [S.add("sync", lambda e, c=c, nb=nb:
                                   e.dma_start(out=ctmp,
                                               in_=c_in[c * 128:(c + 1) * 128,
                                                        nb * 512:(nb + 1) * 512]),
                                   wait=[("v", wstate.get("ctmp_v", 0))],
                                   inc="ld", inc_n=16)]
                else:
                    ltoks = [S.add("sync", lambda e, c=c, r=2 * nb:
                                   e.dma_start(out=ctmp[:, 0:256],
                                               in_=c_all[r, c * 128:(c + 1) * 128, :]),
                                   wait=[("v", wstate.get("ctmp_v", 0))] + C.c_ready,
                                   inc="ld", inc_n=16),
                             S.add("sync", lambda e, c=c, r=2 * nb + 1:
                                   e.dma_start(out=ctmp[:, 256:512],
                                               in_=c_all[r, c * 128:(c + 1) * 128, :]),
                                   inc="ld", inc_n=16)]
                wstate["ctmp_v"] = S.add("vector", lambda e, c=c, nb=nb:
                                         e.tensor_copy(out=c16f[:, c * NSEQ + nb * 512:
                                                                c * NSEQ + (nb + 1) * 512],
                                                       in_=ctmp),
                                         wait=[("_", t) for t in ltoks], inc="v")
        c16_v = S.val("v")
        prep_v = S.val("v")

        st = dict(ps_v=[0, 0], ps_a=[0, 0], x1_v=[0, 0], x2_ld=[0, 0], cu_rd=0)
        for oc in range(DC):
            for nb in range(4):
                par = (oc * 4 + nb) % 2
                ps = (psA, psB)[par]
                for ic in range(DC):
                    S.add("tensor", lambda e, oc=oc, nb=nb, ic=ic, ps=ps:
                          e.matmul(out=ps[0:128, 0:512],
                                   lhsT=UoT[:, ic * D + oc * 128: ic * D + (oc + 1) * 128],
                                   rhs=c16f[:, ic * NSEQ + nb * 512: ic * NSEQ + (nb + 1) * 512],
                                   start=(ic == 0), stop=(ic == DC - 1)),
                          wait=([("v", c16_v), ("v", st["ps_v"][par]), ("v", prep_v)]
                                if ic == 0 else []),
                          inc="t")
                mm = S.val("t")
                st["ps_v"][par] = S.add("vector", lambda e, oc=oc, nb=nb, ps=ps:
                                        e.tensor_copy(out=CU[:, oc * NSEQ + nb * 512:
                                                             oc * NSEQ + (nb + 1) * 512],
                                                      in_=ps[0:128, 0:512]),
                                        wait=[("t", mm)], inc="v")
        cu_v = S.val("v")

        for b in range(n_batches):
            ytok = S.add("sync", lambda e, b=b:
                         e.dma_start(out=yb.rearrange("p (c n) -> p c n", c=DC),
                                     in_=y_in[b].rearrange("(c p) n -> p c n", p=128)),
                         wait=[("t", st.get("yb_rd", 0))], inc="ld", inc_n=16)
            for oc in range(DC):
                for nb in range(4):
                    k = oc * 4 + nb
                    par = k % 2
                    i = k % 2
                    ps = (psA, psB)[par]
                    lx = S.add("sync", lambda e, b=b, oc=oc, nb=nb, i=i:
                               e.dma_start(out=x1t[i],
                                           in_=x1_in[b, oc * 128:(oc + 1) * 128,
                                                     nb * 512:(nb + 1) * 512]),
                               wait=[("v", st["x1_v"][i])], inc="ld", inc_n=16)
                    for ic in range(DC):
                        S.add("tensor", lambda e, oc=oc, nb=nb, ic=ic, ps=ps:
                              e.matmul(out=ps[0:128, 0:512],
                                       lhsT=WoT[:, ic * D + oc * 128: ic * D + (oc + 1) * 128],
                                       rhs=yb[:, ic * NSEQ + nb * 512: ic * NSEQ + (nb + 1) * 512],
                                       start=(ic == 0), stop=(ic == DC - 1)),
                              wait=([("_", ytok), ("v", st["ps_v"][par]),
                                     ("a", st["ps_a"][par]), ("v", cu_v)] if ic == 0 else []),
                              inc="t")
                    mm = S.val("t")
                    st["yb_rd"] = mm
                    sv = S.add("vector", lambda e, oc=oc, nb=nb, ps=ps, i=i:
                               e.scalar_tensor_tensor(out=ht_in[i], in0=ps[0:128, 0:512],
                                                      scalar=1.0,
                                                      in1=CU[:, oc * NSEQ + nb * 512:
                                                             oc * NSEQ + (nb + 1) * 512],
                                                      op0=ALU.mult, op1=ALU.add),
                               wait=[("t", mm), ("a", st["ps_a"][par])], inc="v")
                    st["ps_v"][par] = sv
                    av = S.add("scalar", lambda e, oc=oc, i=i:
                               e.activation(out=ht_out[i], in_=ht_in[i], func=AF.Tanh,
                                            bias=bocol[:, oc:oc + 1]),
                               wait=[("v", sv), ("v", st.get("htout_v", [0, 0])[i])], inc="a")
                    st["ps_a"][par] = av
                    x2v = S.add("vector", lambda e, i=i:
                                e.tensor_add(out=x2t[i], in0=ht_out[i], in1=x1t[i]),
                                wait=[("a", av), ("_", lx), ("_", st["x2_ld"][i])], inc="v")
                    st.setdefault("htout_v", [0, 0])[i] = x2v
                    st["x1_v"][i] = x2v
                    st["x2_ld"][i] = S.add("sync", lambda e, b=b, oc=oc, nb=nb, i=i:
                                           e.dma_start(out=x2_out[b, oc * 128:(oc + 1) * 128,
                                                                  nb * 512:(nb + 1) * 512],
                                                       in_=x2t[i]),
                                           wait=[("v", x2v)], inc="ld", inc_n=16)

        if C is None:
            S.add("sync", None, wait=S.dma_fence("ld") + S.dma_fence("g"))
            S.emit(nc, stack)
    return nc


def _rope_tables():
    inv = 10000.0 ** (-np.arange(0, HD, 2, dtype=np.float64) / HD)
    t = np.arange(NSEQ, dtype=np.float64)
    f = np.concatenate([inv, inv])
    ang = t[None, :] * f[:, None]
    cos = np.cos(ang)
    sin = np.sin(ang)
    return cos, sin


def _perm_block():
    P = np.zeros((64, 64), np.float64)
    for a in range(32):
        P[a + 32, a] = -1.0
    for a in range(32, 64):
        P[a - 32, a] = 1.0
    W = np.zeros((128, 128), np.float64)
    W[0:64, 0:64] = P
    W[64:128, 64:128] = P
    return W


def build_p5(n_batches=NB, C=None):
    if C is None:
        nc = bass.Bass("TRN2", target_bir_lowering=False, detect_race_conditions=False)
        S = Sched(SEMS)
        x2_in = nc.dram_tensor("x2_fm", [n_batches, D, NSEQ], F32, kind="ExternalInput").ap()
        w_dr = {n: nc.dram_tensor(n, [D, D], F32, kind="ExternalInput").ap()
                for n in ("wq", "wk", "wv", "wo")}
        b_dr = {n: nc.dram_tensor(n, [D, 1], F32, kind="ExternalInput").ap()
                for n in ("bq", "bk", "bv", "bo")}
        x3_out = nc.dram_tensor("x3_fm", [n_batches, D, NSEQ], F32, kind="ExternalOutput").ap()
    else:
        nc, S = C.nc, C.S
        x2_in = C.io["x2_fm"]
        w_dr = {n: C.io["l1a_" + n] for n in ("wq", "wk", "wv", "wo")}
        b_dr = {n: C.io["l1a_" + n] for n in ("bq", "bk", "bv", "bo")}
        x3_out = C.io["x3_fm"]
    ou_scr = nc.dram_tensor("p5_ou", [D, NSEQ], BF16, kind="Internal").ap()
    wprep = {n: nc.dram_tensor(f"p5_wp_{n}", [128, DC * D], BF16, kind="Internal").ap()
             for n in ("wq", "wk", "wv", "wo")}

    ident16_c = _const(nc, C, "ident16", np.eye(128, dtype=BF))
    cos64, sin64 = _rope_tables()
    cosq_c = _const(nc, C, "cosq", np.vstack([cos64, cos64]).astype(BF))
    sinq_c = _const(nc, C, "sinq", np.vstack([sin64, sin64]).astype(BF))
    perm_c = _const(nc, C, "permT", _perm_block().astype(BF))
    mask = np.zeros((128, 1920), np.float32)
    for p in range(128):
        u = np.arange(1920) - 896
        mask[p] = np.where(u >= p, 0.0, -1e30)
    mask_c = _const(nc, C, "cmask", mask.astype(BF))

    with ExitStack() as stack:
        if C is None:
            AR = Arena(stack.enter_context(
                nc.sbuf_tensor("AR", [128, 53160], F32)).ap(), 53160)
        else:
            AR = C.AR
            AR.reset(0)
        e_ = stack.enter_context

        id16 = AR.bf16(128, 128)
        permT = AR.bf16(128, 128)
        COSQ = AR.bf16(128, NSEQ)
        SINQ = AR.bf16(128, NSEQ)
        MSK = AR.bf16(128, 1920)
        bcol = AR.f32(128, 24)
        epscol = AR.f32(128, 1)
        ones_col = AR.bf16(128, 2)
        o_or = AR.block(64)
        ones_row = AR.bf16_at(0, 1, o_or, 128)
        rrow = AR.bf16(1, 1024)
        brv16 = AR.bf16(1, 1024)
        sq512 = [AR.bf16(128, 512)]
        wslot = [AR.bf16(128, DC * D), AR.bf16(128, DC * D)]
        mQ = AR.mark()
        wtmp32 = AR.f32(128, D)
        wtmp16 = AR.bf16(128, D)
        AR.reset(mQ)
        Qb = AR.bf16(128, 8 * NSEQ)
        Kb = AR.bf16(128, 8 * NSEQ)
        Vt = AR.bf16(128, NT * HEADS * (HD + 1))
        mC = AR.mark()
        y2 = AR.bf16(128, DC * NSEQ)
        endC_y = AR.mark()
        AR.reset(mC)
        Lbuf = AR.f32(128, 1024)
        Pbuf = [AR.bf16(128, 1024), AR.bf16(128, 1024)]
        Rsb = AR.bf16(64, 1024)
        oust = [AR.bf16(64, 1024), AR.bf16(64, 1024)]
        Oun = AR.bf16(128, 8 * 512)
        tmp1 = AR.bf16(128, 512)
        tmp2 = AR.bf16(128, 512)
        endC_3 = AR.mark()
        AR.reset(max(endC_y, endC_3))
        Mt = AR.bf16(128, NSEQ)
        Rt = AR.bf16(128, NSEQ)
        r1 = AR.block(1024)
        sum16 = AR.bf16_at(0, 1, r1, 2048)
        ss16 = AR.bf16_at(32, 1, r1, 2048)
        rm = AR.block(1024)
        m16 = AR.bf16_at(0, 1, rm, 2048)
        var16 = AR.bf16_at(32, 1, rm, 2048)
        rr = AR.block(1024)
        rstd16 = AR.bf16_at(0, 1, rr, 2048)
        ssd16 = AR.bf16_at(32, 1, rr, 2048)
        x2t = [AR.f32(128, 512), AR.f32(128, 512)]
        x2c = AR.bf16(128, 512)
        x3st = [AR.f32(128, 512), AR.f32(128, 512)]
        ytmp = x3st[0]

        if C is None:
            psA = e_(nc.psum_tensor("psA", [128, 1024], F32)).ap()
            psB = e_(nc.psum_tensor("psB", [128, 1024], F32)).ap()
            psO = e_(nc.psum_tensor("psO", [128, 1024], F32)).ap()
            pt16 = e_(nc.psum_tensor("pt16", [128, 2048], BF16)).ap()
        else:
            psA, psB, psO, pt16 = C.psA, C.psB, C.psO, C.pt16

        toks = []
        for name, sb_, c_ in (("id", id16, ident16_c), ("pm", permT, perm_c),
                              ("cq", COSQ, cosq_c), ("sq", SINQ, sinq_c),
                              ("mk", MSK, mask_c)):
            toks.append(S.add("sync", lambda e, sb_=sb_, c_=c_:
                              e.dma_start(out=sb_, in_=c_.ap()), inc="ld", inc_n=16))
        l_id = toks[0]
        tab_toks = [("_", t) for t in toks]
        S.add("vector", lambda e: e.memset(ones_col[:, 0:1], 1.0), inc="v")
        S.add("vector", lambda e: e.memset(epscol, EPS), inc="v")
        S.add("vector", lambda e: e.memset(ones_row, 1.0), inc="v")

        def set_vt_ones(e):
            ap = Vt.rearrange("p (nt h j) -> p nt h j", nt=NT, h=HEADS, j=HD + 1)
            return e.memset(ap[:, :, :, HD:HD + 1], 1.0)
        S.add("vector", set_vt_ones, inc="v")

        wstate = dict(cast_done=0, tr_done=0, ps_done=[0, 0], stg_ld=0)
        wstg = Lbuf.bitcast(BF16)[:, 0:128]

        def emit_wt(w_dram, dst_dram):
            for r in range(DC):
                L = S.add("sync", lambda e, wd=w_dram, r=r:
                          e.dma_start(out=wtmp32, in_=wd[r * 128:(r + 1) * 128, :]),
                          wait=[("v", wstate["cast_done"])], inc="ld", inc_n=16)
                cv = S.add("vector", lambda e:
                           e.tensor_copy(out=wtmp16, in_=wtmp32),
                           wait=[("_", L), ("t", wstate["tr_done"])], inc="v")
                wstate["cast_done"] = cv
                last_t = 0
                for c in range(DC):
                    slot = c % 2
                    dst_ps = pt16[:, slot * 1024: slot * 1024 + 128]
                    tt = S.add("tensor", lambda e, c=c, dst=dst_ps:
                               e.transpose(out=dst, in_=wtmp16[:, c * 128:(c + 1) * 128],
                                           identity=id16),
                               wait=[("v", cv), ("v", wstate["ps_done"][slot]),
                                     ("_", l_id)], inc="t")
                    ev = S.add("vector", lambda e, dst=dst_ps:
                               e.tensor_copy(out=wstg, in_=dst),
                               wait=[("t", tt), ("_", wstate["stg_ld"])], inc="v")
                    wstate["stg_ld"] = S.add(
                        "sync", lambda e, dd=dst_dram, c=c, r=r:
                        e.dma_start(out=dd[:, c * D + r * 128: c * D + (r + 1) * 128], in_=wstg),
                        wait=[("v", ev)], inc="ld", inc_n=16)
                    wstate["ps_done"][slot] = ev
                    last_t = tt
                wstate["tr_done"] = last_t

        for n in ("wq", "wk", "wv", "wo"):
            emit_wt(w_dr[n], wprep[n])
        for j, n in enumerate(("bq", "bk", "bo")):
            for c in range(DC):
                S.add("sync", lambda e, bd=b_dr[n], c=c, col=j * 8 + c:
                      e.dma_start(out=bcol[:, col:col + 1],
                                  in_=bd[c * 128:(c + 1) * 128, 0:1]),
                      inc="ld", inc_n=16)
        brv_tok = S.add("gpsimd", lambda e:
                        e.dma_start(out=brv16, in_=b_dr["bv"].rearrange("d one -> one d")),
                        inc="g", inc_n=16)
        bias_f = S.dma_fence("ld")
        prep_v = S.val("v")

        st = dict(
            psA_v=0, psA_a=0, psB_v=0, psB_a=0, psO_v=0,
            P_rd=[0, 0], sq_rd=[0, 0], oust_ld=[0, 0],
            x2t_v=[0, 0], x3_ld=[0, 0], x2c_rd=0,
            mt_rd=0, y2_rd=0, qk_rd=0, vt_rd=0, tmp_rd=0,
        )

        for b in range(n_batches):
            S.barrier()
            for k in ("psA_v", "psB_v", "psO_v"):
                st[k] = 0
            st["psA_a"] = st["psB_a"] = 0
            wv_ld = S.add("sync", lambda e: e.dma_start(out=wslot[0], in_=wprep["wv"]),
                          inc="ld", inc_n=16)
            wk_ld = S.add("sync", lambda e: e.dma_start(out=wslot[1], in_=wprep["wk"]),
                          inc="ld", inc_n=16)

            for nb in range(4):
                for oc in range(DC):
                    i = oc % 2
                    lx = S.add("sync", lambda e, b=b, oc=oc, nb=nb, i=i:
                               e.dma_start(out=x2t[i],
                                           in_=x2_in[b, oc * 128:(oc + 1) * 128,
                                                     nb * 512:(nb + 1) * 512]),
                               wait=[("v", st["x2t_v"][i])], inc="ld", inc_n=16)
                    cv = S.add("vector", lambda e, i=i:
                               e.tensor_copy(out=x2c, in_=x2t[i]),
                               wait=[("_", lx), ("t", st["x2c_rd"])], inc="v")
                    st["x2t_v"][i] = cv
                    mm1 = S.add("tensor", lambda e, oc=oc:
                                e.matmul(out=psO[0:1, 0:512], lhsT=ones_col[0:128, 0:1],
                                         rhs=x2c, start=(oc == 0), stop=(oc == DC - 1)),
                                wait=[("v", cv)] + ([("v", st["psO_v"])] if oc == 0 else []),
                                inc="t")
                    sv = S.add("vector", lambda e:
                               e.tensor_mul(out=sq512[0][0:128, :], in0=x2c, in1=x2c),
                               wait=[("t", st["sq_rd"][0])], inc="v")
                    mm2 = S.add("tensor", lambda e, oc=oc:
                                e.matmul(out=psO[0:1, 512:1024], lhsT=ones_col[0:128, 0:1],
                                         rhs=sq512[0][0:128, :], start=(oc == 0),
                                         stop=(oc == DC - 1)),
                                wait=[("v", sv)], inc="t")
                    st["sq_rd"][0] = mm2
                    st["x2c_rd"] = mm2
                s_ev = S.add("vector", lambda e, nb=nb:
                             e.tensor_copy(out=sum16[0:1, nb * 512:(nb + 1) * 512],
                                           in_=psO[0:1, 0:512]),
                             wait=[("t", S.val("t"))], inc="v")
                st["psO_v"] = S.add("vector", lambda e, nb=nb:
                                    e.tensor_copy(out=ss16[0:1, nb * 512:(nb + 1) * 512],
                                                  in_=psO[0:1, 512:1024]),
                                    inc="v")
            rv1 = S.add("vector", lambda e: e.tensor_scalar_mul(out=m16, in0=sum16,
                                                                scalar1=1.0 / D),
                        wait=[("v", S.val("v"))], inc="v")
            rv2 = S.add("vector", lambda e: e.tensor_mul(out=var16, in0=m16, in1=m16),
                        wait=[("v", rv1)], inc="v")
            rv3 = S.add("vector", lambda e: e.tensor_scalar_mul(out=ssd16, in0=ss16,
                                                                scalar1=1.0 / D),
                        inc="v")
            vv = S.add("vector", lambda e: e.tensor_sub(out=var16, in0=ssd16, in1=var16),
                       wait=[("v", rv3), ("v", rv2)], inc="v")
            aL = S.add("scalar", lambda e: e.activation(out=ssd16, in_=var16, func=AF.Ln,
                                                        bias=epscol[0:1, 0:1]),
                       wait=[("v", vv)], inc="a")
            ar_ = S.add("scalar", lambda e: e.activation(out=rstd16, in_=ssd16, func=AF.Exp,
                                                         scale=-0.5), wait=[("a", aL)], inc="a")
            for nb in range(4):
                par = nb % 2
                ps = (psA, psB)[par]
                pv = st["psA_v"] if par == 0 else st["psB_v"]
                pa = st["psA_a"] if par == 0 else st["psB_a"]
                m1 = S.add("tensor", lambda e, nb=nb, ps=ps:
                           e.matmul(out=ps[0:128, 0:512], lhsT=ones_row[0:1, :],
                                    rhs=m16[0:1, nb * 512:(nb + 1) * 512], start=True, stop=True),
                           wait=[("a", ar_), ("v", pv), ("a", pa), ("v", st["mt_rd"]),
                                 ("v", prep_v)], inc="t")
                m2 = S.add("tensor", lambda e, nb=nb, ps=ps:
                           e.matmul(out=ps[0:128, 512:1024], lhsT=ones_row[0:1, :],
                                    rhs=rstd16[0:1, nb * 512:(nb + 1) * 512], start=True,
                                    stop=True),
                           inc="t")
                ev1 = S.add("vector", lambda e, nb=nb, ps=ps:
                            e.tensor_copy(out=Mt[:, nb * 512:(nb + 1) * 512], in_=ps[0:128, 0:512]),
                            wait=[("t", m2)], inc="v")
                ev2 = S.add("vector", lambda e, nb=nb, ps=ps:
                            e.tensor_copy(out=Rt[:, nb * 512:(nb + 1) * 512],
                                          in_=ps[0:128, 512:1024]),
                            inc="v")
                if par == 0:
                    st["psA_v"] = ev2
                else:
                    st["psB_v"] = ev2
            bc_v = S.val("v")
            for oc in range(DC):
                for nb in range(4):
                    k = oc * 4 + nb
                    i = k % 2
                    lx = S.add("sync", lambda e, b=b, oc=oc, nb=nb, i=i:
                               e.dma_start(out=x2t[i],
                                           in_=x2_in[b, oc * 128:(oc + 1) * 128,
                                                     nb * 512:(nb + 1) * 512]),
                               wait=[("v", st["x2t_v"][i])], inc="ld", inc_n=16)
                    v1_ = S.add("vector", lambda e, i=i, nb=nb:
                                e.tensor_sub(out=ytmp, in0=x2t[i],
                                             in1=Mt[:, nb * 512:(nb + 1) * 512]),
                                wait=[("_", lx), ("v", bc_v)], inc="v")
                    st["x2t_v"][i] = v1_
                    v2_ = S.add("vector", lambda e, oc=oc, nb=nb:
                                e.tensor_mul(out=y2[:, oc * NSEQ + nb * 512:
                                                    oc * NSEQ + (nb + 1) * 512],
                                             in0=ytmp, in1=Rt[:, nb * 512:(nb + 1) * 512]),
                                wait=[("t", st["y2_rd"])], inc="v")
            st["mt_rd"] = S.val("v")
            y2_v = S.val("v")

            for nt in range(NT):
                for ob in range(2):
                    par = (nt * 2 + ob) % 2
                    ps = (psA, psB)[par]
                    pv = st["psA_v"] if par == 0 else st["psB_v"]
                    pa = st["psA_a"] if par == 0 else st["psB_a"]
                    for ic in range(DC):
                        S.add("tensor", lambda e, nt=nt, ob=ob, ic=ic, ps=ps:
                              e.matmul(out=ps[0:128, 0:512],
                                       lhsT=y2[:, ic * NSEQ + nt * 128: ic * NSEQ + (nt + 1) * 128],
                                       rhs=wslot[0][:, ic * D + ob * 512: ic * D + (ob + 1) * 512],
                                       start=(ic == 0), stop=False),
                              wait=([("v", y2_v), ("v", pv), ("a", pa), ("_", wv_ld)]
                                    if ic == 0 else []),
                              inc="t")
                    mmv = S.add("tensor", lambda e, ob=ob, ps=ps:
                                e.matmul(out=ps[0:128, 0:512], lhsT=ones_row[0:1, :],
                                         rhs=brv16[0:1, ob * 512:(ob + 1) * 512],
                                         start=False, stop=True),
                                wait=[("_", brv_tok)], inc="t")
                    ev = S.add("vector", lambda e, nt=nt, ob=ob, ps=ps:
                               e.tensor_copy(
                                   out=Vt[:, (nt * HEADS + ob * 8) * 65:
                                          (nt * HEADS + ob * 8 + 8) * 65]
                                   .rearrange("p (h j) -> p h j", j=65)[:, :, 0:64],
                                   in_=ps[0:128, 0:512].rearrange("p (h j) -> p h j", j=64)),
                               wait=[("t", mmv), ("t", st["vt_rd"])], inc="v")
                    if par == 0:
                        st["psA_v"] = ev
                    else:
                        st["psB_v"] = ev
            vproj_t = S.val("t")
            wq_ld = S.add("sync", lambda e: e.dma_start(out=wslot[0], in_=wprep["wq"]),
                          wait=[("t", vproj_t)], inc="ld", inc_n=16)

            for p in range(8):
                for (widx, wld, dstb, bcol0, CT, ST, kscale) in (
                        (0, wq_ld, Qb, 0, COSQ, SINQ, None), (1, wk_ld, Kb, 8, COSQ, SINQ, 0.125)):
                    for nb in range(4):
                        par = nb % 2
                        ps = (psA, psB)[par]
                        pv = st["psA_v"] if par == 0 else st["psB_v"]
                        pa = st["psA_a"] if par == 0 else st["psB_a"]
                        for ic in range(DC):
                            S.add("tensor", lambda e, p=p, nb=nb, ic=ic, ps=ps, widx=widx:
                                  e.matmul(out=ps[0:128, 0:512],
                                           lhsT=wslot[widx][:, ic * D + p * 128: ic * D + (p + 1) * 128],
                                           rhs=y2[:, ic * NSEQ + nb * 512: ic * NSEQ + (nb + 1) * 512],
                                           start=(ic == 0), stop=(ic == DC - 1)),
                                  wait=([("v", y2_v), ("v", pv), ("a", pa), ("_", wld)]
                                        if ic == 0 else []),
                                  inc="t")
                        mm = S.val("t")
                        if kscale is None:
                            ev = S.add("vector", lambda e, p=p, nb=nb, ps=ps, dstb=dstb, col=bcol0 + p:
                                       e.tensor_scalar_add(
                                           out=dstb[:, p * NSEQ + nb * 512: p * NSEQ + (nb + 1) * 512],
                                           in0=ps[0:128, 0:512], scalar1=bcol[:, col:col + 1]),
                                       wait=[("t", mm), ("t", st["qk_rd"])], inc="v")
                        else:
                            ev = S.add("vector", lambda e, p=p, nb=nb, ps=ps, dstb=dstb, col=bcol0 + p, ks=kscale:
                                       e.tensor_scalar(
                                           out=dstb[:, p * NSEQ + nb * 512: p * NSEQ + (nb + 1) * 512],
                                           in0=ps[0:128, 0:512], scalar1=bcol[:, col:col + 1],
                                           scalar2=ks, op0=ALU.add, op1=ALU.mult),
                                       wait=[("t", mm), ("t", st["qk_rd"])], inc="v")
                        if par == 0:
                            st["psA_v"] = ev
                        else:
                            st["psB_v"] = ev
                    for nb in range(4):
                        par = nb % 2
                        ps = (psA, psB)[par]
                        pv = st["psA_v"] if par == 0 else st["psB_v"]
                        pa = st["psA_a"] if par == 0 else st["psB_a"]
                        sl = slice(p * NSEQ + nb * 512, p * NSEQ + (nb + 1) * 512)
                        tsl = slice(nb * 512, (nb + 1) * 512)
                        pm = S.add("tensor", lambda e, ps=ps, dstb=dstb, sl=sl:
                                   e.matmul(out=ps[0:128, 0:512], lhsT=permT,
                                            rhs=dstb[:, sl], start=True, stop=True),
                                   wait=[("v", pv), ("a", pa), ("_", l_id)] + tab_toks[:2],
                                   inc="t")
                        t1 = S.add("vector", lambda e, dstb=dstb, sl=sl, tsl=tsl, CT=CT:
                                   e.tensor_mul(out=tmp1, in0=dstb[:, sl], in1=CT[:, tsl]),
                                   wait=[("v", st["tmp_rd"])] + tab_toks, inc="v")
                        t2 = S.add("vector", lambda e, ps=ps, tsl=tsl, ST=ST:
                                   e.tensor_mul(out=tmp2, in0=ps[0:128, 0:512], in1=ST[:, tsl]),
                                   wait=[("t", pm)], inc="v")
                        t3 = S.add("vector", lambda e, dstb=dstb, sl=sl:
                                   e.tensor_add(out=dstb[:, sl], in0=tmp1, in1=tmp2),
                                   inc="v")
                        st["tmp_rd"] = t3
                        if par == 0:
                            st["psA_v"] = t3
                        else:
                            st["psB_v"] = t3
            qk_v = S.val("v")
            wo_ld = S.add("sync", lambda e: e.dma_start(out=wslot[1], in_=wprep["wo"]),
                          wait=[("t", S.val("t"))], inc="ld", inc_n=16)

            for h in range(HEADS):
                p, hh = h // 2, h % 2
                base = hh * 64
                for qh in range(2):
                    kts = list(range(0, qh * 8 + 8))
                    pvmm = 0
                    for kt in kts:
                        diag = kt >= qh * 8
                        par = kt % 2
                        ps = (psA, psB)[par]
                        pv = st["psA_v"] if par == 0 else st["psB_v"]
                        pa = st["psA_a"] if par == 0 else st["psB_a"]
                        last_mm = 0
                        for half in range(2):
                            last_mm = S.add(
                                "tensor", lambda e, p=p, base=base, kt=kt, qh=qh, half=half, ps=ps:
                                e.matmul(out=ps[0:128, half * 512:(half + 1) * 512],
                                         lhsT=Kb[base:base + 64,
                                                 p * NSEQ + kt * 128: p * NSEQ + (kt + 1) * 128],
                                         rhs=Qb[base:base + 64,
                                                p * NSEQ + qh * 1024 + half * 512:
                                                p * NSEQ + qh * 1024 + (half + 1) * 512],
                                         start=True, stop=True),
                                wait=([("v", pv), ("a", pa), ("v", qk_v)] if half == 0 else []),
                                inc="t")
                        pb = kt % 2
                        if diag:
                            r = kt * 128 - qh * 1024
                            mv = S.add("vector", lambda e, ps=ps, r=r:
                                       e.tensor_add(out=Lbuf, in0=ps[0:128, 0:1024],
                                                    in1=MSK[:, 896 - r: 896 - r + 1024]),
                                       wait=[("t", last_mm), ("a", st.get("lbuf_a", 0))],
                                       inc="v")
                            if par == 0:
                                st["psA_v"] = mv
                            else:
                                st["psB_v"] = mv
                            aexp = S.add("scalar", lambda e, pb=pb:
                                         e.activation(out=Pbuf[pb], in_=Lbuf, func=AF.Exp),
                                         wait=[("v", mv), ("t", st["P_rd"][pb])], inc="a")
                            st["lbuf_a"] = aexp
                        else:
                            aexp = S.add("scalar", lambda e, ps=ps, pb=pb:
                                         e.activation(out=Pbuf[pb], in_=ps[0:128, 0:1024],
                                                      func=AF.Exp),
                                         wait=[("t", last_mm), ("t", st["P_rd"][pb])], inc="a")
                            if par == 0:
                                st["psA_a"] = aexp
                            else:
                                st["psB_a"] = aexp
                        for half in range(2):
                            pvmm = S.add("tensor",
                                         lambda e, kt=kt, h=h, half=half, pb=pb,
                                         k0=kts[0], k1=kts[-1]:
                                         e.matmul(out=psO[0:65, half * 512:(half + 1) * 512],
                                                  lhsT=Vt[:, (kt * HEADS + h) * 65:
                                                          (kt * HEADS + h + 1) * 65],
                                                  rhs=Pbuf[pb][:, half * 512:(half + 1) * 512],
                                                  start=(kt == k0), stop=(kt == k1)),
                                         wait=([("a", aexp)] if half == 0 else []) +
                                              ([("v", st["psO_v"])]
                                               if (kt == kts[0] and half == 0) else []),
                                         inc="t")
                        st["P_rd"][pb] = pvmm
                        st["vt_rd"] = pvmm
                    def recip_fn(e):
                        with nc.allow_low_precision(reason="softmax denom recip bf16"):
                            return e.reciprocal(out=rrow, in_=psO[64:65, 0:1024])
                    vr = S.add("vector", recip_fn, wait=[("t", pvmm)], inc="v")
                    rmm = S.add("tensor", lambda e:
                                e.matmul(out=psA[0:64, 0:512], lhsT=ones_row[0:1, 0:64],
                                         rhs=rrow[0:1, 0:512], start=True, stop=True),
                                wait=[("v", vr), ("a", st["psA_a"]), ("v", st["psA_v"])],
                                inc="t")
                    rmm2 = S.add("tensor", lambda e:
                                 e.matmul(out=psA[0:64, 512:1024], lhsT=ones_row[0:1, 0:64],
                                          rhs=rrow[0:1, 512:1024], start=True, stop=True),
                                 inc="t")
                    vR = S.add("vector", lambda e:
                               e.tensor_copy(out=Rsb, in_=psA[0:64, 0:1024]),
                               wait=[("t", rmm2)], inc="v")
                    st["psA_v"] = vR
                    oi = qh
                    ouev = S.add("vector", lambda e, oi=oi:
                                 e.tensor_mul(out=oust[oi], in0=psO[0:64, 0:1024], in1=Rsb),
                                 wait=[("v", vR), ("_", st["oust_ld"][oi])], inc="v")
                    st["psO_v"] = ouev
                    st["oust_ld"][oi] = S.add(
                        "sync", lambda e, h=h, qh=qh, oi=oi:
                        e.dma_start(out=ou_scr[h * 64:(h + 1) * 64, qh * 1024:(qh + 1) * 1024],
                                    in_=oust[oi]),
                        wait=[("v", ouev)], inc="ld", inc_n=16)
            st["qk_rd"] = S.val("t")

            ou_f = S.dma_fence("ld")
            for nb in range(4):
                lo = S.add("sync", lambda e, nb=nb:
                           e.dma_start(out=Oun.rearrange("p (pr j) -> p pr j", pr=8),
                                       in_=ou_scr.rearrange("(pr p) (nb j) -> p pr nb j",
                                                            p=128, nb=4)[:, :, nb, :]),
                           wait=ou_f + [("t", S.val("t"))], inc="ld", inc_n=16)
                for g2 in range(2):
                    for pr in range(8):
                        for oo in range(4):
                            oc = g2 * 4 + oo
                            ps = (psA, psB)[oo // 2]
                            half = oo % 2
                            S.add("tensor",
                                  lambda e, pr=pr, oc=oc, ps=ps, half=half:
                                  e.matmul(out=ps[0:128, half * 512:(half + 1) * 512],
                                           lhsT=wslot[1][:, pr * D + oc * 128: pr * D + (oc + 1) * 128],
                                           rhs=Oun[:, pr * 512:(pr + 1) * 512],
                                           start=(pr == 0), stop=(pr == 7)),
                                  wait=([("_", lo), ("_", wo_ld),
                                         ("v", st["psA_v"] if oo // 2 == 0 else st["psB_v"]),
                                         ("a", st["psA_a"] if oo // 2 == 0 else st["psB_a"])]
                                        if pr == 0 and half == 0 else []),
                                  inc="t")
                    mm_g = S.val("t")
                    for oo in range(4):
                        oc = g2 * 4 + oo
                        ps = (psA, psB)[oo // 2]
                        half = oo % 2
                        i = oc % 2
                        lx = S.add("sync", lambda e, b=b, oc=oc, nb=nb, i=i:
                                   e.dma_start(out=x2t[i],
                                               in_=x2_in[b, oc * 128:(oc + 1) * 128,
                                                         nb * 512:(nb + 1) * 512]),
                                   wait=[("v", st["x2t_v"][i])], inc="ld", inc_n=16)
                        ev = S.add("vector", lambda e, oc=oc, i=i, ps=ps, half=half:
                                   e.scalar_tensor_tensor(out=x3st[i],
                                                          in0=ps[0:128, half * 512:(half + 1) * 512],
                                                          scalar=bcol[:, 16 + oc:17 + oc],
                                                          in1=x2t[i],
                                                          op0=ALU.add, op1=ALU.add),
                                   wait=[("t", mm_g), ("_", lx), ("_", st["x3_ld"][i])],
                                   inc="v")
                        st["x2t_v"][i] = ev
                        if oo // 2 == 0:
                            st["psA_v"] = ev
                        else:
                            st["psB_v"] = ev
                        st["x3_ld"][i] = S.add(
                            "sync", lambda e, b=b, oc=oc, nb=nb, i=i:
                            e.dma_start(out=x3_out[b, oc * 128:(oc + 1) * 128,
                                                   nb * 512:(nb + 1) * 512],
                                        in_=x3st[i]),
                            wait=[("v", S.val("v"))], inc="ld", inc_n=16)
            st["y2_rd"] = S.val("t")

        if C is None:
            S.add("sync", None, wait=S.dma_fence("ld") + S.dma_fence("g"))
            S.emit(nc, stack)
    return nc


P6_HEADS = 8
P6_INNER = 512
P6_HID = 2730
P6_W = 512
HJ = (P6_HID + 127) // 128


def _xpos_tables():
    hd = HD
    half_idx = np.arange(0, hd, 2, dtype=np.float64)
    inv = 1.0 / 10000.0 ** (half_idx / hd)
    t = np.arange(NSEQ, dtype=np.float64)
    fr = np.concatenate([inv, inv])
    ang = fr[:, None] * t[None, :]
    base = (half_idx + 0.4 * hd) / (1.4 * hd)
    base2 = np.concatenate([base, base])
    power = (t - NSEQ // 2) / float(P6_W // 2)
    scale = base2[:, None] ** power[None, :]
    cos, sin = np.cos(ang), np.sin(ang)
    return [np.vstack([a, a]).astype(BF) for a in
            (cos * scale * 8.0, sin * scale * 8.0, cos / scale, sin / scale)]


def build_p6(n_batches=NB, n_layers=2, attn_only=False, C=None):
    if C is None:
        nc = bass.Bass("TRN2", target_bir_lowering=False, detect_race_conditions=False)
        S = Sched(SEMS)
        x3_in = nc.dram_tensor("x3_fm", [n_batches, D, NSEQ], F32, kind="ExternalInput").ap()
        wqkv_dr = nc.dram_tensor("wqkv", [2, D, 3 * P6_INNER], F32, kind="ExternalInput").ap()
        wol_dr = nc.dram_tensor("wo_l", [2, P6_INNER, D], F32, kind="ExternalInput").ap()
        w1_dr = nc.dram_tensor("w1", [2, D, 2 * P6_HID], F32, kind="ExternalInput").ap()
        w2_dr = nc.dram_tensor("w2", [2, P6_HID, D], F32, kind="ExternalInput").ap()
        qs_dr = nc.dram_tensor("q_scale", [2, HD], F32, kind="ExternalInput").ap()
        ks_dr = nc.dram_tensor("k_scale", [2, HD], F32, kind="ExternalInput").ap()
        lng_dr = nc.dram_tensor("ln_g", [2, D], F32, kind="ExternalInput").ap()
        lnb_dr = nc.dram_tensor("ln_b", [2, D], F32, kind="ExternalInput").ap()
        b1_dr = nc.dram_tensor("b1", [2, 2 * P6_HID], F32, kind="ExternalInput").ap()
        b2_dr = nc.dram_tensor("b2", [2, D], F32, kind="ExternalInput").ap()
        x_out = nc.dram_tensor("xout", [n_batches, NSEQ, D], F32, kind="ExternalOutput").ap()
    else:
        nc, S = C.nc, C.S
        x3_in = C.io["x3_fm"]
        wqkv_dr = C.io["l1f_wqkv"]
        wol_dr = C.io["l1f_wo"]
        w1_dr = C.io["l1f_w1"]
        w2_dr = C.io["l1f_w2"]
        qs_dr = C.io["l1f_qs"]
        ks_dr = C.io["l1f_ks"]
        lng_dr = C.io["l1f_lng"]
        lnb_dr = C.io["l1f_lnb"]
        b1_dr = C.io["l1f_b1"]
        b2_dr = C.io["l1f_b2"]
        x_out = C.io["xout"]

    xi32 = nc.dram_tensor("xi32", [n_batches, D, NSEQ], F32, kind="Internal").ap()
    xi16 = nc.dram_tensor("xi16", [n_batches, D, NSEQ], BF16, kind="Internal").ap()
    ou_scr = nc.dram_tensor("p6_ou", [P6_INNER, NSEQ], BF16, kind="Internal").ap()
    nr_scr = nc.dram_tensor("nr_scr", [16, NSEQ], BF16, kind="Internal").ap()
    wq16 = nc.dram_tensor("wq16", [2, D, 3 * P6_INNER], BF16, kind="Internal").ap()
    wol16 = nc.dram_tensor("wol16", [2, P6_INNER, D], BF16, kind="Internal").ap()
    w116 = nc.dram_tensor("w116", [2, D, 2 * P6_HID], BF16, kind="Internal").ap()
    w216 = nc.dram_tensor("w216", [2, P6_HID, D], BF16, kind="Internal").ap()

    ident16_c = _const(nc, C, "ident16", np.eye(128, dtype=BF))
    ident32_c = _const(nc, C, "ident32", np.eye(128, dtype=np.float32))
    CSq_h, SSq_h, CSk_h, SSk_h = _xpos_tables()
    csq_c = _const(nc, C, "csq", CSq_h)
    ssq_c = _const(nc, C, "ssq", SSq_h)
    csk_c = _const(nc, C, "csk", CSk_h)
    ssk_c = _const(nc, C, "ssk", SSk_h)
    perm_c = _const(nc, C, "permT", _perm_block().astype(BF))
    u = np.arange(1408) - 384
    cm = np.where((u[None, :] >= np.arange(128)[:, None]) &
                  (u[None, :] <= np.arange(128)[:, None] + P6_W), 0.0, -1e30)
    cmask_c = _const(nc, C, "cmask6", cm.astype(BF))
    hind = np.zeros((128, 8 * 16), np.float64)
    for cc in range(8):
        hind[0:64, cc * 16 + 2 * cc] = 1.0
        hind[64:128, cc * 16 + 2 * cc + 1] = 1.0
    hind_c = _const(nc, C, "hind", hind.astype(BF))

    with ExitStack() as stack:
        if C is None:
            AR = Arena(stack.enter_context(
                nc.sbuf_tensor("AR", [128, 53160], F32)).ap(), 53160)
        else:
            AR = C.AR
            AR.reset(0)
        e_ = stack.enter_context

        id16 = AR.bf16(128, 128)
        id32 = AR.f32(128, 128)
        permT = AR.bf16(128, 128)
        CSQ = AR.bf16(128, NSEQ)
        SSQ = AR.bf16(128, NSEQ)
        CSK = AR.bf16(128, NSEQ)
        SSK = AR.bf16(128, NSEQ)
        CMSK = AR.bf16(128, 1408)
        HIND = AR.bf16(128, 8 * 16)
        qscol = AR.f32(128, 1)
        kscol = AR.f32(128, 1)
        lgcol = AR.f32(128, 8)
        lbcol = AR.f32(128, 8)
        b2col = AR.f32(128, 8)
        b1acol = AR.f32(128, HJ)
        b1gcol = AR.f32(128, HJ)
        epscol = AR.f32(128, 1)
        ones_col = AR.bf16(128, 2)
        o_or = AR.block(64)
        ones_row = AR.bf16_at(0, 1, o_or, 128)
        rrow = AR.bf16(1, 512)
        nrA = AR.bf16(1, NSEQ)
        nrB = AR.bf16(1, NSEQ)
        r1 = AR.block(1024)
        sum16 = AR.bf16_at(0, 1, r1, 2048)
        ss16 = AR.bf16_at(32, 1, r1, 2048)
        rm_ = AR.block(1024)
        m16 = AR.bf16_at(0, 1, rm_, 2048)
        var16 = AR.bf16_at(32, 1, rm_, 2048)
        rr_ = AR.block(1024)
        rstd16 = AR.bf16_at(0, 1, rr_, 2048)
        ssd16 = AR.bf16_at(32, 1, rr_, 2048)
        Mt = AR.bf16(128, NSEQ)
        Rt = AR.bf16(128, NSEQ)
        NR = AR.bf16(16, 1024)

        mR1 = AR.mark()
        Qf = AR.bf16(128, 4 * NSEQ)
        Kf = AR.bf16(128, 4 * NSEQ)
        e1 = AR.mark()
        AR.reset(mR1)
        LNH = AR.bf16(128, DC * NSEQ)
        e2 = AR.mark()
        AR.reset(max(e1, e2))

        mR2 = AR.mark()
        Vt = AR.bf16(128, NT * P6_HEADS * (HD + 1))
        Lb6 = AR.f32(128, 512)
        Pb6 = [AR.bf16(128, 512), AR.bf16(128, 512)]
        Rsb = AR.bf16(64, 512)
        oust = [AR.bf16(64, 512), AR.bf16(64, 512)]
        tmp1 = AR.bf16(128, 512)
        tmp2 = AR.bf16(128, 512)
        e1 = AR.mark()
        AR.reset(mR2)
        hg = AR.bf16(128, HJ * 1024)
        ffG = AR.bf16(128, 512)
        ffA = AR.bf16(128, 512)
        e2 = AR.mark()
        AR.reset(max(e1, e2))

        mR3 = AR.mark()
        wqkv_sb = AR.bf16(128, DC * 3 * P6_INNER)
        e1 = AR.mark()
        AR.reset(mR3)
        w1a_sb = [AR.bf16(128, DC * 128), AR.bf16(128, DC * 128)]
        w1g_sb = [AR.bf16(128, DC * 128), AR.bf16(128, DC * 128)]
        w2_sb = [AR.bf16(128, HJ * 128), AR.bf16(128, HJ * 128)]
        e2 = AR.mark()
        AR.reset(max(e1, e2))

        wo_sb = AR.bf16(128, 4 * D)
        Oun = AR.bf16(128, 4 * 512)
        sq512 = AR.bf16(128, 512)
        x2t = [AR.f32(128, 512), AR.f32(128, 512)]
        x2c = AR.bf16(128, 512)
        xot = [AR.f32(128, 512), AR.f32(128, 512)]
        xo16 = [AR.bf16(128, 512), AR.bf16(128, 512)]
        vls = [AR.bf16(128, 128), AR.bf16(128, 128)]
        ytmp = AR.f32(128, 512)
        ytmp2 = AR.f32(128, 512)
        xstage = [AR.f32(128, D), AR.f32(128, D)]
        wtmp32 = AR.f32(128, 1024)
        wtmp16 = AR.bf16(128, 1024)

        if C is None:
            psA = e_(nc.psum_tensor("psA", [128, 1024], F32)).ap()
            psB = e_(nc.psum_tensor("psB", [128, 1024], F32)).ap()
            psO = e_(nc.psum_tensor("psO", [128, 1024], F32)).ap()
            pt16 = e_(nc.psum_tensor("pt16", [128, 2048], BF16)).ap()
        else:
            psA, psB, psO, pt16 = C.psA, C.psB, C.psO, C.pt16

        toks = []
        for sb_, c_ in ((id16, ident16_c), (id32, ident32_c), (permT, perm_c),
                        (CSQ, csq_c), (SSQ, ssq_c), (CSK, csk_c), (SSK, ssk_c),
                        (CMSK, cmask_c), (HIND, hind_c)):
            toks.append(S.add("sync", lambda e, sb_=sb_, c_=c_:
                              e.dma_start(out=sb_, in_=c_.ap()), inc="ld", inc_n=16))
        tab_f = [("_", t) for t in toks]
        S.add("vector", lambda e: e.memset(ones_col[:, 0:1], 1.0), inc="v")
        S.add("vector", lambda e: e.memset(epscol, EPS), inc="v")
        S.add("vector", lambda e: e.memset(ones_row, 1.0), inc="v")

        def set_vt_ones(e):
            ap = Vt.rearrange("p (nt h j) -> p nt h j", nt=NT, h=P6_HEADS, j=HD + 1)
            return e.memset(ap[:, :, :, HD:HD + 1], 1.0)
        vto = S.add("vector", set_vt_ones, inc="v")

        wstate = dict(cast_done=0, st_ld=0)

        def emit_cast(src, dst, rows, cols):
            nt_ = (rows + 127) // 128
            cw = 1024
            ncw = (cols + cw - 1) // cw
            for r in range(nt_):
                pr = min(128, rows - r * 128)
                for cb in range(ncw):
                    w = min(cw, cols - cb * cw)
                    L = S.add("sync", lambda e, r=r, cb=cb, pr=pr, w=w, src=src:
                              e.dma_start(out=wtmp32[0:pr, 0:w],
                                          in_=src[r * 128:r * 128 + pr, cb * cw:cb * cw + w]),
                              wait=[("v", wstate["cast_done"])], inc="ld", inc_n=16)
                    cv = S.add("vector", lambda e, pr=pr, w=w:
                               e.tensor_copy(out=wtmp16[0:pr, 0:w], in_=wtmp32[0:pr, 0:w]),
                               wait=[("_", L), ("_", wstate["st_ld"])], inc="v")
                    wstate["cast_done"] = cv
                    wstate["st_ld"] = S.add(
                        "sync", lambda e, r=r, cb=cb, pr=pr, w=w, dst=dst:
                        e.dma_start(out=dst[r * 128:r * 128 + pr, cb * cw:cb * cw + w],
                                    in_=wtmp16[0:pr, 0:w]),
                        wait=[("v", cv)], inc="ld", inc_n=16)

        for l in range(2):
            emit_cast(wqkv_dr[l], wq16[l], D, 3 * P6_INNER)
            emit_cast(wol_dr[l], wol16[l], P6_INNER, D)
            emit_cast(w1_dr[l], w116[l], D, 2 * P6_HID)
            emit_cast(w2_dr[l], w216[l], P6_HID, D)
        prep_f = S.dma_fence("ld")

        st = dict(
            psA_v=0, psA_a=0, psB_v=0, psB_a=0, psO_v=0, psO_a=0,
            P_rd=[0, 0], sq_rd=0, oust_ld=[0, 0],
            x2t_v=[0, 0], x2c_rd=0, xot_ld=[0, 0], xo16_ld=[0, 0], xo16_rd=[0, 0],
            mt_rd=0, lnh_rd=0, qf_rd=0, vt_rd=0, tmp1_rd=0, tmp2_rd=0, hg_rd=0,
            w1_ld=[0, 0], w1_rd=[0, 0], w2_ld=[0, 0], w2_rd=[0, 0],
            nr_rd=0, nrow_rd=0, rrow_rd=0,
            lb6_a=0, oun_rd=0, vls_rd=[0, 0],
            xstage_ld=[0, 0], ytmp2_rd=0,
        )

        def ln_rows_and_bcast(shifted_src=None):
            rv1 = S.add("vector", lambda e: e.tensor_scalar_mul(out=m16, in0=sum16,
                                                                scalar1=1.0 / D),
                        wait=[("v", S.val("v"))], inc="v")
            rv2 = S.add("vector", lambda e: e.tensor_mul(out=var16, in0=m16, in1=m16),
                        wait=[("v", rv1)], inc="v")
            rv3 = S.add("vector", lambda e: e.tensor_scalar_mul(out=ssd16, in0=ss16,
                                                                scalar1=1.0 / D),
                        inc="v")
            vv = S.add("vector", lambda e: e.tensor_sub(out=var16, in0=ssd16, in1=var16),
                       wait=[("v", rv3), ("v", rv2)], inc="v")
            aL = S.add("scalar", lambda e: e.activation(out=ssd16, in_=var16, func=AF.Ln,
                                                        bias=epscol[0:1, 0:1]),
                       wait=[("v", vv)], inc="a")
            ar_ = S.add("scalar", lambda e: e.activation(out=rstd16, in_=ssd16, func=AF.Exp,
                                                         scale=-0.5), wait=[("a", aL)], inc="a")
            for nb in range(4):
                par = nb % 2
                ps = (psA, psB)[par]
                pv = st["psA_v"] if par == 0 else st["psB_v"]
                pa = st["psA_a"] if par == 0 else st["psB_a"]
                S.add("tensor", lambda e, nb=nb, ps=ps:
                      e.matmul(out=ps[0:128, 0:512], lhsT=ones_row[0:1, :],
                               rhs=m16[0:1, nb * 512:(nb + 1) * 512], start=True, stop=True),
                      wait=[("a", ar_), ("v", pv), ("a", pa), ("v", st["mt_rd"])], inc="t")
                m2 = S.add("tensor", lambda e, nb=nb, ps=ps:
                           e.matmul(out=ps[0:128, 512:1024], lhsT=ones_row[0:1, :],
                                    rhs=rstd16[0:1, nb * 512:(nb + 1) * 512], start=True,
                                    stop=True),
                           inc="t")
                S.add("vector", lambda e, nb=nb, ps=ps:
                      e.tensor_copy(out=Mt[:, nb * 512:(nb + 1) * 512], in_=ps[0:128, 0:512]),
                      wait=[("t", m2)], inc="v")
                ev2 = S.add("vector", lambda e, nb=nb, ps=ps:
                            e.tensor_copy(out=Rt[:, nb * 512:(nb + 1) * 512],
                                          in_=ps[0:128, 512:1024]),
                            inc="v")
                if par == 0:
                    st["psA_v"] = ev2
                else:
                    st["psB_v"] = ev2
            return S.val("v")

        def fm_ln_stats(load_tile):
            for nb in range(4):
                for oc in range(DC):
                    cw = load_tile(oc, nb)
                    mm1 = S.add("tensor", lambda e, oc=oc:
                                e.matmul(out=psO[0:1, 0:512], lhsT=ones_col[0:128, 0:1],
                                         rhs=x2c, start=(oc == 0), stop=(oc == DC - 1)),
                                wait=cw + ([("v", st["psO_v"]), ("a", st["psO_a"])]
                                           if oc == 0 else []),
                                inc="t")
                    sv = S.add("vector", lambda e:
                               e.tensor_mul(out=sq512, in0=x2c, in1=x2c),
                               wait=cw + [("t", st["sq_rd"])], inc="v")
                    mm2 = S.add("tensor", lambda e, oc=oc:
                                e.matmul(out=psO[0:1, 512:1024], lhsT=ones_col[0:128, 0:1],
                                         rhs=sq512, start=(oc == 0), stop=(oc == DC - 1)),
                                wait=[("v", sv)], inc="t")
                    st["sq_rd"] = mm2
                    st["x2c_rd"] = mm2
                S.add("vector", lambda e, nb=nb:
                      e.tensor_copy(out=sum16[0:1, nb * 512:(nb + 1) * 512], in_=psO[0:1, 0:512]),
                      wait=[("t", S.val("t"))], inc="v")
                st["psO_v"] = S.add("vector", lambda e, nb=nb:
                                    e.tensor_copy(out=ss16[0:1, nb * 512:(nb + 1) * 512],
                                                  in_=psO[0:1, 512:1024]),
                                    inc="v")

        def load_shift16(dstt, src16, oc, nb, width=512, war=()):
            shift = oc >= 4
            if shift and nb == 0:
                ms = S.add("vector", lambda e, dstt=dstt: e.memset(dstt[:, 0:1], 0.0),
                           wait=list(war), inc="v")
                L = S.add("gpsimd", lambda e, oc=oc, src16=src16, dstt=dstt, width=width:
                          e.dma_start(out=dstt[:, 1:width],
                                      in_=src16[oc * 128:(oc + 1) * 128, 0:width - 1]),
                          wait=list(war), inc="g", inc_n=16)
                return [("_", L), ("v", ms)]
            off = nb * width - (1 if shift else 0)
            L = S.add("gpsimd", lambda e, oc=oc, off=off, src16=src16, dstt=dstt, width=width:
                      e.dma_start(out=dstt[:, 0:width],
                                  in_=src16[oc * 128:(oc + 1) * 128, off:off + width]),
                      wait=list(war), inc="g", inc_n=16)
            return [("_", L)]

        first = True
        for b in range(n_batches):
            S.barrier()
            for k in ("psA_v", "psB_v", "psO_v"):
                st[k] = 0
            st["psA_a"] = st["psB_a"] = st["psO_a"] = 0
            if first:
                S.add("sync", None, wait=prep_f + tab_f)
                first = False

            def entry_tile(oc, nb):
                i = oc % 2
                lx = S.add("sync", lambda e, b=b, oc=oc, nb=nb, i=i:
                           e.dma_start(out=x2t[i],
                                       in_=x3_in[b, oc * 128:(oc + 1) * 128,
                                                 nb * 512:(nb + 1) * 512]),
                           wait=[("v", st["x2t_v"][i])], inc="ld", inc_n=16)
                cv = S.add("vector", lambda e, i=i:
                           e.tensor_copy(out=x2c, in_=x2t[i]),
                           wait=[("_", lx), ("t", st["x2c_rd"])], inc="v")
                st["x2t_v"][i] = cv
                return [("v", cv)]

            fm_ln_stats(entry_tile)
            bc_v = ln_rows_and_bcast()
            for oc in range(DC):
                for nb in range(4):
                    k = oc * 4 + nb
                    i = k % 2
                    lx = S.add("sync", lambda e, b=b, oc=oc, nb=nb, i=i:
                               e.dma_start(out=x2t[i],
                                           in_=x3_in[b, oc * 128:(oc + 1) * 128,
                                                     nb * 512:(nb + 1) * 512]),
                               wait=[("v", st["x2t_v"][i])], inc="ld", inc_n=16)
                    v1_ = S.add("vector", lambda e, i=i, nb=nb:
                                e.tensor_sub(out=ytmp, in0=x2t[i],
                                             in1=Mt[:, nb * 512:(nb + 1) * 512]),
                                wait=[("_", lx), ("v", bc_v)], inc="v")
                    st["x2t_v"][i] = v1_
                    v2_ = S.add("vector", lambda e, i=i, nb=nb:
                                e.tensor_mul(out=xot[i], in0=ytmp,
                                             in1=Rt[:, nb * 512:(nb + 1) * 512]),
                                wait=[("_", st["xot_ld"][i])], inc="v")
                    v3_ = S.add("vector", lambda e, i=i:
                                e.tensor_copy(out=xo16[i], in_=xot[i]),
                                wait=[("_", st["xo16_ld"][i])], inc="v")
                    st["xot_ld"][i] = S.add("sync", lambda e, b=b, oc=oc, nb=nb, i=i:
                                            e.dma_start(out=xi32[b, oc * 128:(oc + 1) * 128,
                                                                 nb * 512:(nb + 1) * 512],
                                                        in_=xot[i]),
                                            wait=[("v", v2_)], inc="ld", inc_n=16)
                    st["xo16_ld"][i] = S.add("sync", lambda e, b=b, oc=oc, nb=nb, i=i:
                                             e.dma_start(out=xi16[b, oc * 128:(oc + 1) * 128,
                                                                  nb * 512:(nb + 1) * 512],
                                                         in_=xo16[i]),
                                             wait=[("v", v3_)], inc="ld", inc_n=16)
            st["mt_rd"] = S.val("v")

        for l in range(n_layers):
            S.barrier()
            lt = []
            lt.append(S.add("gpsimd", lambda e, l=l:
                            e.dma_start(out=qscol[0:64, 0:1],
                                        in_=qs_dr[l].rearrange("(d one) -> d one", one=1)),
                            inc="g", inc_n=16))
            lt.append(S.add("gpsimd", lambda e, l=l:
                            e.dma_start(out=qscol[64:128, 0:1],
                                        in_=qs_dr[l].rearrange("(d one) -> d one", one=1)),
                            inc="g", inc_n=16))
            lt.append(S.add("gpsimd", lambda e, l=l:
                            e.dma_start(out=kscol[0:64, 0:1],
                                        in_=ks_dr[l].rearrange("(d one) -> d one", one=1)),
                            inc="g", inc_n=16))
            lt.append(S.add("gpsimd", lambda e, l=l:
                            e.dma_start(out=kscol[64:128, 0:1],
                                        in_=ks_dr[l].rearrange("(d one) -> d one", one=1)),
                            inc="g", inc_n=16))
            for c in range(DC):
                for (col, dr) in ((lgcol, lng_dr), (lbcol, lnb_dr), (b2col, b2_dr)):
                    lt.append(S.add("sync", lambda e, l=l, c=c, col=col, dr=dr:
                                    e.dma_start(out=col[:, c:c + 1],
                                                in_=dr[l, c * 128:(c + 1) * 128]
                                                .rearrange("(d one) -> d one", one=1)),
                                    inc="ld", inc_n=16))
            for j in range(HJ):
                pr = min(128, P6_HID - j * 128)
                lt.append(S.add("sync", lambda e, l=l, j=j, pr=pr:
                                e.dma_start(out=b1acol[0:pr, j:j + 1],
                                            in_=b1_dr[l, j * 128: j * 128 + pr]
                                            .rearrange("(d one) -> d one", one=1)),
                                inc="ld", inc_n=16))
                lt.append(S.add("sync", lambda e, l=l, j=j, pr=pr:
                                e.dma_start(out=b1gcol[0:pr, j:j + 1],
                                            in_=b1_dr[l, P6_HID + j * 128: P6_HID + j * 128 + pr]
                                            .rearrange("(d one) -> d one", one=1)),
                                inc="ld", inc_n=16))
            lay_f = [("_", t) for t in lt]

            for b in range(n_batches):
                S.barrier()
                for k in ("psA_v", "psB_v", "psO_v"):
                    st[k] = 0
                st["psA_a"] = st["psB_a"] = st["psO_a"] = 0
                S.add("tensor", None, wait=lay_f)
                wqkv_ld = S.add("sync", lambda e, l=l:
                                e.dma_start(out=wqkv_sb.rearrange("p (c n) -> p c n", c=DC),
                                            in_=wq16[l].rearrange("(c p) n -> p c n", p=128)),
                                inc="ld", inc_n=16)
                wol_ld = S.add("sync", lambda e, l=l:
                               e.dma_start(out=wo_sb.rearrange("p (c n) -> p c n", c=4),
                                           in_=wol16[l].rearrange("(c p) n -> p c n", p=128)),
                               inc="ld", inc_n=16)
                vto = S.add("vector", set_vt_ones, inc="v")

                for ot in range(8):
                    dstb = Qf if ot < 4 else Kf
                    for nb in range(4):
                        par = nb % 2
                        ps = (psA, psB)[par]
                        pv = st["psA_v"] if par == 0 else st["psB_v"]
                        pa = st["psA_a"] if par == 0 else st["psB_a"]
                        for ic in range(DC):
                            i = ic % 2
                            ltile = load_shift16(xo16[i], xi16[b], ic, nb,
                                                 war=[("t", st["xo16_rd"][i])])
                            mmw = ltile + ([("v", pv), ("a", pa), ("_", wqkv_ld)]
                                           if ic == 0 else [])
                            st["xo16_rd"][i] = S.add(
                                "tensor", lambda e, ot=ot, ic=ic, ps=ps, i=i:
                                e.matmul(out=ps[0:128, 0:512],
                                         lhsT=wqkv_sb[:, ic * 1536 + ot * 128:
                                                      ic * 1536 + (ot + 1) * 128],
                                         rhs=xo16[i],
                                         start=(ic == 0), stop=(ic == DC - 1)),
                                wait=mmw, inc="t")
                        mm = S.val("t")
                        ev = S.add("vector", lambda e, ot=ot, nb=nb, ps=ps, dstb=dstb:
                                   e.tensor_copy(
                                       out=dstb[:, (ot % 4) * NSEQ + nb * 512:
                                                (ot % 4) * NSEQ + (nb + 1) * 512],
                                       in_=ps[0:128, 0:512]),
                                   wait=[("t", mm), ("t", st["qf_rd"])], inc="v")
                        if par == 0:
                            st["psA_v"] = ev
                        else:
                            st["psB_v"] = ev
                qk_v0 = S.val("v")

                for nt in range(NT):
                    par = nt % 2
                    ps = (psA, psB)[par]
                    pv = st["psA_v"] if par == 0 else st["psB_v"]
                    pa = st["psA_a"] if par == 0 else st["psB_a"]
                    for ic in range(DC):
                        i = ic % 2
                        shift = ic >= 4
                        if shift and nt == 0:
                            ms = S.add("vector", lambda e, i=i: e.memset(vls[i][:, 0:1], 0.0),
                                       wait=[("t", st["vls_rd"][i])], inc="v")
                            L = S.add("gpsimd", lambda e, ic=ic, b=b, i=i:
                                      e.dma_start(out=vls[i][:, 1:128],
                                                  in_=xi16[b, ic * 128:(ic + 1) * 128, 0:127]),
                                      wait=[("t", st["vls_rd"][i])], inc="g", inc_n=16)
                            ltile = [("_", L), ("v", ms)]
                        else:
                            o0 = nt * 128 - (1 if shift else 0)
                            L = S.add("gpsimd", lambda e, ic=ic, b=b, i=i, o0=o0:
                                      e.dma_start(out=vls[i],
                                                  in_=xi16[b, ic * 128:(ic + 1) * 128,
                                                           o0:o0 + 128]),
                                      wait=[("t", st["vls_rd"][i])], inc="g", inc_n=16)
                            ltile = [("_", L)]
                        st["vls_rd"][i] = S.add(
                            "tensor", lambda e, nt=nt, ic=ic, ps=ps, i=i:
                            e.matmul(out=ps[0:128, 0:512],
                                     lhsT=vls[i],
                                     rhs=wqkv_sb[:, ic * 1536 + 1024: ic * 1536 + 1536],
                                     start=(ic == 0), stop=(ic == DC - 1)),
                            wait=ltile + ([("v", pv), ("a", pa)] if ic == 0 else []),
                            inc="t")
                    mm = S.val("t")
                    ev = S.add("vector", lambda e, nt=nt, ps=ps:
                               e.tensor_copy(
                                   out=Vt[:, nt * P6_HEADS * 65: (nt + 1) * P6_HEADS * 65]
                                   .rearrange("p (h j) -> p h j", j=65)[:, :, 0:64],
                                   in_=ps[0:128, 0:512].rearrange("p (h j) -> p h j", j=64)),
                               wait=[("t", mm), ("t", st["vt_rd"]), ("v", vto)], inc="v")
                    if par == 0:
                        st["psA_v"] = ev
                    else:
                        st["psB_v"] = ev

                for half in range(2):
                    for sb2 in range(2):
                        nbx = half * 2 + sb2
                        for cc in range(8):
                            src = Qf if cc < 4 else Kf
                            c4 = cc % 4
                            sv = S.add("vector",
                                       lambda e, src=src, c4=c4, nbx=nbx:
                                       e.tensor_mul(
                                           out=sq512,
                                           in0=src[:, c4 * NSEQ + nbx * 512:
                                                   c4 * NSEQ + (nbx + 1) * 512],
                                           in1=src[:, c4 * NSEQ + nbx * 512:
                                                   c4 * NSEQ + (nbx + 1) * 512]),
                                       wait=[("t", st["sq_rd"]), ("v", qk_v0)], inc="v")
                            mm = S.add("tensor", lambda e, cc=cc, sb2=sb2:
                                       e.matmul(out=psO[0:16, sb2 * 512:(sb2 + 1) * 512],
                                                lhsT=HIND[:, cc * 16:(cc + 1) * 16],
                                                rhs=sq512, start=(cc == 0), stop=(cc == 7)),
                                       wait=[("v", sv)] +
                                            ([("v", st["psO_v"]), ("a", st["psO_a"])]
                                             if (cc == 0 and sb2 == 0) else []),
                                       inc="t")
                            st["sq_rd"] = mm
                    lmm = S.val("t")
                    a1 = S.add("scalar", lambda e:
                               e.activation(out=NR, in_=psO[0:16, 0:1024], func=AF.Ln),
                               wait=[("t", lmm), ("_", st["nr_rd"])],
                               inc="a")
                    st["psO_a"] = a1
                    a2 = S.add("scalar", lambda e:
                               e.activation(out=NR, in_=NR, func=AF.Exp, scale=-0.5),
                               inc="a")
                    st["nr_rd"] = S.add("gpsimd", lambda e, half=half:
                                        e.dma_start(out=nr_scr[:, half * 1024:(half + 1) * 1024],
                                                    in_=NR),
                                        wait=[("a", a2)], inc="g", inc_n=16)
                    st["psO_v"] = 0 if False else st["psO_v"]
                nr_g = st["nr_rd"]

                for qk in range(2):
                    src = (Qf, Kf)[qk]
                    scol_ = (qscol, kscol)[qk]
                    CT = (CSQ, CSK)[qk]
                    ST_ = (SSQ, SSK)[qk]
                    for c4 in range(4):
                        la = S.add("gpsimd", lambda e, qk=qk, c4=c4:
                                   e.dma_start(out=nrA, in_=nr_scr[qk * 8 + 2 * c4:
                                                                   qk * 8 + 2 * c4 + 1, :]),
                                   wait=[("_", nr_g), ("t", st["nrow_rd"])], inc="g", inc_n=16)
                        lb_ = S.add("gpsimd", lambda e, qk=qk, c4=c4:
                                    e.dma_start(out=nrB, in_=nr_scr[qk * 8 + 2 * c4 + 1:
                                                                    qk * 8 + 2 * c4 + 2, :]),
                                    inc="g", inc_n=16)
                        for nb in range(4):
                            sl = slice(c4 * NSEQ + nb * 512, c4 * NSEQ + (nb + 1) * 512)
                            tsl = slice(nb * 512, (nb + 1) * 512)
                            r1_ = S.add("tensor", lambda e, nb=nb:
                                        e.matmul(out=psA[0:64, 0:512],
                                                 lhsT=ones_row[0:1, 0:64],
                                                 rhs=nrA[0:1, nb * 512:(nb + 1) * 512],
                                                 start=True, stop=True),
                                        wait=[("_", la), ("_", lb_), ("v", st["psA_v"]),
                                              ("a", st["psA_a"])], inc="t")
                            r2_ = S.add("tensor", lambda e, nb=nb:
                                        e.matmul(out=psA[64:128, 0:512],
                                                 lhsT=ones_row[0:1, 0:64],
                                                 rhs=nrB[0:1, nb * 512:(nb + 1) * 512],
                                                 start=True, stop=True),
                                        inc="t")
                            st["nrow_rd"] = r2_
                            sv = S.add("vector", lambda e, src=src, sl=sl, scol_=scol_:
                                       e.scalar_tensor_tensor(out=tmp1, in0=src[:, sl],
                                                              scalar=scol_[:, 0:1],
                                                              in1=psA[0:128, 0:512],
                                                              op0=ALU.mult, op1=ALU.mult),
                                       wait=[("t", r2_), ("t", st["tmp1_rd"]),
                                             ("v", qk_v0)] + lay_f[0:1], inc="v")
                            st["psA_v"] = sv
                            pm = S.add("tensor", lambda e:
                                       e.matmul(out=psB[0:128, 0:512], lhsT=permT,
                                                rhs=tmp1, start=True, stop=True),
                                       wait=[("v", sv), ("v", st["psB_v"]),
                                            ("a", st["psB_a"])], inc="t")
                            st["tmp1_rd"] = pm
                            t2 = S.add("vector", lambda e, tsl=tsl, ST_=ST_:
                                       e.tensor_mul(out=tmp2, in0=psB[0:128, 0:512],
                                                    in1=ST_[:, tsl]),
                                       wait=[("t", pm), ("v", st["tmp2_rd"])], inc="v")
                            st["psB_v"] = t2
                            t1 = S.add("vector", lambda e, tsl=tsl, CT=CT:
                                       e.tensor_mul(out=ytmp.bitcast(BF16)[:, 0:512],
                                                    in0=tmp1, in1=CT[:, tsl]),
                                       inc="v")
                            t3 = S.add("vector", lambda e, src=src, sl=sl:
                                       e.tensor_add(out=src[:, sl],
                                                    in0=ytmp.bitcast(BF16)[:, 0:512],
                                                    in1=tmp2),
                                       inc="v")
                            st["tmp2_rd"] = t3
                qk_v = S.val("v")

                for h in range(P6_HEADS):
                    c4, hh = h // 2, h % 2
                    base = hh * 64
                    for w in range(4):
                        kt0 = max(0, w * 4 - 4)
                        kts = list(range(kt0, w * 4 + 4))
                        pvmm = 0
                        for jx, kt in enumerate(kts):
                            par = jx % 2
                            pss = psA[0:128, 0:512] if par == 0 else psA[0:128, 512:1024]
                            pv = st["psA_v"] if par == 0 else st.get("psA2_v", 0)
                            pa = st["psA_a"] if par == 0 else st.get("psA2_a", 0)
                            smm = S.add("tensor",
                                        lambda e, c4=c4, base=base, kt=kt, w=w, pss=pss:
                                        e.matmul(out=pss,
                                                 lhsT=Kf[base:base + 64,
                                                         c4 * NSEQ + kt * 128:
                                                         c4 * NSEQ + (kt + 1) * 128],
                                                 rhs=Qf[base:base + 64,
                                                        c4 * NSEQ + w * 512:
                                                        c4 * NSEQ + (w + 1) * 512],
                                                 start=True, stop=True),
                                        wait=[("v", pv), ("a", pa), ("v", qk_v)], inc="t")
                            r = kt * 128 - w * 512
                            mv = S.add("vector", lambda e, pss=pss, r=r:
                                       e.tensor_add(out=Lb6, in0=pss,
                                                    in1=CMSK[:, 384 - r: 384 - r + 512]),
                                       wait=[("t", smm), ("a", st["lb6_a"])], inc="v")
                            if par == 0:
                                st["psA_v"] = mv
                            else:
                                st["psA2_v"] = mv
                            pb = jx % 2
                            aexp = S.add("scalar", lambda e, pb=pb:
                                         e.activation(out=Pb6[pb], in_=Lb6, func=AF.Exp),
                                         wait=[("v", mv), ("t", st["P_rd"][pb])], inc="a")
                            st["lb6_a"] = aexp
                            pvmm = S.add("tensor", lambda e, kt=kt, h=h, pb=pb,
                                         k0=kts[0], k1=kts[-1]:
                                         e.matmul(out=psO[0:65, 0:512],
                                                  lhsT=Vt[:, (kt * P6_HEADS + h) * 65:
                                                          (kt * P6_HEADS + h + 1) * 65],
                                                  rhs=Pb6[pb],
                                                  start=(kt == k0), stop=(kt == k1)),
                                         wait=([("v", st["psO_v"])] if kt == kts[0] else []) +
                                              [("a", aexp)],
                                         inc="t")
                            st["P_rd"][pb] = pvmm
                            st["vt_rd"] = pvmm
                        def recip6(e):
                            with nc.allow_low_precision(reason="softmax denom bf16"):
                                return e.reciprocal(out=rrow, in_=psO[64:65, 0:512])
                        vr = S.add("vector", recip6,
                                   wait=[("t", pvmm), ("t", st["rrow_rd"])], inc="v")
                        rmm = S.add("tensor", lambda e:
                                    e.matmul(out=psO[0:64, 512:1024],
                                             lhsT=ones_row[0:1, 0:64],
                                             rhs=rrow, start=True, stop=True),
                                    wait=[("v", vr)], inc="t")
                        st["rrow_rd"] = rmm
                        vR = S.add("vector", lambda e:
                                   e.tensor_copy(out=Rsb, in_=psO[0:64, 512:1024]),
                                   wait=[("t", rmm)], inc="v")
                        oi = w % 2
                        ouev = S.add("vector", lambda e, oi=oi:
                                     e.tensor_mul(out=oust[oi], in0=psO[0:64, 0:512],
                                                  in1=Rsb),
                                     wait=[("v", vR), ("_", st["oust_ld"][oi])], inc="v")
                        st["psO_v"] = ouev
                        st["oust_ld"][oi] = S.add(
                            "sync", lambda e, h=h, w=w, oi=oi:
                            e.dma_start(out=ou_scr[h * 64:(h + 1) * 64,
                                                   w * 512:(w + 1) * 512],
                                        in_=oust[oi]),
                            wait=[("v", ouev)], inc="ld", inc_n=16)
                st["qf_rd"] = S.val("t")

                ou_f = S.dma_fence("ld")
                for nb in range(4):
                    lo = S.add("sync", lambda e, nb=nb:
                               e.dma_start(out=Oun.rearrange("p (pr j) -> p pr j", pr=4),
                                           in_=ou_scr.rearrange("(pr p) (nb j) -> p pr nb j",
                                                                p=128, nb=4)[:, :, nb, :]),
                               wait=ou_f + [("t", st["oun_rd"])], inc="ld", inc_n=16)
                    for g2 in range(2):
                        for pr in range(4):
                            for oo in range(4):
                                oc = g2 * 4 + oo
                                ps = (psA, psB)[oo // 2]
                                hf = oo % 2
                                S.add("tensor",
                                      lambda e, pr=pr, oc=oc, ps=ps, hf=hf:
                                      e.matmul(out=ps[0:128, hf * 512:(hf + 1) * 512],
                                               lhsT=wo_sb[:, pr * D + oc * 128:
                                                          pr * D + (oc + 1) * 128],
                                               rhs=Oun[:, pr * 512:(pr + 1) * 512],
                                               start=(pr == 0), stop=(pr == 3)),
                                      wait=([("_", lo), ("_", wol_ld),
                                             ("v", st["psA_v"] if oo // 2 == 0 else st["psB_v"]),
                                             ("v", st.get("psA2_v", 0) if oo // 2 == 0 else 0),
                                             ("a", st["psA_a"] if oo // 2 == 0 else st["psB_a"])]
                                            if pr == 0 and hf == 0 else []),
                                      inc="t")
                        mm_g = S.val("t")
                        st["oun_rd"] = mm_g
                        for oo in range(4):
                            oc = g2 * 4 + oo
                            ps = (psA, psB)[oo // 2]
                            hf = oo % 2
                            i = oc % 2
                            lx = S.add("sync", lambda e, b=b, oc=oc, nb=nb, i=i:
                                       e.dma_start(out=x2t[i],
                                                   in_=xi32[b, oc * 128:(oc + 1) * 128,
                                                            nb * 512:(nb + 1) * 512]),
                                       wait=[("v", st["x2t_v"][i])], inc="ld", inc_n=16)
                            ev = S.add("vector", lambda e, i=i, ps=ps, hf=hf:
                                       e.tensor_add(out=xot[i],
                                                    in0=ps[0:128, hf * 512:(hf + 1) * 512],
                                                    in1=x2t[i]),
                                       wait=[("t", mm_g), ("_", lx),
                                             ("_", st["xot_ld"][i])], inc="v")
                            st["x2t_v"][i] = ev
                            if oo // 2 == 0:
                                st["psA_v"] = ev
                                st["psA2_v"] = ev
                            else:
                                st["psB_v"] = ev
                            cv = S.add("vector", lambda e, i=i:
                                       e.tensor_copy(out=xo16[i], in_=xot[i]),
                                       wait=[("_", st["xo16_ld"][i]),
                                             ("t", st["xo16_rd"][i])], inc="v")
                            st["xot_ld"][i] = S.add(
                                "sync", lambda e, b=b, oc=oc, nb=nb, i=i:
                                e.dma_start(out=xi32[b, oc * 128:(oc + 1) * 128,
                                                     nb * 512:(nb + 1) * 512],
                                            in_=xot[i]),
                                wait=[("v", cv)], inc="ld", inc_n=16)
                            st["xo16_ld"][i] = S.add(
                                "sync", lambda e, b=b, oc=oc, nb=nb, i=i:
                                e.dma_start(out=xi16[b, oc * 128:(oc + 1) * 128,
                                                     nb * 512:(nb + 1) * 512],
                                            in_=xo16[i]),
                                wait=[("v", cv)], inc="ld", inc_n=16)
                wo_res_f = S.dma_fence("ld")
                if attn_only:
                    continue

                def ln2_tile(oc, nb):
                    cw = load_shift16(x2c, xi16[b], oc, nb,
                                      war=[("t", st["x2c_rd"])] + (wo_res_f if oc == 0 and nb == 0 else []))
                    return cw
                fm_ln_stats(ln2_tile)
                bc2 = ln_rows_and_bcast()
                for oc in range(DC):
                    for nb in range(4):
                        k = oc * 4 + nb
                        i = k % 2
                        shift = oc >= 4
                        if shift and nb == 0:
                            msk_ = S.add("vector", lambda e, i=i: e.memset(x2t[i][:, 0:1], 0.0),
                                         wait=[("v", st["x2t_v"][i])], inc="v")
                            lxw = [("v", msk_)]
                            lx = S.add("gpsimd", lambda e, b=b, oc=oc, i=i:
                                       e.dma_start(out=x2t[i][:, 1:512],
                                                   in_=xi32[b, oc * 128:(oc + 1) * 128, 0:511]),
                                       wait=[("v", st["x2t_v"][i])], inc="g", inc_n=16)
                        else:
                            off = nb * 512 - (1 if shift else 0)
                            lx = S.add("gpsimd", lambda e, b=b, oc=oc, off=off, i=i:
                                       e.dma_start(out=x2t[i],
                                                   in_=xi32[b, oc * 128:(oc + 1) * 128,
                                                            off:off + 512]),
                                       wait=[("v", st["x2t_v"][i])], inc="g", inc_n=16)
                            lxw = []
                        v1_ = S.add("vector", lambda e, i=i, nb=nb:
                                    e.tensor_sub(out=ytmp, in0=x2t[i],
                                                 in1=Mt[:, nb * 512:(nb + 1) * 512]),
                                    wait=[("_", lx), ("v", bc2)] + lxw, inc="v")
                        st["x2t_v"][i] = v1_
                        v2_ = S.add("vector", lambda e, oc=oc, nb=nb:
                                    e.scalar_tensor_tensor(out=ytmp2, in0=ytmp,
                                                           scalar=lgcol[:, oc:oc + 1],
                                                           in1=Rt[:, nb * 512:(nb + 1) * 512],
                                                           op0=ALU.mult, op1=ALU.mult),
                                    wait=[("v", st["ytmp2_rd"]) if False else ("v", 0)], inc="v")
                        v3_ = S.add("vector", lambda e, oc=oc, nb=nb:
                                    e.tensor_scalar_add(
                                        out=LNH[:, oc * NSEQ + nb * 512:
                                                oc * NSEQ + (nb + 1) * 512],
                                        in0=ytmp2, scalar1=lbcol[:, oc:oc + 1]),
                                    wait=[("t", st["lnh_rd"])], inc="v")
                lnh_v = S.val("v")

                for nh in range(2):
                    hg_v = 0
                    for j in range(HJ):
                        pr = min(128, P6_HID - j * 128)
                        wi = j % 2
                        wl1 = S.add("sync", lambda e, l=l, j=j, pr=pr, wi=wi:
                                    e.dma_start(
                                        out=w1a_sb[wi].rearrange("p (c n) -> p c n", c=DC)[:, :, 0:pr],
                                        in_=w116[l].rearrange("(c p) n -> p c n", p=128)
                                        [:, :, j * 128: j * 128 + pr]),
                                    wait=[("t", st["w1_rd"][wi])], inc="ld", inc_n=16)
                        wl2 = S.add("sync", lambda e, l=l, j=j, pr=pr, wi=wi:
                                    e.dma_start(
                                        out=w1g_sb[wi].rearrange("p (c n) -> p c n", c=DC)[:, :, 0:pr],
                                        in_=w116[l].rearrange("(c p) n -> p c n", p=128)
                                        [:, :, P6_HID + j * 128: P6_HID + j * 128 + pr]),
                                    inc="ld", inc_n=16)
                        for sb2 in range(2):
                            nb = nh * 2 + sb2
                            for ic in range(DC):
                                S.add("tensor", lambda e, j=j, pr=pr, ic=ic, nb=nb, wi=wi:
                                      e.matmul(out=psA[0:pr, 0:512],
                                               lhsT=w1a_sb[wi][:, ic * 128: ic * 128 + pr],
                                               rhs=LNH[:, ic * NSEQ + nb * 512:
                                                       ic * NSEQ + (nb + 1) * 512],
                                               start=(ic == 0), stop=(ic == DC - 1)),
                                      wait=([("_", wl1), ("v", lnh_v), ("v", st["psA_v"]),
                                             ("v", st.get("psA2_v", 0)), ("a", st["psA_a"])]
                                            if ic == 0 else []),
                                      inc="t")
                                S.add("tensor", lambda e, j=j, pr=pr, ic=ic, nb=nb, wi=wi:
                                      e.matmul(out=psB[0:pr, 0:512],
                                               lhsT=w1g_sb[wi][:, ic * 128: ic * 128 + pr],
                                               rhs=LNH[:, ic * NSEQ + nb * 512:
                                                       ic * NSEQ + (nb + 1) * 512],
                                               start=(ic == 0), stop=(ic == DC - 1)),
                                      wait=([("_", wl2), ("v", st["psB_v"]), ("a", st["psB_a"])]
                                            if ic == 0 else []),
                                      inc="t")
                            mmj = S.val("t")
                            st["w1_rd"][wi] = mmj
                            ag = S.add("scalar", lambda e, j=j, pr=pr:
                                       e.activation(out=ffG[0:pr, :], in_=psB[0:pr, 0:512],
                                                    func=AF.Gelu,
                                                    bias=b1gcol[0:pr, j:j + 1]),
                                       wait=[("t", mmj), ("v", st.get("ffG_rd", 0))], inc="a")
                            st["psB_a"] = ag
                            av = S.add("vector", lambda e, j=j, pr=pr:
                                       e.tensor_scalar_add(out=ffA[0:pr, :],
                                                           in0=psA[0:pr, 0:512],
                                                           scalar1=b1acol[0:pr, j:j + 1]),
                                       wait=[("t", mmj)], inc="v")
                            st["psA_v"] = av
                            st["psA_a"] = 0 if False else st["psA_a"]
                            hv = S.add("vector", lambda e, j=j, pr=pr, sb2=sb2:
                                       e.tensor_mul(out=hg[0:pr, j * 1024 + sb2 * 512:
                                                           j * 1024 + (sb2 + 1) * 512],
                                                    in0=ffA[0:pr, :], in1=ffG[0:pr, :]),
                                       wait=[("a", ag), ("t", st["hg_rd"])], inc="v")
                            st["ffG_rd"] = hv
                            hg_v = hv
                    for oc in range(DC):
                        wi = oc % 2
                        wl_a = S.add("sync", lambda e, l=l, oc=oc, wi=wi:
                                     e.dma_start(
                                         out=w2_sb[wi].rearrange("p (c n) -> p c n", c=HJ)[:, 0:21, :],
                                         in_=w216[l][0:2688, oc * 128:(oc + 1) * 128]
                                         .rearrange("(c p) n -> p c n", p=128)),
                                     wait=[("t", st["w2_rd"][wi])], inc="ld", inc_n=16)
                        wl = S.add("sync", lambda e, l=l, oc=oc, wi=wi:
                                   e.dma_start(
                                       out=w2_sb[wi][0:42, 21 * 128:22 * 128],
                                       in_=w216[l][2688:2730, oc * 128:(oc + 1) * 128]),
                                   inc="ld", inc_n=16)
                        for sb2 in range(2):
                            nb = nh * 2 + sb2
                            par = sb2
                            ps = (psA, psB)[par]
                            pv = st["psA_v"] if par == 0 else st["psB_v"]
                            pa = st["psA_a"] if par == 0 else st["psB_a"]
                            for j in range(HJ):
                                pr = min(128, P6_HID - j * 128)
                                S.add("tensor", lambda e, j=j, pr=pr, oc=oc, nb=nb,
                                      sb2=sb2, wi=wi, ps=ps:
                                      e.matmul(out=ps[0:128, 0:512],
                                               lhsT=w2_sb[wi][0:pr, j * 128:(j + 1) * 128],
                                               rhs=hg[0:pr, j * 1024 + sb2 * 512:
                                                      j * 1024 + (sb2 + 1) * 512],
                                               start=(j == 0), stop=(j == HJ - 1)),
                                      wait=([("_", wl), ("v", hg_v), ("v", pv), ("a", pa)]
                                            if j == 0 else []) +
                                           ([("_", wl_a)] if j == 0 else []),
                                      inc="t")
                            mmo = S.val("t")
                            st["w2_rd"][wi] = mmo
                            st["hg_rd"] = mmo
                            i = (oc * 2 + sb2) % 2
                            lx = S.add("sync", lambda e, b=b, oc=oc, nb=nb, i=i:
                                       e.dma_start(out=x2t[i],
                                                   in_=xi32[b, oc * 128:(oc + 1) * 128,
                                                            nb * 512:(nb + 1) * 512]),
                                       wait=[("v", st["x2t_v"][i])], inc="ld", inc_n=16)
                            ev = S.add("vector", lambda e, oc=oc, i=i, ps=ps:
                                       e.scalar_tensor_tensor(out=xot[i], in0=ps[0:128, 0:512],
                                                              scalar=b2col[:, oc:oc + 1],
                                                              in1=x2t[i],
                                                              op0=ALU.add, op1=ALU.add),
                                       wait=[("t", mmo), ("_", lx), ("_", st["xot_ld"][i])],
                                       inc="v")
                            st["x2t_v"][i] = ev
                            if par == 0:
                                st["psA_v"] = ev
                            else:
                                st["psB_v"] = ev
                            cv = S.add("vector", lambda e, i=i:
                                       e.tensor_copy(out=xo16[i], in_=xot[i]),
                                       wait=[("_", st["xo16_ld"][i]),
                                             ("t", st["xo16_rd"][i])], inc="v")
                            st["xot_ld"][i] = S.add(
                                "sync", lambda e, b=b, oc=oc, nb=nb, i=i:
                                e.dma_start(out=xi32[b, oc * 128:(oc + 1) * 128,
                                                     nb * 512:(nb + 1) * 512],
                                            in_=xot[i]),
                                wait=[("v", ev)], inc="ld", inc_n=16)
                            st["xo16_ld"][i] = S.add(
                                "sync", lambda e, b=b, oc=oc, nb=nb, i=i:
                                e.dma_start(out=xi16[b, oc * 128:(oc + 1) * 128,
                                                     nb * 512:(nb + 1) * 512],
                                            in_=xo16[i]),
                                wait=[("v", cv)], inc="ld", inc_n=16)
                st["lnh_rd"] = S.val("t")

        for b in range(n_batches):
            S.barrier()
            for k in ("psA_v", "psB_v", "psO_v"):
                st[k] = 0
            st["psA_a"] = st["psB_a"] = 0
            for nt in range(NT):
                i = nt % 2
                last_tr = 0
                for c in range(DC):
                    j = c % 2
                    lx1 = S.add("sync", lambda e, b=b, c=c, nt=nt, j=j:
                                e.dma_start(out=x2t[j][:, 0:128],
                                            in_=xi32[b, c * 128:(c + 1) * 128,
                                                     nt * 128:(nt + 1) * 128]),
                                wait=[("v", st["x2t_v"][j])], inc="ld", inc_n=16)
                    lx2 = S.add("sync", lambda e, b=b, c=c, nt=nt, j=j:
                                e.dma_start(out=x2t[j][:, 128:256],
                                            in_=x3_in[b, c * 128:(c + 1) * 128,
                                                      nt * 128:(nt + 1) * 128]),
                                inc="ld", inc_n=16)
                    sv = S.add("vector", lambda e, j=j:
                               e.tensor_add(out=xot[j][:, 0:128], in0=x2t[j][:, 0:128],
                                            in1=x2t[j][:, 128:256]),
                               wait=[("_", lx1), ("_", lx2), ("t", st.get("xot_tr", 0))],
                               inc="v")
                    st["x2t_v"][j] = sv
                    par = c % 2
                    ps = (psA, psB)[par]
                    pv = st["psA_v"] if par == 0 else st["psB_v"]
                    tr = S.add("tensor", lambda e, j=j, ps=ps:
                               e.transpose(out=ps[0:128, 0:128], in_=xot[j][:, 0:128],
                                           identity=id32),
                               wait=[("v", sv), ("v", pv)], inc="t")
                    st["xot_tr"] = tr
                    evv = S.add("vector", lambda e, c=c, i=i, ps=ps:
                                e.tensor_copy(out=xstage[i][:, c * 128:(c + 1) * 128],
                                              in_=ps[0:128, 0:128]),
                                wait=[("t", tr), ("_", st["xstage_ld"][i])], inc="v")
                    if par == 0:
                        st["psA_v"] = evv
                    else:
                        st["psB_v"] = evv
                    last_tr = evv
                st["xstage_ld"][i] = S.add(
                    "sync", lambda e, b=b, nt=nt, i=i:
                    e.dma_start(out=x_out[b, nt * 128:(nt + 1) * 128, :], in_=xstage[i]),
                    wait=[("v", last_tr)], inc="ld", inc_n=16)

        if C is None:
            S.add("sync", None, wait=S.dma_fence("ld") + S.dma_fence("g"))
            S.emit(nc, stack)
    return nc


def build_fused(n_batches=NB):
    nc = bass.Bass("TRN2", target_bir_lowering=False, detect_race_conditions=False)
    S = Sched(SEMS + ("cc",))
    NL = NSEQ // NCORES

    io = {}
    io["x"] = nc.dram_tensor("x", [n_batches, NSEQ, D], F32, kind="ExternalInput").ap()
    for n in ("wq", "wk", "wv", "wo"):
        io["l0a_" + n] = nc.dram_tensor("l0a_" + n, [D, D], F32, kind="ExternalInput").ap()
        io["l1a_" + n] = nc.dram_tensor("l1a_" + n, [D, D], F32, kind="ExternalInput").ap()
    for n in ("bq", "bk", "bv", "bo"):
        io["l0a_" + n] = nc.dram_tensor("l0a_" + n, [D, 1], F32, kind="ExternalInput").ap()
        io["l1a_" + n] = nc.dram_tensor("l1a_" + n, [D, 1], F32, kind="ExternalInput").ap()
    for n in ("wf", "wi", "wc", "uf", "ui", "uc", "wo", "uo"):
        io["l0f_" + n] = nc.dram_tensor("l0f_" + n, [D, D], F32, kind="ExternalInput").ap()
    for n in ("bf", "bi", "bc", "bo", "in_c"):
        io["l0f_" + n] = nc.dram_tensor("l0f_" + n, [D, 1], F32, kind="ExternalInput").ap()
    for n, shp in (("wqkv", [2, D, 3 * P6_INNER]), ("wo", [2, P6_INNER, D]),
                   ("w1", [2, D, 2 * P6_HID]), ("w2", [2, P6_HID, D]),
                   ("qs", [2, HD]), ("ks", [2, HD]), ("lng", [2, D]),
                   ("lnb", [2, D]), ("b1", [2, 2 * P6_HID]), ("b2", [2, D])):
        io["l1f_" + n] = nc.dram_tensor("l1f_" + n, shp, F32, kind="ExternalInput").ap()

    io["x1_fm"] = nc.dram_tensor("x1_fm", [n_batches, D, NSEQ], F32, kind="Internal").ap()
    io["yL_fm"] = nc.dram_tensor("yL_fm", [n_batches, D, NSEQ], BF16, kind="Internal").ap()
    io["y_all"] = nc.dram_tensor("y_all", [NCORES * n_batches, D, NSEQ], BF16,
                                 kind="Internal", addr_space="Shared").ap()
    io["c_mine"] = nc.dram_tensor("c_mine", [D, NL], F32, kind="Internal").ap()
    io["c_all"] = nc.dram_tensor("c_all", [NCORES, D, NL], F32, kind="Internal",
                                 addr_space="Shared").ap()
    io["x2_fm"] = nc.dram_tensor("x2_fm", [n_batches, D, NSEQ], F32, kind="Internal").ap()
    io["x3_fm"] = nc.dram_tensor("x3_fm", [n_batches, D, NSEQ], F32, kind="Internal").ap()
    io["xout"] = nc.dram_tensor("xout", [n_batches, NSEQ, D], F32,
                                kind="ExternalOutput").ap()

    C = FCtx()
    C.nc, C.S, C.io = nc, S, io
    C.consts = {}
    C.y_ready = [("cc", 1)]
    C.c_ready = [("cc", 2)]
    rg = [list(range(NCORES))]

    with ExitStack() as stack:
        C.AR = Arena(stack.enter_context(
            nc.sbuf_tensor("AR", [128, 53160], F32)).ap(), 53160)
        e_ = stack.enter_context
        C.psA = e_(nc.psum_tensor("psA", [128, 1024], F32)).ap()
        C.psB = e_(nc.psum_tensor("psB", [128, 1024], F32)).ap()
        C.psO = e_(nc.psum_tensor("psO", [128, 1024], F32)).ap()
        C.pt16 = e_(nc.psum_tensor("pt16", [128, 2048], BF16)).ap()

        build_p1(n_batches, C=C)
        S.barrier()
        S.add("gpsimd", lambda e: e.collective_compute(
            "AllGather", ALU.bypass, rg,
            ins=[io["yL_fm"].opt()], outs=[io["y_all"].opt()]), inc="cc")
        build_p3(nloc=NL, nsteps=NCORES * n_batches, C=C)
        S.barrier()
        S.add("gpsimd", lambda e: e.collective_compute(
            "AllGather", ALU.bypass, rg,
            ins=[io["c_mine"].opt()], outs=[io["c_all"].opt()]), inc="cc")
        build_p4(n_batches, C=C)
        S.barrier()
        build_p5(n_batches, C=C)
        S.barrier()
        build_p6(n_batches, C=C)
        S.add("sync", None, wait=S.dma_fence("ld") + S.dma_fence("g"))
        S.emit(nc, stack)
    return nc


def _col(v):
    return np.ascontiguousarray(np.asarray(v).reshape(-1, 1))


def kernel(**inputs):
    x = np.asarray(inputs["L0_attn_x"])
    cores = list(range(NCORES))
    nc = build_fused()

    def feed(c):
        f = {"x": np.ascontiguousarray(x[4 * c:4 * c + 4])}
        for n in ("wq", "wk", "wv", "wo"):
            f["l0a_" + n] = inputs["L0_attn_" + n]
            f["l1a_" + n] = inputs["L1_attn_W" + n[1:]]
        for n in ("bq", "bk", "bv", "bo"):
            f["l0a_" + n] = _col(inputs["L0_attn_" + n])
            f["l1a_" + n] = _col(inputs["L1_attn_" + n])
        for n in ("wf", "wi", "wc", "uf", "ui", "uc", "wo", "uo"):
            f["l0f_" + n] = inputs["L0_ffn_" + n]
        for n in ("bf", "bi", "bc", "bo", "in_c"):
            f["l0f_" + n] = _col(inputs["L0_ffn_" + n])
        for n, src in (("wqkv", "wqkv"), ("wo", "wo"), ("w1", "w1"), ("w2", "w2"),
                       ("qs", "q_scale"), ("ks", "k_scale"), ("lng", "ln_g"),
                       ("lnb", "ln_b"), ("b1", "b1"), ("b2", "b2")):
            f["l1f_" + n] = inputs["L1_ffn_" + src]
        return f

    res = run_bass_kernel_spmd(nc, [feed(c) for c in cores], core_ids=cores).results
    return np.concatenate([r["xout"] for r in res], axis=0)


def kernel_unfused(**inputs):
    x = np.asarray(inputs["L0_attn_x"])
    cores = list(range(NCORES))

    nc1 = build_p1()
    in1 = [{"x": np.ascontiguousarray(x[4 * c:4 * c + 4]),
            "wq": inputs["L0_attn_wq"], "wk": inputs["L0_attn_wk"],
            "wv": inputs["L0_attn_wv"], "wo": inputs["L0_attn_wo"],
            "bq": _col(inputs["L0_attn_bq"]), "bk": _col(inputs["L0_attn_bk"]),
            "bv": _col(inputs["L0_attn_bv"]), "bo": _col(inputs["L0_attn_bo"])}
           for c in cores]
    r1 = run_bass_kernel_spmd(nc1, in1, core_ids=cores).results
    x1 = [r["x1_fm"] for r in r1]
    yl = [r["yL_fm"] for r in r1]

    y_all = np.concatenate(yl, axis=0)
    nloc = NSEQ // NCORES
    nc3 = build_p3(nloc=nloc)
    in3 = [{"y_re": np.ascontiguousarray(y_all[:, :, c * nloc:(c + 1) * nloc]),
            "wf": inputs["L0_ffn_wf"], "wi": inputs["L0_ffn_wi"],
            "wc": inputs["L0_ffn_wc"], "uf": inputs["L0_ffn_uf"],
            "ui": inputs["L0_ffn_ui"], "uc": inputs["L0_ffn_uc"],
            "bf": _col(inputs["L0_ffn_bf"]), "bi": _col(inputs["L0_ffn_bi"]),
            "bc": _col(inputs["L0_ffn_bc"]), "in_c": _col(inputs["L0_ffn_in_c"])}
           for c in cores]
    r3 = run_bass_kernel_spmd(nc3, in3, core_ids=cores).results
    c_full = np.concatenate([r["c_fin"] for r in r3], axis=1)

    nc4 = build_p4()
    in4 = [{"x1_fm": x1[c], "yL_fm": yl[c], "c_full": c_full,
            "wo": inputs["L0_ffn_wo"], "uo": inputs["L0_ffn_uo"],
            "bo": _col(inputs["L0_ffn_bo"])}
           for c in cores]
    r4 = run_bass_kernel_spmd(nc4, in4, core_ids=cores).results
    x2 = [r["x2_fm"] for r in r4]

    nc5 = build_p5()
    in5 = [{"x2_fm": x2[c],
            "wq": inputs["L1_attn_Wq"], "wk": inputs["L1_attn_Wk"],
            "wv": inputs["L1_attn_Wv"], "wo": inputs["L1_attn_Wo"],
            "bq": _col(inputs["L1_attn_bq"]), "bk": _col(inputs["L1_attn_bk"]),
            "bv": _col(inputs["L1_attn_bv"]), "bo": _col(inputs["L1_attn_bo"])}
           for c in cores]
    r5 = run_bass_kernel_spmd(nc5, in5, core_ids=cores).results
    x3 = [r["x3_fm"] for r in r5]

    nc6 = build_p6()
    in6 = [{"x3_fm": x3[c], "wqkv": inputs["L1_ffn_wqkv"], "wo_l": inputs["L1_ffn_wo"],
            "w1": inputs["L1_ffn_w1"], "w2": inputs["L1_ffn_w2"],
            "q_scale": inputs["L1_ffn_q_scale"], "k_scale": inputs["L1_ffn_k_scale"],
            "ln_g": inputs["L1_ffn_ln_g"], "ln_b": inputs["L1_ffn_ln_b"],
            "b1": inputs["L1_ffn_b1"], "b2": inputs["L1_ffn_b2"]}
           for c in cores]
    r6 = run_bass_kernel_spmd(nc6, in6, core_ids=cores).results
    return np.concatenate([r["xout"] for r in r6], axis=0)
